# version 09
# baseline (speedup 1.0000x reference)
import numpy as np
from contextlib import ExitStack
import concourse.bass as bass
import concourse.mybir as mybir
from concourse.bass_utils import run_bass_kernel_spmd

F32 = mybir.dt.float32
BF16 = mybir.dt.bfloat16
AF = mybir.ActivationFunctionType
ALU = mybir.AluOpType
AX = mybir.AxisListType

D = 1024
HU = 4096
T = 2048
NS = 2
TC = NS * T
DC = D // 128
HC = HU // 128
TT = 512
NJS = T // TT
NJ = TC // TT
LN_EPS = 1e-5
K_CLAMP = 60.0
K_EPS = 1e-8

ALL_PHASES = ["p0", "k", "v", "r", "scan", "wo", "cmk", "cmv", "route",
              "exp0", "qkv1", "att1", "msaqkv", "msaatt", "msaproj",
              "exp1", "final"]


class Prog:

    def __init__(self):
        self.nc = bass.Bass("TRN2", target_bir_lowering=False)
        self.es = ExitStack()
        self.sems = {}
        self.cnt = {}
        self.names = set()
        self.bank_guard = {}
        self.local_es = None

    def sem(self, name):
        if name not in self.sems:
            self.sems[name] = self.es.enter_context(self.nc.semaphore(name))
            self.cnt[name] = 0
        return self.sems[name]

    def inc(self, inst, name, k=1):
        inst.then_inc(self.sem(name), k)
        self.cnt[name] += k
        return self.cnt[name]

    def wait(self, eng, name, val=None):
        eng.wait_ge(self.sem(name), self.cnt[name] if val is None else val)

    def _chained(self, eng, semname, opname, a, kw):
        s = self.sem(semname)
        if self.cnt[semname]:
            eng.wait_ge(s, self.cnt[semname])
        inst = getattr(eng, opname)(*a, **kw)
        self.inc(inst, semname, 1)
        return inst

    def v(self, opname, *a, **kw):
        return self._chained(self.nc.vector, "dv", opname, a, kw)

    def g(self, opname, *a, **kw):
        return self._chained(self.nc.gpsimd, "po", opname, a, kw)

    def s(self, opname, *a, **kw):
        return self._chained(self.nc.scalar, "ac", opname, a, kw)

    def sb(self, name, shape, dtype=F32):
        assert name not in self.names, name
        self.names.add(name)
        return self.es.enter_context(self.nc.sbuf_tensor(name, shape, dtype))

    def sbl(self, name, shape, dtype=F32):
        assert self.local_es is not None
        self.uniq = getattr(self, "uniq", 0) + 1
        name = f"{name}_{self.uniq}"
        self.names.add(name)
        return self.local_es.enter_context(self.nc.sbuf_tensor(name, shape, dtype))

    def barrier(self):
        self.nc.all_engine_barrier()

    def tile_end(self):
        self.wait(self.nc.gpsimd, "st")
        self.wait(self.nc.sync, "la")
        for s_ in ("lb", "lb0", "lb1", "lc0", "lc1", "tp0", "tp1", "tp2", "tp3",
                   "tq0", "tq1", "tq2", "tq3"):
            if s_ in self.sems:
                self.wait(self.nc.sync, s_)
        for i in range(4):
            if ("st%d" % i) in self.sems:
                self.wait(self.nc.gpsimd, "st%d" % i)
        self.barrier()

    def bank_acquire(self, b):
        for semname, val in self.bank_guard.get(b, {}).items():
            self.nc.tensor.wait_ge(self.sem(semname), val)

    def bank_release(self, b, *semnames):
        self.bank_guard[b] = {sn: self.cnt[sn] for sn in semnames}

    def store_slot(self, i, out, in_):
        sn = "st%d" % (i % 4)
        self.inc(self.nc.gpsimd.dma_start(out=out, in_=in_), sn, 16)

    def slot_guard(self, eng, i):
        sn = "st%d" % (i % 4)
        if sn in self.sems:
            eng.wait_ge(self.sem(sn), self.cnt[sn])


def declare_io(nc, dump):
    io = {}
    io["x_in"] = nc.dram_tensor("x_in", [NS, T, D], F32, kind="ExternalInput").ap()
    wspecs = {
        "L0_attn_ln0_g": [D], "L0_attn_ln0_b": [D],
        "L0_attn_pre_g": [D], "L0_attn_pre_b": [D],
        "L0_attn_post_g": [D], "L0_attn_post_b": [D],
        "L0_attn_tm_decay": [HU], "L0_attn_tm_first": [HU],
        "L0_attn_tm_mix_k": [D], "L0_attn_tm_mix_v": [D], "L0_attn_tm_mix_r": [D],
        "L0_attn_tm_Wk": [D, HU], "L0_attn_tm_Wv": [D, HU],
        "L0_attn_tm_Wr": [D, HU], "L0_attn_tm_Wo": [HU, D],
        "L0_attn_cm_mix_k": [D], "L0_attn_cm_mix_r": [D],
        "L0_attn_cm_Wk": [D, HU], "L0_attn_cm_Wv": [HU, D], "L0_attn_cm_Wr": [D, D],
        "L0_ffn_centroids": [64, D], "L0_ffn_expert_biases": [64],
        "L0_ffn_Ws_up": [D, 512], "L0_ffn_Ws_down": [512, D],
        "L0_ffn_W_up": [64, D, 512], "L0_ffn_W_down": [64, 512, D],
        "L1_attn_Wq": [D, D], "L1_attn_bq": [D],
        "L1_attn_Wk": [D, 128], "L1_attn_bk": [128],
        "L1_attn_Wv": [D, 128], "L1_attn_bv": [128],
        "L1_attn_Wo": [D, D], "L1_attn_bo": [D],
        "L1_ffn_ln1_w": [D], "L1_ffn_ln1_b": [D],
        "L1_ffn_w_qkv": [D, 3 * D], "L1_ffn_w_proj": [D, D],
        "L1_ffn_ln2_w": [D], "L1_ffn_ln2_b": [D],
        "L1_ffn_gate_w": [D, 8], "L1_ffn_gate_b": [8],
        "L1_ffn_w1": [8, D, HU], "L1_ffn_w2": [8, HU, D],
    }
    for k, shp in wspecs.items():
        io[k] = nc.dram_tensor(k, shp, F32, kind="ExternalInput").ap()
    io["out"] = nc.dram_tensor("out", [NS, T, D], F32, kind="ExternalOutput").ap()
    return io


def declare_scratch(nc, dump):
    specs = {
        "x_fm": ([DC, 128, TC], F32),
        "xn":   ([DC, 128, TC], BF16),
        "x0":   ([DC, 128, TC], F32),
        "xn1":  ([NS, DC, 128, T + 1], BF16),
        "kk":   ([HC, 128, TC], BF16),
        "vv_":  ([HC, 128, TC], BF16),
        "sr":   ([HC, 128, TC], BF16),
        "rw":   ([HC, 128, TC], BF16),
        "x0b":  ([DC, 128, TC], F32),
        "xn2":  ([NS, DC, 128, T + 1], BF16),
        "ck":   ([HC, 128, TC], BF16),
        "at0":  ([DC, 128, TC], F32),
        "acc0": ([DC, 128, TC], F32),
        "y0":   ([DC, 128, TC], F32),
        "xn_1": ([DC, 128, TC], BF16),
        "q1":   ([DC, 128, TC], BF16),
        "at1":  ([DC, 128, TC], F32),
        "mq":   ([DC, 128, TC], BF16),
        "mk_":  ([DC, 128, TC], BF16),
        "vtm":  ([NS, T // 128, 128, D], BF16),
        "mo":   ([DC, 128, TC], BF16),
        "xf1":  ([DC, 128, TC], F32),
        "xg":   ([DC, 128, TC], BF16),
        "pre_out": ([DC, 128, TC], F32),
        "dbg_wt": ([64, TC], BF16),
        "k1":  ([128, TC], BF16),
        "v1":  ([128, TC], BF16),
        "w1h": ([8, TC], BF16),
    }
    scr = {}
    for k, (shp, dt) in specs.items():
        kind = "ExternalOutput" if k in dump else "Internal"
        scr[k] = nc.dram_tensor("s_" + k, shp, dt, kind=kind).ap()
    return scr


def make_consts(p, io):
    nc = p.nc
    c = {}
    import ml_dtypes
    ident_np = np.eye(128, dtype=np.float32)
    h_idf = nc.inline_tensor(ident_np, name="c_idf")
    h_idb = nc.inline_tensor(ident_np.astype(ml_dtypes.bfloat16), name="c_idb")
    h_ones_col = nc.inline_tensor(np.ones((128, 1), np.float32), name="c_onescol")
    h_ones_row = nc.inline_tensor(np.ones((1, 128), np.float32), name="c_onesrow")
    h_ones_row_b = nc.inline_tensor(np.ones((1, 128), ml_dtypes.bfloat16),
                                    name="c_onesrowb")
    oh64 = np.zeros((64, 64 * 128), np.float32)
    for e in range(64):
        oh64[e, e * 128:(e + 1) * 128] = 1.0
    h_oh64 = nc.inline_tensor(oh64.astype(ml_dtypes.bfloat16), name="c_oh64")
    oh8 = np.zeros((8, 8 * 128), np.float32)
    for e in range(8):
        oh8[e, e * 128:(e + 1) * 128] = 1.0
    h_oh8 = nc.inline_tensor(oh8.astype(ml_dtypes.bfloat16), name="c_oh8")
    cols = {"ln_eps": LN_EPS, "k_eps": K_EPS, "k_clamp": K_CLAMP, "zero": 0.0,
            "one": 1.0, "neg1e30": -1e30, "inv_d": 1.0 / D, "half": 0.5}
    cc_np = np.zeros((128, len(cols)), np.float32)
    c["col_idx"] = {k: i for i, k in enumerate(cols)}
    for k, v in cols.items():
        cc_np[:, c["col_idx"][k]] = v
    h_cc = nc.inline_tensor(cc_np, name="c_cols")
    band = np.full((128, 384), -1e30, np.float32)
    for i in range(128):
        band[i, i:i + 256] = 0.0
    h_band = nc.inline_tensor(band, name="c_band")

    c["idf"] = p.sb("sb_idf", [128, 128], F32)
    c["idb"] = p.sb("sb_idb", [128, 128], BF16)
    c["ones_col"] = p.sb("sb_onescol", [128, 1], F32)
    c["ones_row"] = p.sb("sb_onesrow", [1, 128], F32)
    c["ones_row_b"] = p.sb("sb_onesrowb", [1, 128], BF16)
    c["cols"] = p.sb("sb_cols", [128, len(cols)], F32)
    c["zcol_b"] = p.sb("sb_zcolb", [128, 4], BF16)
    c["h_oh64"] = h_oh64
    c["h_oh8"] = h_oh8
    c["h_band"] = h_band

    p.g("memset", c["zcol_b"][:], 0.0)
    for dst, src in [(c["idf"], h_idf.ap()), (c["idb"], h_idb.ap()),
                     (c["ones_col"], h_ones_col.ap()),
                     (c["ones_row"], h_ones_row.ap()),
                     (c["ones_row_b"], h_ones_row_b.ap()),
                     (c["cols"], h_cc.ap())]:
        p.inc(nc.sync.dma_start(out=dst[:], in_=src[:]), "la", 16)

    vec_specs = [
        ("ln0_g", "L0_attn_ln0_g", DC), ("ln0_b", "L0_attn_ln0_b", DC),
        ("pre_g", "L0_attn_pre_g", DC), ("pre_b", "L0_attn_pre_b", DC),
        ("post_g", "L0_attn_post_g", DC), ("post_b", "L0_attn_post_b", DC),
        ("mix_k", "L0_attn_tm_mix_k", DC), ("mix_v", "L0_attn_tm_mix_v", DC),
        ("mix_r", "L0_attn_tm_mix_r", DC),
        ("cmix_k", "L0_attn_cm_mix_k", DC), ("cmix_r", "L0_attn_cm_mix_r", DC),
        ("tm_decay", "L0_attn_tm_decay", HC), ("tm_first", "L0_attn_tm_first", HC),
        ("eb", "L0_ffn_expert_biases", "64"),
        ("bq", "L1_attn_bq", DC), ("bk", "L1_attn_bk", 1), ("bv", "L1_attn_bv", 1),
        ("bo", "L1_attn_bo", DC),
        ("ln1_w", "L1_ffn_ln1_w", DC), ("ln1_b", "L1_ffn_ln1_b", DC),
        ("ln2_w", "L1_ffn_ln2_w", DC), ("ln2_b", "L1_ffn_ln2_b", DC),
        ("gate_b", "L1_ffn_gate_b", "8"),
    ]
    stage = p.sb("sb_vecstage", [HC, 128], F32)
    pstat = p.psum[0]
    for name, key, nch in vec_specs:
        if nch in ("64", "8"):
            rows, colw = 1, int(nch)
            src = io[key].rearrange("(o a) -> o a", o=1)
        elif nch == 1:
            rows, colw = 1, 128
            src = io[key].rearrange("(o a) -> o a", o=1)
        else:
            rows, colw = nch, 128
            src = io[key].rearrange("(a b) -> a b", b=128)
        dst = p.sb("sb_v_" + name, [colw, rows], F32)
        c[name] = dst
        p.wait(nc.sync, "mm")
        ldv = p.inc(nc.sync.dma_start(out=stage[0:rows, 0:colw], in_=src), "la", 16)
        nc.tensor.wait_ge(p.sem("la"), ldv)
        p.wait(nc.tensor, "dv")
        p.inc(nc.tensor.transpose(pstat[0:colw, 0:rows], stage[0:rows, 0:colw],
                                  c["idf"][0:rows, 0:rows]), "mm", 1)
        p.wait(nc.vector, "mm")
        p.v("tensor_copy", out=dst[:], in_=pstat[0:colw, 0:rows])
    c["decay"] = p.sb("sb_v_decay", [128, HC], F32)
    c["first"] = p.sb("sb_v_first", [128, HC], F32)
    p.wait(nc.scalar, "dv")
    p.s("activation", out=c["decay"][:], in_=c["tm_decay"][:], func=AF.Exp)
    p.s("activation", out=c["decay"][:], in_=c["decay"][:], func=AF.Exp,
        scale=-1.0)
    p.s("activation", out=c["first"][:], in_=c["tm_first"][:], func=AF.Exp)
    p.barrier()
    return c


def col(p, c, name):
    i = c["col_idx"][name]
    return c["cols"][:, i:i + 1]


def load_weight_bf16(p, src_ap_fn, dst_tiles, n_tiles, ncols):
    nc = p.nc
    hw = 512
    nh = (ncols + hw - 1) // hw
    p.wait(nc.sync, "mm")
    for i in range(n_tiles):
        for h in range(nh):
            cw = min(hw, ncols - h * hw)
            k = (i * nh + h) % 2
            sn = "ws%d" % k
            if p.cnt.get(sn, 0):
                nc.sync.wait_ge(p.sem("dv"), p.wcast_guard[k][0])
                nc.sync.wait_ge(p.sem("po"), p.wcast_guard[k][1])
            ldv = p.inc(nc.sync.dma_start(out=p.wstage[k][:, 0:cw],
                                          in_=src_ap_fn(i)[:, h * hw:h * hw + cw]),
                        sn, 16)
            if (i * nh + h) % 2 == 0:
                nc.vector.wait_ge(p.sem(sn), ldv)
                p.v("tensor_copy", out=dst_tiles[i][:, h * hw:h * hw + cw],
                    in_=p.wstage[k][:, 0:cw])
            else:
                nc.gpsimd.wait_ge(p.sem(sn), ldv)
                p.g("tensor_copy", out=dst_tiles[i][:, h * hw:h * hw + cw],
                    in_=p.wstage[k][:, 0:cw])
            p.wcast_guard[k] = (p.cnt["dv"], p.cnt["po"])
    p.barrier()


def ln_fm(p, cst, xch, out, W, g=None, b=None):
    nc = p.nc
    ps_sx, ps_sq, ps_mb, ps_rb = p.psum[4], p.psum[5], p.psum[6], p.psum[7]
    sq = p.tmp_sq
    st_ = p.tmp_stats
    p.wait(nc.gpsimd, "mm")
    p.wait(nc.tensor, "dv")
    p.wait(nc.gpsimd, "dv")
    for dc in range(DC):
        p.g("tensor_tensor", out=sq[:, 0:W], in0=xch[dc], in1=xch[dc],
            op=ALU.mult)
        nc.tensor.wait_ge(p.sem("po"), p.cnt["po"])
        nc.tensor.matmul(ps_sq[0:1, 0:W], cst["ones_col"][:], sq[:, 0:W],
                         start=(dc == 0), stop=(dc == DC - 1))
        p.inc(nc.tensor.matmul(ps_sx[0:1, 0:W], cst["ones_col"][:], xch[dc],
                               start=(dc == 0), stop=(dc == DC - 1)), "mm", 1)
        if dc < DC - 1:
            nc.gpsimd.wait_ge(p.sem("mm"), p.cnt["mm"])
    p.wait(nc.vector, "mm")
    m = st_[0:1, 0:W]
    msq = st_[0:1, TT:TT + W]
    var = st_[0:1, 2 * TT:2 * TT + W]
    rstd = st_[0:1, 3 * TT:3 * TT + W]
    p.v("tensor_scalar", out=m, in0=ps_sx[0:1, 0:W], scalar1=1.0 / D,
        scalar2=None, op0=ALU.mult)
    p.v("tensor_scalar", out=msq, in0=ps_sq[0:1, 0:W], scalar1=1.0 / D,
        scalar2=None, op0=ALU.mult)
    p.v("tensor_tensor", out=var, in0=m, in1=m, op=ALU.mult)
    p.v("tensor_tensor", out=var, in0=msq, in1=var, op=ALU.subtract)
    p.wait(nc.scalar, "dv")
    p.s("activation", out=rstd, in_=var, func=AF.Ln,
        bias=col(p, cst, "ln_eps")[0:1, :])
    p.s("activation", out=rstd, in_=rstd, func=AF.Exp, scale=-0.5)
    p.wait(nc.tensor, "ac")
    p.wait(nc.tensor, "dv")
    nc.tensor.matmul(ps_mb[:, 0:W], cst["ones_row"][:], m, start=True, stop=True)
    p.inc(nc.tensor.matmul(ps_rb[:, 0:W], cst["ones_row"][:], rstd,
                           start=True, stop=True), "mm", 1)
    p.wait(nc.vector, "mm")
    for dc in range(DC):
        p.v("tensor_tensor", out=out[dc], in0=xch[dc], in1=ps_mb[:, 0:W],
            op=ALU.subtract)
        p.v("tensor_tensor", out=out[dc], in0=out[dc], in1=ps_rb[:, 0:W],
            op=ALU.mult)
        if g is not None:
            p.v("tensor_scalar", out=out[dc], in0=out[dc],
                scalar1=g[:, dc:dc + 1], scalar2=b[:, dc:dc + 1],
                op0=ALU.mult, op1=ALU.add)


def phase_p0(p, cst, io, scr):
    nc = p.nc
    xtm = p.sbl("p0_xtm", [128, 4 * D], F32)
    xfm = [p.sbl(f"p0_xfm{i}", [128, TT], F32) for i in range(DC)]
    xnf = [p.sbl(f"p0_xnf{i}", [128, TT], F32) for i in range(DC)]
    x0f = [p.sbl(f"p0_x0f{i}", [128, TT], F32) for i in range(DC)]
    xn1f = [p.sbl(f"p0_xn1f{i}", [128, TT], F32) for i in range(DC)]
    obf = p.sbl("p0_obf", [128, TT], BF16)

    with nc.allow_non_contiguous_dma(reason="one-column zero init, 32 tiny DMAs"):
        for s in range(NS):
            for dc in range(DC):
                p.inc(nc.gpsimd.dma_start(out=scr["xn1"][s, dc, :, 0:1],
                                          in_=cst["zcol_b"][:, 0:1]), "st", 16)
                p.inc(nc.gpsimd.dma_start(out=scr["xn2"][s, dc, :, 0:1],
                                          in_=cst["zcol_b"][:, 0:1]), "st", 16)

    for j in range(NJ):
        s, jj = j // NJS, j % NJS
        t0 = jj * TT
        c0 = s * T + t0
        ld = p.inc(nc.sync.dma_start(
            out=xtm[:].rearrange("p (a d) -> p a d", a=4),
            in_=io["x_in"][s, t0:t0 + TT, :].rearrange("(a p) d -> p a d", p=128)),
            "la", 16)
        nc.tensor.wait_ge(p.sem("la"), ld)
        for a in range(4):
            for dc in range(DC):
                bank = p.psum[(a * DC + dc) % 4]
                mmv = p.inc(nc.tensor.transpose(
                    bank[:, 0:128], xtm[:, a * D + dc * 128: a * D + (dc + 1) * 128],
                    cst["idf"][:]), "mm", 1)
                nc.vector.wait_ge(p.sem("mm"), mmv)
                p.v("tensor_copy", out=xfm[dc][:, a * 128:(a + 1) * 128],
                    in_=bank[:, 0:128])
                nc.tensor.wait_ge(p.sem("dv"), p.cnt["dv"])
        p.wait(nc.gpsimd, "dv")
        for dc in range(DC):
            p.inc(nc.gpsimd.dma_start(out=scr["x_fm"][dc, :, c0:c0 + TT],
                                      in_=xfm[dc][:]), "st", 16)
        ln_fm(p, cst, [x[:] for x in xfm], [x[:] for x in xnf], TT)
        p.wait(nc.gpsimd, "dv")
        for dc in range(DC):
            p.g("tensor_copy", out=obf[:], in_=xnf[dc][:])
            nc.sync.wait_ge(p.sem("po"), p.cnt["po"])
            p.inc(nc.sync.dma_start(out=scr["xn"][dc, :, c0:c0 + TT],
                                    in_=obf[:]), "la", 16)
            nc.gpsimd.wait_ge(p.sem("la"), p.cnt["la"])
        ln_fm(p, cst, [x[:] for x in xnf], [x[:] for x in x0f], TT,
              g=cst["ln0_g"], b=cst["ln0_b"])
        p.wait(nc.gpsimd, "dv")
        for dc in range(DC):
            p.inc(nc.gpsimd.dma_start(out=scr["x0"][dc, :, c0:c0 + TT],
                                      in_=x0f[dc][:]), "st", 16)
        ln_fm(p, cst, [x[:] for x in x0f], [x[:] for x in xn1f], TT,
              g=cst["pre_g"], b=cst["pre_b"])
        p.wait(nc.gpsimd, "dv")
        for dc in range(DC):
            p.g("tensor_copy", out=obf[:], in_=xn1f[dc][:])
            nc.sync.wait_ge(p.sem("po"), p.cnt["po"])
            p.inc(nc.sync.dma_start(out=scr["xn1"][s, dc, :, t0 + 1:t0 + 1 + TT],
                                    in_=obf[:]), "la", 16)
            nc.gpsimd.wait_ge(p.sem("la"), p.cnt["la"])
        p.tile_end()


def phase_proj(p, cst, io, scr, kind):
    nc = p.nc
    wkey, mixname, src_name, dst_name = {
        "k": ("L0_attn_tm_Wk", "mix_k", "xn1", "kk"),
        "v": ("L0_attn_tm_Wv", "mix_v", "xn1", "vv_"),
        "r": ("L0_attn_tm_Wr", "mix_r", "xn1", "sr"),
        "cmk": ("L0_attn_cm_Wk", "cmix_k", "xn2", "ck"),
    }[kind]
    mix = cst[mixname]
    wbf = [p.sbl(f"pj_w{i}", [128, HU], BF16) for i in range(DC)]
    win = [[p.sbl(f"pj_win{b}_{i}", [128, TT + 1], BF16) for i in range(DC)]
           for b in range(2)]
    xk = [[p.sbl(f"pj_xk{b}_{i}", [128, TT], BF16) for i in range(DC)]
          for b in range(2)]
    tmpd = [p.sbl(f"pj_tmp{i}", [128, TT], F32) for i in range(2)]
    etmp = [p.sbl(f"pj_etmp{i}", [128, TT], F32) for i in range(2)]
    p.stage_bf = [p.sbl(f"pj_stg{i}", [128, TT], BF16) for i in range(4)]

    load_weight_bf16(p, lambda i: io[wkey][i * 128:(i + 1) * 128, :], wbf, DC, HU)

    mix_done = [0, 0]
    mix_done_po = [0, 0]
    mm_done = [0, 0]
    for j in range(NJ):
        s, jj = j // NJS, j % NJS
        t0 = jj * TT
        c0 = s * T + t0
        par = j % 2
        lsn = "lc%d" % par
        nc.sync.wait_ge(p.sem("dv"), mix_done[par])
        nc.sync.wait_ge(p.sem("po"), mix_done_po[par])
        for dc in range(DC):
            p.inc(nc.sync.dma_start(out=win[par][dc][:],
                                    in_=scr[src_name][s, dc, :, t0:t0 + TT + 1]),
                  lsn, 16)
        lav = p.cnt[lsn]
        nc.vector.wait_ge(p.sem(lsn), lav)
        nc.gpsimd.wait_ge(p.sem(lsn), lav)
        nc.vector.wait_ge(p.sem("mm"), mm_done[par])
        nc.gpsimd.wait_ge(p.sem("mm"), mm_done[par])
        for dc in range(DC):
            xx = win[par][dc][:, 0:TT]
            xn_ = win[par][dc][:, 1:TT + 1]
            if dc % 2 == 0:
                p.v("tensor_tensor", out=tmpd[0][:], in0=xn_, in1=xx,
                    op=ALU.subtract)
                p.v("scalar_tensor_tensor", out=xk[par][dc][:], in0=tmpd[0][:],
                    scalar=mix[:, dc:dc + 1], in1=xx, op0=ALU.mult, op1=ALU.add)
            else:
                p.g("tensor_tensor", out=tmpd[1][:], in0=xn_, in1=xx,
                    op=ALU.subtract)
                p.g("tensor_scalar", out=tmpd[1][:], in0=tmpd[1][:],
                    scalar1=mix[:, dc:dc + 1], scalar2=None, op0=ALU.mult)
                p.g("tensor_tensor", out=xk[par][dc][:], in0=tmpd[1][:], in1=xx,
                    op=ALU.add)
        mix_done[par] = p.cnt["dv"]
        mix_done_po[par] = p.cnt["po"]
        nc.tensor.wait_ge(p.sem("dv"), mix_done[par])
        nc.tensor.wait_ge(p.sem("po"), mix_done_po[par])
        for cc in range(HC):
            b = cc % 4
            sl = cc % 4
            p.bank_acquire(b)
            for dc in range(DC):
                inst = nc.tensor.matmul(p.psum[b][:],
                                        wbf[dc][:, cc * 128:(cc + 1) * 128],
                                        xk[par][dc][:], start=(dc == 0),
                                        stop=(dc == DC - 1))
            mmv = p.inc(inst, "mm", 1)
            if kind == "k":
                nc.vector.wait_ge(p.sem("mm"), mmv)
                nc.vector.wait_ge(p.sem("ac"), p.cnt["ac"])
                p.v("tensor_scalar_min", etmp[par][:], p.psum[b][:], K_CLAMP)
                p.slot_guard(nc.scalar, sl)
                p.wait(nc.scalar, "dv")
                p.s("activation", out=p.stage_bf[sl][:], in_=etmp[par][:],
                    func=AF.Exp)
                p.bank_release(b, "dv")
            else:
                nc.scalar.wait_ge(p.sem("mm"), mmv)
                p.slot_guard(nc.scalar, sl)
                fn = {"v": AF.Copy, "r": AF.Sigmoid, "cmk": AF.Silu}[kind]
                p.s("activation", out=p.stage_bf[sl][:], in_=p.psum[b][:], func=fn)
                p.bank_release(b, "ac")
            nc.gpsimd.wait_ge(p.sem("ac"), p.cnt["ac"])
            p.store_slot(sl, out=scr[dst_name][cc, :, c0:c0 + TT],
                         in_=p.stage_bf[sl][:])
        mm_done[par] = p.cnt["mm"]
    p.tile_end()


def phase_scan(p, cst, io, scr):
    nc = p.nc
    kb = [p.sbl(f"sc_k{i}", [128, T], BF16) for i in range(2)]
    vb = [p.sbl(f"sc_v{i}", [128, T], BF16) for i in range(2)]
    sb_ = [p.sbl(f"sc_s{i}", [128, T], BF16) for i in range(2)]
    kv = [p.sbl(f"sc_kv{i}", [128, T], F32) for i in range(2)]
    skb = [p.sbl(f"sc_skb{i}", [128, T + 1], F32) for i in range(2)]
    skvb = [p.sbl(f"sc_skvb{i}", [128, T + 1], F32) for i in range(2)]
    wk = [p.sbl(f"sc_wk{i}", [128, T], F32) for i in range(2)]
    wkv = [p.sbl(f"sc_wkv{i}", [128, T], F32) for i in range(2)]
    ob = [p.sbl(f"sc_ob{i}", [128, T], BF16) for i in range(2)]
    for i in range(2):
        p.v("memset", skb[i][:, 0:1], 0.0)
        p.v("memset", skvb[i][:, 0:1], 0.0)
    dv_guard = [0, 0]
    ac_guard = [0, 0]
    po_guard = [0, 0]
    kv_guard = [0, 0]
    for u in range(NS * HC):
        s, cc = u // HC, u % HC
        par = u % 2
        c0 = s * T
        nc.sync.wait_ge(p.sem("dv"), dv_guard[par])
        nc.sync.wait_ge(p.sem("po"), po_guard[par])
        nc.sync.wait_ge(p.sem("ac"), ac_guard[par])
        lsn = "lc%d" % par
        for dst, srcn in [(kb[par], "kk"), (vb[par], "vv_"), (sb_[par], "sr")]:
            p.inc(nc.sync.dma_start(out=dst[:], in_=scr[srcn][cc, :, c0:c0 + T]),
                  lsn, 16)
        lav = p.cnt[lsn]
        dcol = cst["decay"][:, cc:cc + 1]
        fcol = cst["first"][:, cc:cc + 1]
        nc.vector.wait_ge(p.sem(lsn), lav)
        p.v("tensor_tensor_scan", out=skb[par][:, 1:T + 1],
            data0=dcol.broadcast_to((128, T)), data1=kb[par][:], initial=0.0,
            op0=ALU.mult, op1=ALU.add)
        nc.gpsimd.wait_ge(p.sem(lsn), lav)
        nc.gpsimd.wait_ge(p.sem("dv"), kv_guard[par])
        p.g("tensor_tensor", out=kv[par][:], in0=kb[par][:], in1=vb[par][:],
            op=ALU.mult)
        pkv = p.cnt["po"]
        nc.vector.wait_ge(p.sem("po"), pkv)
        p.v("tensor_tensor_scan", out=skvb[par][:, 1:T + 1],
            data0=dcol.broadcast_to((128, T)), data1=kv[par][:], initial=0.0,
            op0=ALU.mult, op1=ALU.add)
        kv_guard[par] = p.cnt["dv"]
        nc.vector.wait_ge(p.sem("ac"), ac_guard[par])
        p.v("scalar_tensor_tensor", out=wk[par][:], in0=kb[par][:], scalar=fcol,
            in1=skb[par][:, 0:T], op0=ALU.mult, op1=ALU.add)
        dv_guard[par] = p.cnt["dv"]
        nc.scalar.wait_ge(p.sem("dv"), p.cnt["dv"])
        p.s("activation", out=wk[par][:], in_=wk[par][:], func=AF.Ln,
            bias=col(p, cst, "k_eps"))
        p.s("activation", out=wk[par][:], in_=wk[par][:], func=AF.Exp,
            scale=-1.0)
        dvr = p.cnt["ac"]
        ac_guard[par] = dvr
        nc.gpsimd.wait_ge(p.sem("dv"), kv_guard[par])
        p.g("tensor_scalar", out=wkv[par][:], in0=kv[par][:], scalar1=fcol,
            scalar2=None, op0=ALU.mult)
        p.g("tensor_tensor", out=wkv[par][:], in0=wkv[par][:],
            in1=skvb[par][:, 0:T], op=ALU.add)
        nc.gpsimd.wait_ge(p.sem("ac"), dvr)
        p.g("tensor_tensor", out=wkv[par][:], in0=wkv[par][:], in1=wk[par][:],
            op=ALU.mult)
        p.slot_guard(nc.gpsimd, par)
        p.g("tensor_tensor", out=ob[par][:], in0=wkv[par][:], in1=sb_[par][:],
            op=ALU.mult)
        po_guard[par] = p.cnt["po"]
        nc.gpsimd.wait_ge(p.sem("po"), p.cnt["po"])
        p.store_slot(par, out=scr["rw"][cc, :, c0:c0 + T], in_=ob[par][:])
    p.tile_end()


def phase_wo(p, cst, io, scr):
    nc = p.nc
    wo = [p.sbl(f"wo_w{i}", [128, D], BF16) for i in range(HC)]
    rwt = [p.sbl(f"wo_rw{i}", [128, TT], BF16) for i in range(HC)]
    x0t = [p.sbl(f"wo_x0t{i}", [128, TT], F32) for i in range(DC)]
    x0n = [p.sbl(f"wo_x0n{i}", [128, TT], F32) for i in range(DC)]
    xn2f = [p.sbl(f"wo_xn2f{i}", [128, TT], F32) for i in range(DC)]
    obf = p.sbl("wo_obf", [128, TT], BF16)

    load_weight_bf16(p, lambda i: io["L0_attn_tm_Wo"][i * 128:(i + 1) * 128, :],
                     wo, HC, D)

    for j in range(NJ):
        s, jj = j // NJS, j % NJS
        t0 = jj * TT
        c0 = s * T + t0
        for cc in range(HC):
            p.inc(nc.sync.dma_start(out=rwt[cc][:],
                                    in_=scr["rw"][cc, :, c0:c0 + TT]), "la", 16)
        for dc in range(DC):
            p.inc(nc.sync.dma_start(out=x0t[dc][:],
                                    in_=scr["x0"][dc, :, c0:c0 + TT]), "la", 16)
        nc.tensor.wait_ge(p.sem("la"), p.cnt["la"])
        p.wait(nc.vector, "la")
        for dc in range(DC):
            b = dc % 4
            p.bank_acquire(b)
            for cc in range(HC):
                inst = nc.tensor.matmul(p.psum[b][:],
                                        wo[cc][:, dc * 128:(dc + 1) * 128],
                                        rwt[cc][:], start=(cc == 0),
                                        stop=(cc == HC - 1))
            mmv = p.inc(inst, "mm", 1)
            nc.vector.wait_ge(p.sem("mm"), mmv)
            p.v("tensor_tensor", out=x0n[dc][:], in0=x0t[dc][:], in1=p.psum[b][:],
                op=ALU.add)
            p.bank_release(b, "dv")
        p.wait(nc.gpsimd, "dv")
        for dc in range(DC):
            p.inc(nc.gpsimd.dma_start(out=scr["x0b"][dc, :, c0:c0 + TT],
                                      in_=x0n[dc][:]), "st", 16)
        ln_fm(p, cst, [x[:] for x in x0n], [x[:] for x in xn2f], TT,
              g=cst["post_g"], b=cst["post_b"])
        p.wait(nc.gpsimd, "dv")
        for dc in range(DC):
            p.g("tensor_copy", out=obf[:], in_=xn2f[dc][:])
            nc.sync.wait_ge(p.sem("po"), p.cnt["po"])
            p.inc(nc.sync.dma_start(out=scr["xn2"][s, dc, :, t0 + 1:t0 + 1 + TT],
                                    in_=obf[:]), "la", 16)
            nc.gpsimd.wait_ge(p.sem("la"), p.cnt["la"])
        p.tile_end()


def phase_cmv(p, cst, io, scr):
    nc = p.nc
    wv = [p.sbl(f"cv_wv{i}", [128, D], BF16) for i in range(HC)]
    wr = [p.sbl(f"cv_wr{i}", [128, D], BF16) for i in range(DC)]
    ckt = [p.sbl(f"cv_ck{i}", [128, TT], BF16) for i in range(HC)]
    win = [p.sbl(f"cv_win{i}", [128, TT + 1], BF16) for i in range(DC)]
    xr = [p.sbl(f"cv_xr{i}", [128, TT], BF16) for i in range(DC)]
    tmpd = [p.sbl(f"cv_tmp{i}", [128, TT], F32) for i in range(2)]
    x0t = [p.sbl(f"cv_x0t{i}", [128, TT], F32) for i in range(DC)]
    sg = p.sbl("cv_sg", [128, TT], F32)
    att = [p.sbl(f"cv_att{i}", [128, TT], F32) for i in range(2)]

    load_weight_bf16(p, lambda i: io["L0_attn_cm_Wv"][i * 128:(i + 1) * 128, :],
                     wv, HC, D)
    load_weight_bf16(p, lambda i: io["L0_attn_cm_Wr"][i * 128:(i + 1) * 128, :],
                     wr, DC, D)

    for j in range(NJ):
        s, jj = j // NJS, j % NJS
        t0 = jj * TT
        c0 = s * T + t0
        for cc in range(HC):
            p.inc(nc.sync.dma_start(out=ckt[cc][:],
                                    in_=scr["ck"][cc, :, c0:c0 + TT]), "la", 16)
        for dc in range(DC):
            p.inc(nc.sync.dma_start(out=win[dc][:],
                                    in_=scr["xn2"][s, dc, :, t0:t0 + TT + 1]),
                  "la", 16)
            p.inc(nc.sync.dma_start(out=x0t[dc][:],
                                    in_=scr["x0b"][dc, :, c0:c0 + TT]), "la", 16)
        p.wait(nc.vector, "la")
        p.wait(nc.gpsimd, "la")
        for dc in range(DC):
            xx = win[dc][:, 0:TT]
            xn_ = win[dc][:, 1:TT + 1]
            if dc % 2 == 0:
                p.v("tensor_tensor", out=tmpd[0][:], in0=xn_, in1=xx,
                    op=ALU.subtract)
                p.v("scalar_tensor_tensor", out=xr[dc][:], in0=tmpd[0][:],
                    scalar=cst["cmix_r"][:, dc:dc + 1], in1=xx,
                    op0=ALU.mult, op1=ALU.add)
            else:
                p.g("tensor_tensor", out=tmpd[1][:], in0=xn_, in1=xx,
                    op=ALU.subtract)
                p.g("tensor_scalar", out=tmpd[1][:], in0=tmpd[1][:],
                    scalar1=cst["cmix_r"][:, dc:dc + 1], scalar2=None,
                    op0=ALU.mult)
                p.g("tensor_tensor", out=xr[dc][:], in0=tmpd[1][:], in1=xx,
                    op=ALU.add)
        p.wait(nc.tensor, "dv")
        p.wait(nc.tensor, "po")
        for dc in range(DC):
            b = dc % 2
            br = 2 + dc % 2
            p.bank_acquire(b)
            for cc in range(HC):
                inst = nc.tensor.matmul(p.psum[b][:],
                                        wv[cc][:, dc * 128:(dc + 1) * 128],
                                        ckt[cc][:], start=(cc == 0),
                                        stop=(cc == HC - 1))
            p.inc(inst, "mm", 1)
            p.bank_acquire(br)
            for kc in range(DC):
                inst = nc.tensor.matmul(p.psum[br][:],
                                        wr[kc][:, dc * 128:(dc + 1) * 128],
                                        xr[kc][:], start=(kc == 0),
                                        stop=(kc == DC - 1))
            mmv = p.inc(inst, "mm", 1)
            nc.scalar.wait_ge(p.sem("mm"), mmv)
            p.wait(nc.scalar, "dv")
            p.s("activation", out=sg[:], in_=p.psum[br][:], func=AF.Sigmoid)
            p.bank_release(br, "ac")
            nc.vector.wait_ge(p.sem("ac"), p.cnt["ac"])
            a = att[dc % 2]
            p.slot_guard(nc.vector, dc % 2)
            p.v("tensor_tensor", out=a[:], in0=x0t[dc][:], in1=p.psum[b][:],
                op=ALU.add)
            p.bank_release(b, "dv")
            p.v("tensor_tensor", out=a[:], in0=a[:], in1=sg[:], op=ALU.add)
            nc.gpsimd.wait_ge(p.sem("dv"), p.cnt["dv"])
            p.store_slot(dc % 2, out=scr["at0"][dc, :, c0:c0 + TT], in_=a[:])
        p.tile_end()



def phase_route(p, cst, io, scr):
    nc = p.nc
    centT = [p.sbl(f"rt_ct{i}", [128, 64], F32) for i in range(DC)]
    xnf = [p.sbl(f"rt_xnf{i}", [128, TT], F32) for i in range(DC)]
    wsu = [p.sbl(f"rt_wsu{i}", [128, 512], BF16) for i in range(DC)]
    wsd = [p.sbl(f"rt_wsd{i}", [128, D], BF16) for i in range(4)]
    xnt = [p.sbl(f"rt_xn{i}", [128, TT], BF16) for i in range(DC)]
    xft = [p.sbl(f"rt_xf{i}", [128, TT], F32) for i in range(DC)]
    a0t = [p.sbl(f"rt_a0{i}", [128, TT], F32) for i in range(DC)]
    aff = p.sbl("rt_aff", [64, TT], F32)
    biased = p.sbl("rt_biased", [64, TT], F32)
    btm = p.sbl("rt_btm", [128, 4 * 64], F32)
    maxs = p.sbl("rt_maxs", [128, 8 * 4], F32)
    throw = p.sbl("rt_throw", [1, TT], F32)
    mask = p.sbl("rt_mask", [64, TT], F32)
    sgm = p.sbl("rt_sgm", [64, TT], F32)
    srow = p.sbl("rt_srow", [1, TT], F32)
    hs = [p.sbl(f"rt_hs{i}", [128, TT], BF16) for i in range(4)]
    accb = [p.sbl(f"rt_acc{i}", [128, TT], F32) for i in range(2)]
    wtt = p.sbl("rt_wtt", [64, TT], BF16)

    cstage = p.sbl("rt_cstage", [64, D], F32)
    ldv = p.inc(nc.sync.dma_start(out=cstage[:],
                                  in_=io["L0_ffn_centroids"][:]), "ws0", 16)
    nc.tensor.wait_ge(p.sem("ws0"), ldv)
    for dc in range(DC):
        b = dc % 2
        p.bank_acquire(b)
        mmv = p.inc(nc.tensor.transpose(
            p.psum[b][:, 0:64], cstage[0:64, dc * 128:(dc + 1) * 128],
            cst["idf"][0:64, 0:64]), "mm", 1)
        nc.vector.wait_ge(p.sem("mm"), mmv)
        p.v("tensor_copy", out=centT[dc][:], in_=p.psum[b][:, 0:64])
        p.bank_release(b, "dv")
    load_weight_bf16(p, lambda i: io["L0_ffn_Ws_up"][i * 128:(i + 1) * 128, :],
                     wsu, DC, 512)
    load_weight_bf16(p, lambda i: io["L0_ffn_Ws_down"][i * 128:(i + 1) * 128, :],
                     wsd, 4, D)

    for j in range(NJ):
        s, jj = j // NJS, j % NJS
        t0 = jj * TT
        c0 = s * T + t0
        for dc in range(DC):
            p.inc(nc.sync.dma_start(out=xnt[dc][:],
                                    in_=scr["xn"][dc, :, c0:c0 + TT]), "la", 16)
            p.inc(nc.sync.dma_start(out=xft[dc][:],
                                    in_=scr["x_fm"][dc, :, c0:c0 + TT]), "la", 16)
            p.inc(nc.sync.dma_start(out=a0t[dc][:],
                                    in_=scr["at0"][dc, :, c0:c0 + TT]), "la", 16)
        nc.tensor.wait_ge(p.sem("la"), p.cnt["la"])
        p.wait(nc.vector, "la")
        p.wait(nc.gpsimd, "la")
        ln_fm(p, cst, [x[:] for x in xft], [x[:] for x in xnf], TT)
        p.wait(nc.tensor, "dv")
        p.bank_acquire(0)
        for dc in range(DC):
            inst = nc.tensor.matmul(p.psum[0][0:64, :], centT[dc][:], xnf[dc][:],
                                    start=(dc == 0), stop=(dc == DC - 1))
        mmv = p.inc(inst, "mm", 1)
        nc.vector.wait_ge(p.sem("mm"), mmv)
        p.v("tensor_copy", out=aff[:], in_=p.psum[0][0:64, :])
        p.v("tensor_scalar", out=biased[:], in0=aff[:], scalar1=cst["eb"][:],
            scalar2=None, op0=ALU.add)
        p.bank_release(0, "dv")
        p.wait(nc.tensor, "dv")
        for a in range(4):
            b = 1 + a % 2
            p.bank_acquire(b)
            mmv = p.inc(nc.tensor.transpose(
                p.psum[b][:, 0:64], biased[:, a * 128:(a + 1) * 128],
                cst["idf"][0:64, 0:64]), "mm", 1)
            nc.vector.wait_ge(p.sem("mm"), mmv)
            p.v("tensor_copy", out=btm[:, a * 64:(a + 1) * 64],
                in_=p.psum[b][:, 0:64])
            p.bank_release(b, "dv")
        for a in range(4):
            p.v("max", out=maxs[:, a * 8:(a + 1) * 8],
                in_=btm[:, a * 64:(a + 1) * 64])
        p.wait(nc.tensor, "dv")
        for a in range(4):
            b = 1 + a % 2
            p.bank_acquire(b)
            mmv = p.inc(nc.tensor.transpose(
                p.psum[b][0:1, 0:128], maxs[:, a * 8 + 7:a * 8 + 8],
                cst["idf"][:]), "mm", 1)
            nc.vector.wait_ge(p.sem("mm"), mmv)
            p.v("tensor_copy", out=throw[0:1, a * 128:(a + 1) * 128],
                in_=p.psum[b][0:1, 0:128])
            p.bank_release(b, "dv")
        p.wait(nc.tensor, "dv")
        p.bank_acquire(2)
        mmv = p.inc(nc.tensor.matmul(p.psum[2][0:64, :], cst["ones_row"][:, 0:64],
                                     throw[:], start=True, stop=True), "mm", 1)
        nc.vector.wait_ge(p.sem("mm"), mmv)
        p.v("tensor_tensor", out=mask[:], in0=biased[:], in1=p.psum[2][0:64, :],
            op=ALU.is_ge)
        p.bank_release(2, "dv")
        p.wait(nc.scalar, "dv")
        p.s("activation", out=sgm[:], in_=aff[:], func=AF.Sigmoid)
        nc.vector.wait_ge(p.sem("ac"), p.cnt["ac"])
        p.v("tensor_tensor", out=mask[:], in0=mask[:], in1=sgm[:], op=ALU.mult)
        p.wait(nc.tensor, "dv")
        p.bank_acquire(3)
        mmv = p.inc(nc.tensor.matmul(p.psum[3][0:1, :], cst["ones_col"][0:64, :],
                                     mask[:], start=True, stop=True), "mm", 1)
        nc.vector.wait_ge(p.sem("mm"), mmv)
        p.v("tensor_scalar", out=srow[:], in0=p.psum[3][0:1, :], scalar1=K_EPS,
            scalar2=None, op0=ALU.add)
        p.v("reciprocal", out=srow[:], in_=srow[:])
        p.bank_release(3, "dv")
        p.wait(nc.tensor, "dv")
        p.bank_acquire(3)
        mmv = p.inc(nc.tensor.matmul(p.psum[3][0:64, :], cst["ones_row"][:, 0:64],
                                     srow[:], start=True, stop=True), "mm", 1)
        nc.vector.wait_ge(p.sem("mm"), mmv)
        p.v("tensor_tensor", out=wtt[:], in0=mask[:],
            in1=p.psum[3][0:64, :], op=ALU.mult)
        p.bank_release(3, "dv")
        nc.gpsimd.wait_ge(p.sem("dv"), p.cnt["dv"])
        p.inc(nc.gpsimd.dma_start(out=scr["dbg_wt"][0:64, c0:c0 + TT],
                                  in_=wtt[:]), "st", 16)
        for fc in range(4):
            b = 4 + fc % 2
            p.bank_acquire(b)
            for dc in range(DC):
                inst = nc.tensor.matmul(p.psum[b][:],
                                        wsu[dc][:, fc * 128:(fc + 1) * 128],
                                        xnt[dc][:], start=(dc == 0),
                                        stop=(dc == DC - 1))
            mmv = p.inc(inst, "mm", 1)
            nc.scalar.wait_ge(p.sem("mm"), mmv)
            p.wait(nc.scalar, "dv")
            p.s("activation", out=hs[fc][:], in_=p.psum[b][:], func=AF.Silu)
            p.bank_release(b, "ac")
        p.wait(nc.tensor, "ac")
        for dc in range(DC):
            b = 6 + dc % 2
            p.bank_acquire(b)
            for fc in range(4):
                inst = nc.tensor.matmul(p.psum[b][:],
                                        wsd[fc][:, dc * 128:(dc + 1) * 128],
                                        hs[fc][:], start=(fc == 0),
                                        stop=(fc == 3))
            mmv = p.inc(inst, "mm", 1)
            nc.vector.wait_ge(p.sem("mm"), mmv)
            ab = accb[dc % 2]
            p.slot_guard(nc.vector, dc % 2)
            p.v("tensor_tensor", out=ab[:], in0=xft[dc][:], in1=p.psum[b][:],
                op=ALU.add)
            p.bank_release(b, "dv")
            p.v("tensor_tensor", out=ab[:], in0=ab[:], in1=a0t[dc][:], op=ALU.add)
            nc.gpsimd.wait_ge(p.sem("dv"), p.cnt["dv"])
            p.store_slot(dc % 2, out=scr["acc0"][dc, :, c0:c0 + TT], in_=ab[:])
        p.tile_end()


def phase_exp0(p, cst, io, scr):
    nc = p.nc
    xft = [p.sbl(f"e0_xf{i}", [128, T], BF16) for i in range(DC)]
    acc = [p.sbl(f"e0_acc{i}", [128, T], F32) for i in range(DC)]
    wup = [[p.sbl(f"e0_wu{b}_{i}", [128, 512], BF16) for i in range(DC)]
           for b in range(2)]
    wdn = [[p.sbl(f"e0_wd{b}_{i}", [128, D], BF16) for i in range(4)]
           for b in range(2)]
    wsg = [p.sbl(f"e0_wsg{i}", [128, 1024], F32) for i in range(2)]
    hsb = [p.sbl(f"e0_h{i}", [128, TT], BF16) for i in range(4)]
    xn1f = [p.sbl(f"e0_xn1f{i}", [128, TT], F32) for i in range(DC)]
    obf = p.sbl("e0_obf", [128, TT], BF16)
    oh64 = p.sbl("e0_oh64", [64, 64 * 128], BF16)
    wts = p.sbl("e0_wts", [64, T], BF16)
    p.inc(nc.sync.dma_start(out=oh64[:], in_=cst["h_oh64"].ap()[:]), "la", 16)

    buf_free = [0, 0]
    buf_ready = [0, 0]
    stage_cast = [0, 0]

    def prefetch(e, par):
        units = ([("L0_ffn_W_up", i, 512, wup[par][i]) for i in range(DC)]
                 + [("L0_ffn_W_down", i, 1024, wdn[par][i]) for i in range(4)])
        nc.sync.wait_ge(p.sem("mm"), buf_free[par])
        for u, (key, i, cols, dst) in enumerate(units):
            k = u % 2
            sn = "we%d" % k
            nc.sync.wait_ge(p.sem("po"), stage_cast[k])
            ldv = p.inc(nc.sync.dma_start(
                out=wsg[k][:, 0:cols],
                in_=io[key][e, i * 128:(i + 1) * 128, :]), sn, 16)
            nc.gpsimd.wait_ge(p.sem(sn), ldv)
            p.g("tensor_copy", out=dst[:], in_=wsg[k][:, 0:cols])
            stage_cast[k] = p.cnt["po"]
        buf_ready[par] = p.cnt["po"]

    for s in range(NS):
        cs = s * T
        p.inc(nc.sync.dma_start(out=wts[:], in_=scr["dbg_wt"][0:64, cs:cs + T]),
              "la", 16)
        for dc in range(DC):
            p.inc(nc.sync.dma_start(out=xft[dc][:],
                                    in_=scr["xn"][dc, :, cs:cs + T]), "la", 16)
            p.inc(nc.sync.dma_start(out=acc[dc][:],
                                    in_=scr["acc0"][dc, :, cs:cs + T]), "la", 16)
        p.wait(nc.tensor, "la")
        p.wait(nc.vector, "la")
        if s == 0:
            prefetch(0, 0)
        for e in range(64):
            par = e % 2
            if e + 1 < 64:
                prefetch(e + 1, 1 - par)
            elif s == 0:
                prefetch(0, 1 - par)
            nc.tensor.wait_ge(p.sem("po"), buf_ready[par])
            for jj in range(NJS):
                t0 = jj * TT
                p.bank_acquire(4)
                mmv = p.inc(nc.tensor.matmul(
                    p.psum[4][:], oh64[:, e * 128:(e + 1) * 128],
                    wts[0:64, t0:t0 + TT], start=True, stop=True), "mm", 1)
                for fc in range(4):
                    b = fc % 2
                    p.bank_acquire(b)
                    for dc in range(DC):
                        inst = nc.tensor.matmul(
                            p.psum[b][:], wup[par][dc][:, fc * 128:(fc + 1) * 128],
                            xft[dc][:, t0:t0 + TT], start=(dc == 0),
                            stop=(dc == DC - 1))
                    mmv = p.inc(inst, "mm", 1)
                    nc.scalar.wait_ge(p.sem("mm"), mmv)
                    p.s("activation", out=hsb[fc][:], in_=p.psum[b][:],
                        func=AF.Silu)
                    p.bank_release(b, "ac")
                    nc.vector.wait_ge(p.sem("ac"), p.cnt["ac"])
                    p.v("tensor_tensor", out=hsb[fc][:], in0=hsb[fc][:],
                        in1=p.psum[4][:], op=ALU.mult)
                p.bank_release(4, "dv")
                p.wait(nc.tensor, "dv")
                for dc in range(DC):
                    b = 2 + dc % 2
                    p.bank_acquire(b)
                    for fc in range(4):
                        inst = nc.tensor.matmul(
                            p.psum[b][:], wdn[par][fc][:, dc * 128:(dc + 1) * 128],
                            hsb[fc][:], start=(fc == 0), stop=(fc == 3))
                    mmv = p.inc(inst, "mm", 1)
                    nc.vector.wait_ge(p.sem("mm"), mmv)
                    p.v("tensor_tensor", out=acc[dc][:, t0:t0 + TT],
                        in0=acc[dc][:, t0:t0 + TT], in1=p.psum[b][:], op=ALU.add)
                    p.bank_release(b, "dv")
            buf_free[par] = p.cnt["mm"]
        p.barrier()
        p.wait(nc.gpsimd, "dv")
        for dc in range(DC):
            p.inc(nc.gpsimd.dma_start(out=scr["y0"][dc, :, cs:cs + T],
                                      in_=acc[dc][:]), "st", 16)
        for jj in range(NJS):
            t0 = jj * TT
            ln_fm(p, cst, [acc[dc][:, t0:t0 + TT] for dc in range(DC)],
                  [x[:] for x in xn1f], TT)
            p.wait(nc.gpsimd, "dv")
            for dc in range(DC):
                p.g("tensor_copy", out=obf[:], in_=xn1f[dc][:])
                nc.sync.wait_ge(p.sem("po"), p.cnt["po"])
                p.inc(nc.sync.dma_start(out=scr["xn_1"][dc, :, cs + t0:cs + t0 + TT],
                                        in_=obf[:]), "la", 16)
                nc.gpsimd.wait_ge(p.sem("la"), p.cnt["la"])
            p.tile_end()
        p.tile_end()


def phase_qkv1(p, cst, io, scr):
    nc = p.nc
    wq = [p.sbl(f"q1_wq{i}", [128, D], BF16) for i in range(DC)]
    wk = [p.sbl(f"q1_wk{i}", [128, 128], BF16) for i in range(DC)]
    wv = [p.sbl(f"q1_wv{i}", [128, 128], BF16) for i in range(DC)]
    xnt = [p.sbl(f"q1_xn{i}", [128, TT], BF16) for i in range(DC)]

    load_weight_bf16(p, lambda i: io["L1_attn_Wq"][i * 128:(i + 1) * 128, :],
                     wq, DC, D)
    load_weight_bf16(p, lambda i: io["L1_attn_Wk"][i * 128:(i + 1) * 128, :],
                     wk, DC, 128)
    load_weight_bf16(p, lambda i: io["L1_attn_Wv"][i * 128:(i + 1) * 128, :],
                     wv, DC, 128)
    stg = [p.sbl(f"q1_stg{i}", [128, TT], BF16) for i in range(4)]

    for j in range(NJ):
        s, jj = j // NJS, j % NJS
        t0 = jj * TT
        c0 = s * T + t0
        for dc in range(DC):
            p.inc(nc.sync.dma_start(out=xnt[dc][:],
                                    in_=scr["xn_1"][dc, :, c0:c0 + TT]), "la", 16)
        nc.tensor.wait_ge(p.sem("la"), p.cnt["la"])
        for do in range(DC):
            b = do % 4
            sl = do % 4
            p.bank_acquire(b)
            for dc in range(DC):
                inst = nc.tensor.matmul(p.psum[b][:],
                                        wq[dc][:, do * 128:(do + 1) * 128],
                                        xnt[dc][:], start=(dc == 0),
                                        stop=(dc == DC - 1))
            mmv = p.inc(inst, "mm", 1)
            nc.vector.wait_ge(p.sem("mm"), mmv)
            p.slot_guard(nc.vector, sl)
            p.v("tensor_scalar", out=stg[sl][:], in0=p.psum[b][:],
                scalar1=cst["bq"][:, do:do + 1], scalar2=None, op0=ALU.add)
            p.bank_release(b, "dv")
            nc.gpsimd.wait_ge(p.sem("dv"), p.cnt["dv"])
            p.store_slot(sl, out=scr["q1"][do, :, c0:c0 + TT], in_=stg[sl][:])
        for which, wt, bias, dst in [(0, wk, "bk", "k1"), (1, wv, "bv", "v1")]:
            b = 4 + which
            sl = which
            p.bank_acquire(b)
            for dc in range(DC):
                inst = nc.tensor.matmul(p.psum[b][:], wt[dc][:], xnt[dc][:],
                                        start=(dc == 0), stop=(dc == DC - 1))
            mmv = p.inc(inst, "mm", 1)
            nc.vector.wait_ge(p.sem("mm"), mmv)
            p.slot_guard(nc.vector, sl)
            p.v("tensor_scalar", out=stg[sl][:], in0=p.psum[b][:],
                scalar1=cst[bias][:], scalar2=None, op0=ALU.add)
            p.bank_release(b, "dv")
            nc.gpsimd.wait_ge(p.sem("dv"), p.cnt["dv"])
            p.store_slot(sl, out=scr[dst][:, c0:c0 + TT], in_=stg[sl][:])
        p.tile_end()


def phase_att1(p, cst, io, scr):
    nc = p.nc
    PAD = 128
    TP = T + 2 * PAD
    NP = 4
    wo1 = [p.sbl(f"a1_wo{i}", [128, D], BF16) for i in range(DC)]
    qt = [p.sbl(f"a1_q{i}", [64, T], BF16) for i in range(16)]
    kp = [p.sbl(f"a1_kp{g}", [64, TP], BF16) for g in range(2)]
    vp = [p.sbl(f"a1_vp{g}", [64, TP], BF16) for g in range(2)]
    vtm = [p.sbl(f"a1_vtm{g}", [128, (TP // 128) * 64], BF16) for g in range(2)]
    band = p.sbl("a1_band", [128, 384], F32)
    sc_sb = [p.sbl(f"a1_sc{i}", [128, 384], F32) for i in range(NP)]
    ncol = [p.sbl(f"a1_nc{i}", [128, 1], F32) for i in range(NP)]
    ssum = [p.sbl(f"a1_ss{i}", [128, 1], F32) for i in range(NP)]
    rc = [p.sbl(f"a1_rc{i}", [128, 1], F32) for i in range(NP)]
    probs = [p.sbl(f"a1_pr{i}", [128, 384], BF16) for i in range(NP)]
    probsT = [p.sbl(f"a1_prT{i}", [128, 384], BF16) for i in range(NP)]
    ao = [p.sbl(f"a1_ao{i}", [128, 128], BF16) for i in range(DC)]
    att = [p.sbl(f"a1_att{i}", [128, 128], F32) for i in range(2)]

    p.inc(nc.sync.dma_start(out=band[:], in_=cst["h_band"].ap()[:]), "la", 16)
    load_weight_bf16(p, lambda i: io["L1_attn_Wo"][i * 128:(i + 1) * 128, :],
                     wo1, DC, D)

    st = {"exp": [0] * NP, "rm": [0] * NP, "tc_mm": [0] * NP,
          "scale": [0] * NP, "scmm": [0] * 16}

    for s in range(NS):
        cs = s * T
        p.g("memset", kp[0][:], 0.0)
        p.g("memset", kp[1][:], 0.0)
        p.g("memset", vp[0][:], 0.0)
        p.g("memset", vp[1][:], 0.0)
        nc.sync.wait_ge(p.sem("po"), p.cnt["po"])
        for g in range(2):
            p.inc(nc.sync.dma_start(out=kp[g][:, PAD:PAD + T],
                                    in_=scr["k1"][g * 64:(g + 1) * 64, cs:cs + T]),
                  "la", 16)
            p.inc(nc.sync.dma_start(out=vp[g][:, PAD:PAD + T],
                                    in_=scr["v1"][g * 64:(g + 1) * 64, cs:cs + T]),
                  "la", 16)
        for h in range(16):
            p.inc(nc.sync.dma_start(
                out=qt[h][:],
                in_=scr["q1"][h // 2, (h % 2) * 64:(h % 2) * 64 + 64, cs:cs + T]),
                "la", 16)
        nc.tensor.wait_ge(p.sem("la"), p.cnt["la"])
        for g in range(2):
            for wc in range(TP // 128):
                b = wc % 2
                p.bank_acquire(b)
                mmv = p.inc(nc.tensor.transpose(
                    p.psum[b][:].bitcast(BF16)[:, 0:64],
                    vp[g][:, wc * 128:(wc + 1) * 128],
                    cst["idb"][0:64, 0:64]), "mm", 1)
                nc.vector.wait_ge(p.sem("mm"), mmv)
                p.v("tensor_copy", out=vtm[g][:, wc * 64:(wc + 1) * 64],
                    in_=p.psum[b][:].bitcast(BF16)[:, 0:64])
                p.bank_release(b, "dv")
        p.wait(nc.tensor, "dv")
        wo_done = 0
        for i in range(T // 128):
            nc.vector.wait_ge(p.sem("mm"), wo_done)

            def emitA(h):
                g = h // 8
                b = h % 2
                p.bank_acquire(b)
                st["scmm"][h] = p.inc(nc.tensor.matmul(
                    p.psum[b][:, 0:384],
                    qt[h][:, i * 128:(i + 1) * 128],
                    kp[g][:, i * 128:i * 128 + 384],
                    start=True, stop=True), "mm", 1)

            def emitT1a(h):
                par = h % NP
                b = h % 2
                nc.vector.wait_ge(p.sem("mm"), st["scmm"][h])
                nc.vector.wait_ge(p.sem("ac"), st["exp"][par])
                p.v("tensor_tensor", out=sc_sb[par][:], in0=p.psum[b][:, 0:384],
                    in1=band[:], op=ALU.add)
                p.bank_release(b, "dv")
                p.v("reduce_max", out=ncol[par][:], in_=sc_sb[par][:], axis=AX.X,
                    negate=True)
                st["rm"][par] = p.cnt["dv"]

            def emitTX(h):
                par = h % NP
                nc.scalar.wait_ge(p.sem("dv"), st["rm"][par])
                nc.scalar.wait_ge(p.sem("mm"), st["tc_mm"][par])
                p.s("activation", out=probs[par][:], in_=sc_sb[par][:],
                    func=AF.Exp, bias=ncol[par][:], accum_out=ssum[par][:])
                st["exp"][par] = p.cnt["ac"]

            def emitTail2(h):
                par = h % NP
                nc.vector.wait_ge(p.sem("ac"), st["exp"][par])
                p.v("reciprocal", out=rc[par][:], in_=ssum[par][:])
                p.v("tensor_scalar", out=probs[par][:], in0=probs[par][:],
                    scalar1=rc[par][:], scalar2=None, op0=ALU.mult)
                st["scale"][par] = p.cnt["dv"]

            def emitB(h):
                par = h % NP
                g = h // 8
                ch, po_ = h // 2, (h % 2) * 64
                nc.tensor.wait_ge(p.sem("dv"), st["scale"][par])
                for wc in range(3):
                    b2 = 2 + wc % 2
                    p.bank_acquire(b2)
                    mmv = p.inc(nc.tensor.transpose(
                        p.psum[b2][:].bitcast(BF16)[:, 0:128],
                        probs[par][:, wc * 128:(wc + 1) * 128], cst["idb"][:]),
                        "mm", 1)
                    if wc % 2 == 0:
                        nc.vector.wait_ge(p.sem("mm"), mmv)
                        p.v("tensor_copy",
                            out=probsT[par][:, wc * 128:(wc + 1) * 128],
                            in_=p.psum[b2][:].bitcast(BF16)[:, 0:128])
                        p.bank_release(b2, "dv")
                    else:
                        nc.scalar.wait_ge(p.sem("mm"), mmv)
                        p.s("activation",
                            out=probsT[par][:, wc * 128:(wc + 1) * 128],
                            in_=p.psum[b2][:].bitcast(BF16)[:, 0:128],
                            func=AF.Copy)
                        p.bank_release(b2, "ac")
                st["tc_mm"][par] = p.cnt["mm"]
                nc.tensor.wait_ge(p.sem("dv"), p.cnt["dv"])
                nc.tensor.wait_ge(p.sem("ac"), p.cnt["ac"])
                b3 = 4 + h % 2
                p.bank_acquire(b3)
                for wc in range(3):
                    inst = nc.tensor.matmul(
                        p.psum[b3][0:64, 0:128],
                        vtm[g][:, (i + wc) * 64:(i + wc + 1) * 64],
                        probsT[par][:, wc * 128:(wc + 1) * 128],
                        start=(wc == 0), stop=(wc == 2))
                mmv = p.inc(inst, "mm", 1)
                nc.vector.wait_ge(p.sem("mm"), mmv)
                p.v("tensor_copy", out=ao[ch][po_:po_ + 64, :],
                    in_=p.psum[b3][0:64, 0:128])
                p.bank_release(b3, "dv")

            for h in range(16):
                emitA(h)
                emitT1a(h)
                if h >= 1:
                    emitTX(h - 1)
                if h >= 2:
                    emitB(h - 2)
                if h >= 1:
                    emitTail2(h - 1)
            emitTX(15)
            emitTail2(15)
            emitB(14)
            emitB(15)
            p.wait(nc.tensor, "dv")
            for do in range(DC):
                b = 6 + do % 2
                p.bank_acquire(b)
                for dc in range(DC):
                    inst = nc.tensor.matmul(p.psum[b][:, 0:128],
                                            wo1[dc][:, do * 128:(do + 1) * 128],
                                            ao[dc][:], start=(dc == 0),
                                            stop=(dc == DC - 1))
                mmv = p.inc(inst, "mm", 1)
                nc.vector.wait_ge(p.sem("mm"), mmv)
                a = att[do % 2]
                p.slot_guard(nc.vector, do % 2)
                p.v("tensor_scalar", out=a[:], in0=p.psum[b][:, 0:128],
                    scalar1=cst["bo"][:, do:do + 1], scalar2=None, op0=ALU.add)
                p.bank_release(b, "dv")
                nc.gpsimd.wait_ge(p.sem("dv"), p.cnt["dv"])
                p.store_slot(do % 2,
                             out=scr["at1"][do, :, cs + i * 128:cs + (i + 1) * 128],
                             in_=a[:])
            wo_done = p.cnt["mm"]
        p.tile_end()


def phase_msaqkv(p, cst, io, scr):
    nc = p.nc
    wqkv = [p.sbl(f"mq_w{i}", [128, 3 * D], BF16) for i in range(DC)]
    xnt = [p.sbl(f"mq_xn{i}", [128, TT], BF16) for i in range(DC)]
    xf32 = [p.sbl(f"mq_xf{i}", [128, TT], F32) for i in range(DC)]
    xln = [p.sbl(f"mq_xln{i}", [128, TT], F32) for i in range(DC)]
    xlb = [p.sbl(f"mq_xlb{i}", [128, TT], BF16) for i in range(DC)]
    vfm = [p.sbl(f"mq_vfm{i}", [128, TT], BF16) for i in range(DC)]
    vtms = [p.sbl(f"mq_vtm{i}", [128, D], BF16) for i in range(4)]
    stg = [p.sbl(f"mq_stg{i}", [128, TT], BF16) for i in range(4)]

    load_weight_bf16(p, lambda i: io["L1_ffn_w_qkv"][i * 128:(i + 1) * 128, :],
                     wqkv, DC, 3 * D)

    for j in range(NJ):
        s, jj = j // NJS, j % NJS
        t0 = jj * TT
        c0 = s * T + t0
        for dc in range(DC):
            p.inc(nc.sync.dma_start(out=xnt[dc][:],
                                    in_=scr["xn_1"][dc, :, c0:c0 + TT]), "la", 16)
        p.wait(nc.gpsimd, "la")
        p.wait(nc.vector, "la")
        for dc in range(DC):
            p.g("tensor_copy", out=xf32[dc][:], in_=xnt[dc][:])
        ln_fm(p, cst, [x[:] for x in xf32], [x[:] for x in xln], TT,
              g=cst["ln1_w"], b=cst["ln1_b"])
        p.wait(nc.gpsimd, "dv")
        for dc in range(DC):
            p.g("tensor_copy", out=xlb[dc][:], in_=xln[dc][:])
        p.wait(nc.tensor, "po")
        for oc in range(24):
            b = oc % 4
            sl = oc % 4
            p.bank_acquire(b)
            for dc in range(DC):
                inst = nc.tensor.matmul(p.psum[b][:],
                                        wqkv[dc][:, oc * 128:(oc + 1) * 128],
                                        xlb[dc][:], start=(dc == 0),
                                        stop=(dc == DC - 1))
            mmv = p.inc(inst, "mm", 1)
            if oc < 16:
                nc.scalar.wait_ge(p.sem("mm"), mmv)
                p.slot_guard(nc.scalar, sl)
                p.s("activation", out=stg[sl][:], in_=p.psum[b][:], func=AF.Copy)
                p.bank_release(b, "ac")
                nc.gpsimd.wait_ge(p.sem("ac"), p.cnt["ac"])
                dst = scr["mq"] if oc < 8 else scr["mk_"]
                p.store_slot(sl, out=dst[oc % 8, :, c0:c0 + TT], in_=stg[sl][:])
            else:
                nc.scalar.wait_ge(p.sem("mm"), mmv)
                p.wait(nc.scalar, "mm")
                p.s("activation", out=vfm[oc - 16][:], in_=p.psum[b][:],
                    func=AF.Copy)
                p.bank_release(b, "ac")
        p.wait(nc.tensor, "ac")
        for a in range(4):
            for dc in range(DC):
                b = 4 + (a * DC + dc) % 2
                p.bank_acquire(b)
                mmv = p.inc(nc.tensor.transpose(
                    p.psum[b][:].bitcast(BF16)[:, 0:128],
                    vfm[dc][:, a * 128:(a + 1) * 128], cst["idb"][:]), "mm", 1)
                nc.vector.wait_ge(p.sem("mm"), mmv)
                p.v("tensor_copy", out=vtms[a][:, dc * 128:(dc + 1) * 128],
                    in_=p.psum[b][:].bitcast(BF16)[:, 0:128])
                p.bank_release(b, "dv")
        nc.gpsimd.wait_ge(p.sem("dv"), p.cnt["dv"])
        for a in range(4):
            r = (t0 + a * 128) // 128
            p.inc(nc.gpsimd.dma_start(out=scr["vtm"][s, r], in_=vtms[a][:]),
                  "st", 16)
        p.tile_end()


def phase_msaatt(p, cst, io, scr):
    nc = p.nc
    NP = 4
    mqs = [p.sbl(f"ma_q{i}", [128, T], BF16) for i in range(DC)]
    mks = [p.sbl(f"ma_k{i}", [128, T], BF16) for i in range(DC)]
    vt = [p.sbl(f"ma_vt{i}", [128, D], BF16) for i in range(T // 128)]
    sc_sb = [p.sbl(f"ma_sc{i}", [128, T], F32) for i in range(NP)]
    probs = [p.sbl(f"ma_pr{i}", [128, T], BF16) for i in range(NP)]
    probsT = [p.sbl(f"ma_prT{i}", [128, T], BF16) for i in range(NP)]
    nmax = [p.sbl(f"ma_nmax{i}", [128, 1], F32) for i in range(NP)]
    ssum = [p.sbl(f"ma_ss{i}", [128, 1], F32) for i in range(NP)]
    rc = [p.sbl(f"ma_rc{i}", [128, 1], F32) for i in range(NP)]
    mo = [p.sbl(f"ma_mo{i}", [128, 128], BF16) for i in range(DC)]

    st = {"exp": [0] * NP, "rm": [0] * NP, "tc_mm": [0] * NP,
          "vmm": [0] * NP, "scale": [0] * NP, "scmm": [[0] * 4 for _ in range(16)]}

    for s in range(NS):
        cs = s * T
        for ch in range(DC):
            p.inc(nc.sync.dma_start(
                out=mqs[ch][:], in_=scr["mq"][ch, :, cs:cs + T]), "la", 16)
            p.inc(nc.sync.dma_start(
                out=mks[ch][:], in_=scr["mk_"][ch, :, cs:cs + T]), "la", 16)
        for wc in range(T // 128):
            p.inc(nc.sync.dma_start(out=vt[wc][:], in_=scr["vtm"][s, wc]),
                  "la", 16)
        nc.tensor.wait_ge(p.sem("la"), p.cnt["la"])
        p.wait(nc.vector, "la")
        p.wait(nc.scalar, "la")
        for i in range(T // 128):
            nc.vector.wait_ge(p.sem("st0"), p.cnt.get("st0", 0))

            def emitA(h):
                par = h % NP
                ch, po_ = h // 2, (h % 2) * 64
                for fq in range(4):
                    p.bank_acquire(fq)
                    st["scmm"][h][fq] = p.inc(nc.tensor.matmul(
                        p.psum[fq][:],
                        mqs[ch][po_:po_ + 64, i * 128:(i + 1) * 128],
                        mks[ch][po_:po_ + 64, fq * TT:(fq + 1) * TT],
                        start=True, stop=True), "mm", 1)

            def emitT1a(h):
                par = h % NP
                nc.scalar.wait_ge(p.sem("dv"), st["rm"][par])
                nc.vector.wait_ge(p.sem("mm"), st["scmm"][h][0])
                nc.vector.wait_ge(p.sem("ac"), st["exp"][par])
                p.v("tensor_copy", out=sc_sb[par][:, 0:TT], in_=p.psum[0][:])
                p.bank_release(0, "dv")
                for fq in range(1, 4):
                    nc.scalar.wait_ge(p.sem("mm"), st["scmm"][h][fq])
                    p.s("activation", out=sc_sb[par][:, fq * TT:(fq + 1) * TT],
                        in_=p.psum[fq][:], func=AF.Copy)
                    p.bank_release(fq, "ac")
                nc.vector.wait_ge(p.sem("ac"), p.cnt["ac"])
                p.v("reduce_max", out=nmax[par][:], in_=sc_sb[par][:],
                    axis=AX.X, negate=True)
                p.v("tensor_scalar", out=nmax[par][:], in0=nmax[par][:],
                    scalar1=0.125, scalar2=None, op0=ALU.mult)
                st["rm"][par] = p.cnt["dv"]

            def emitTX(h):
                par = h % NP
                nc.scalar.wait_ge(p.sem("dv"), st["rm"][par])
                nc.scalar.wait_ge(p.sem("mm"), st["tc_mm"][par])
                p.s("activation", out=probs[par][:], in_=sc_sb[par][:],
                    func=AF.Exp, scale=0.125, bias=nmax[par][:],
                    accum_out=ssum[par][:])
                st["exp"][par] = p.cnt["ac"]

            def emitTail2(h):
                par = h % NP
                nc.vector.wait_ge(p.sem("ac"), st["exp"][par])
                p.v("reciprocal", out=rc[par][:], in_=ssum[par][:])
                p.v("tensor_scalar", out=probs[par][:], in0=probs[par][:],
                    scalar1=rc[par][:], scalar2=None, op0=ALU.mult)
                st["scale"][par] = p.cnt["dv"]

            def emitB(h):
                par = h % NP
                ch, po_ = h // 2, (h % 2) * 64
                nc.tensor.wait_ge(p.sem("dv"), st["scale"][par])
                for wc in range(T // 128):
                    b2 = 4 + wc % 2
                    p.bank_acquire(b2)
                    mmv = p.inc(nc.tensor.transpose(
                        p.psum[b2][:].bitcast(BF16)[:, 0:128],
                        probs[par][:, wc * 128:(wc + 1) * 128], cst["idb"][:]),
                        "mm", 1)
                    if wc % 2 == 0:
                        nc.vector.wait_ge(p.sem("mm"), mmv)
                        p.v("tensor_copy",
                            out=probsT[par][:, wc * 128:(wc + 1) * 128],
                            in_=p.psum[b2][:].bitcast(BF16)[:, 0:128])
                        p.bank_release(b2, "dv")
                    else:
                        nc.scalar.wait_ge(p.sem("mm"), mmv)
                        p.s("activation",
                            out=probsT[par][:, wc * 128:(wc + 1) * 128],
                            in_=p.psum[b2][:].bitcast(BF16)[:, 0:128],
                            func=AF.Copy)
                        p.bank_release(b2, "ac")
                st["tc_mm"][par] = p.cnt["mm"]
                nc.tensor.wait_ge(p.sem("dv"), p.cnt["dv"])
                nc.tensor.wait_ge(p.sem("ac"), p.cnt["ac"])
                b3 = 6 + h % 2
                p.bank_acquire(b3)
                for wc in range(T // 128):
                    inst = nc.tensor.matmul(
                        p.psum[b3][0:64, 0:128],
                        vt[wc][:, h * 64:(h + 1) * 64],
                        probsT[par][:, wc * 128:(wc + 1) * 128],
                        start=(wc == 0), stop=(wc == T // 128 - 1))
                mmv = p.inc(inst, "mm", 1)
                st["vmm"][par] = mmv
                nc.vector.wait_ge(p.sem("mm"), mmv)
                p.v("tensor_copy", out=mo[ch][po_:po_ + 64, :],
                    in_=p.psum[b3][0:64, 0:128])
                p.bank_release(b3, "dv")

            for h in range(16):
                emitA(h)
                emitT1a(h)
                if h >= 1:
                    emitTX(h - 1)
                if h >= 2:
                    emitB(h - 2)
                if h >= 1:
                    emitTail2(h - 1)
            emitTX(15)
            emitTail2(15)
            emitB(14)
            emitB(15)
            nc.gpsimd.wait_ge(p.sem("dv"), p.cnt["dv"])
            p.wait(nc.gpsimd, "st0")
            for dc in range(DC):
                p.store_slot(0, out=scr["mo"][dc, :, cs + i * 128:cs + (i + 1) * 128],
                             in_=mo[dc][:])
        p.tile_end()


def phase_msaproj(p, cst, io, scr):
    nc = p.nc
    wp = [p.sbl(f"mp_w{i}", [128, D], BF16) for i in range(DC)]
    gwT = [p.sbl(f"mp_gw{i}", [128, 8], F32) for i in range(DC)]
    mot = [p.sbl(f"mp_mo{i}", [128, TT], BF16) for i in range(DC)]
    xnt = [p.sbl(f"mp_xn{i}", [128, TT], BF16) for i in range(DC)]
    y0t = [p.sbl(f"mp_y0{i}", [128, TT], F32) for i in range(2)]
    a1t = [p.sbl(f"mp_a1{i}", [128, TT], F32) for i in range(2)]
    xf1 = [p.sbl(f"mp_xf1{i}", [128, TT], F32) for i in range(DC)]
    xgf = [p.sbl(f"mp_xg{i}", [128, TT], F32) for i in range(DC)]
    obf = p.sbl("mp_obf", [128, TT], BF16)
    pot = [p.sbl(f"mp_po{i}", [128, TT], F32) for i in range(2)]
    lg = p.sbl("mp_lg", [8, TT], F32)
    ltm = p.sbl("mp_ltm", [128, 8], F32)
    lmax = p.sbl("mp_lmax", [128, 1], F32)
    ohtm = p.sbl("mp_ohtm", [128, 8], F32)
    ohr = p.sbl("mp_ohr", [8, TT], BF16)

    load_weight_bf16(p, lambda i: io["L1_ffn_w_proj"][i * 128:(i + 1) * 128, :],
                     wp, DC, D)
    for dc in range(DC):
        p.inc(nc.sync.dma_start(out=gwT[dc][:],
                                in_=io["L1_ffn_gate_w"][dc * 128:(dc + 1) * 128, :]),
              "la", 16)

    for j in range(NJ):
        s, jj = j // NJS, j % NJS
        t0 = jj * TT
        c0 = s * T + t0
        for dc in range(DC):
            p.inc(nc.sync.dma_start(out=mot[dc][:],
                                    in_=scr["mo"][dc, :, c0:c0 + TT]), "la", 16)
            p.inc(nc.sync.dma_start(out=xnt[dc][:],
                                    in_=scr["xn_1"][dc, :, c0:c0 + TT]), "la", 16)
        nc.tensor.wait_ge(p.sem("la"), p.cnt["la"])
        p.wait(nc.gpsimd, "la")
        p.wait(nc.vector, "la")
        for dc in range(DC):
            p.g("tensor_copy", out=xf1[dc][:], in_=xnt[dc][:])
        p.wait(nc.vector, "po")
        for do in range(DC):
            b = do % 4
            p.bank_acquire(b)
            for dc in range(DC):
                inst = nc.tensor.matmul(p.psum[b][:],
                                        wp[dc][:, do * 128:(do + 1) * 128],
                                        mot[dc][:], start=(dc == 0),
                                        stop=(dc == DC - 1))
            mmv = p.inc(inst, "mm", 1)
            nc.vector.wait_ge(p.sem("mm"), mmv)
            p.v("tensor_tensor", out=xf1[do][:], in0=xf1[do][:],
                in1=p.psum[b][:], op=ALU.add)
            p.bank_release(b, "dv")
        p.wait(nc.gpsimd, "dv")
        for dc in range(DC):
            p.inc(nc.gpsimd.dma_start(out=scr["xf1"][dc, :, c0:c0 + TT],
                                      in_=xf1[dc][:]), "st", 16)
        ln_fm(p, cst, [x[:] for x in xf1], [x[:] for x in xgf], TT,
              g=cst["ln2_w"], b=cst["ln2_b"])
        p.wait(nc.gpsimd, "dv")
        for dc in range(DC):
            p.g("tensor_copy", out=obf[:], in_=xgf[dc][:])
            nc.sync.wait_ge(p.sem("po"), p.cnt["po"])
            p.inc(nc.sync.dma_start(out=scr["xg"][dc, :, c0:c0 + TT],
                                    in_=obf[:]), "la", 16)
            nc.gpsimd.wait_ge(p.sem("la"), p.cnt["la"])
        for dc in range(DC):
            p.wait(nc.sync, "dv")
            p.slot_guard(nc.sync, 2 + dc % 2)
            p.inc(nc.sync.dma_start(out=y0t[dc % 2][:],
                                    in_=scr["y0"][dc, :, c0:c0 + TT]), "lb", 16)
            p.inc(nc.sync.dma_start(out=a1t[dc % 2][:],
                                    in_=scr["at1"][dc, :, c0:c0 + TT]), "lb", 16)
            p.wait(nc.vector, "lb")
            po_ = pot[dc % 2]
            p.v("tensor_tensor", out=po_[:], in0=y0t[dc % 2][:],
                in1=a1t[dc % 2][:], op=ALU.add)
            p.v("tensor_tensor", out=po_[:], in0=po_[:], in1=xf1[dc][:],
                op=ALU.add)
            nc.gpsimd.wait_ge(p.sem("dv"), p.cnt["dv"])
            p.store_slot(2 + dc % 2, out=scr["pre_out"][dc, :, c0:c0 + TT],
                         in_=po_[:])
        p.wait(nc.tensor, "dv")
        p.bank_acquire(6)
        for dc in range(DC):
            inst = nc.tensor.matmul(p.psum[6][0:8, :], gwT[dc][:], xgf[dc][:],
                                    start=(dc == 0), stop=(dc == DC - 1))
        mmv = p.inc(inst, "mm", 1)
        nc.vector.wait_ge(p.sem("mm"), mmv)
        p.v("tensor_scalar", out=lg[:], in0=p.psum[6][0:8, :],
            scalar1=cst["gate_b"][:], scalar2=None, op0=ALU.add)
        p.bank_release(6, "dv")
        for a in range(4):
            p.wait(nc.tensor, "dv")
            p.bank_acquire(7)
            mmv = p.inc(nc.tensor.transpose(
                p.psum[7][:, 0:8], lg[:, a * 128:(a + 1) * 128],
                cst["idf"][0:8, 0:8]), "mm", 1)
            nc.vector.wait_ge(p.sem("mm"), mmv)
            p.v("tensor_copy", out=ltm[:], in_=p.psum[7][:, 0:8])
            p.bank_release(7, "dv")
            p.v("reduce_max", out=lmax[:], in_=ltm[:], axis=AX.X)
            p.v("tensor_scalar", out=ohtm[:], in0=ltm[:], scalar1=lmax[:],
                scalar2=None, op0=ALU.is_ge)
            p.wait(nc.tensor, "dv")
            p.bank_acquire(7)
            mmv = p.inc(nc.tensor.transpose(
                p.psum[7][0:8, 0:128], ohtm[:], cst["idf"][:]), "mm", 1)
            nc.vector.wait_ge(p.sem("mm"), mmv)
            p.v("tensor_copy", out=ohr[0:8, a * 128:(a + 1) * 128],
                in_=p.psum[7][0:8, 0:128])
            p.bank_release(7, "dv")
        nc.gpsimd.wait_ge(p.sem("dv"), p.cnt["dv"])
        p.inc(nc.gpsimd.dma_start(out=scr["w1h"][0:8, c0:c0 + TT],
                                  in_=ohr[:]), "st", 16)
        p.tile_end()


def phase_exp1(p, cst, io, scr):
    nc = p.nc
    wu1 = [p.sbl(f"e1_wu{i}", [128, HU], BF16) for i in range(DC)]
    wd1 = [p.sbl(f"e1_wd{i}", [128, D], BF16) for i in range(HC)]
    hsb = [p.sbl(f"e1_h{i}", [128, TT], BF16) for i in range(HC)]
    xgt = [[p.sbl(f"e1_xg{b}_{i}", [128, TT], BF16) for i in range(DC)]
           for b in range(2)]
    pot = [p.sbl(f"e1_po{i}", [128, TT], F32) for i in range(2)]
    oh8e = [p.sbl(f"e1_oh8{b}", [8, 128], BF16) for b in range(2)]
    w1h = [p.sbl(f"e1_w1h{b}", [8, TT], BF16) for b in range(2)]
    msk = [p.sbl(f"e1_msk{b}", [128, TT], BF16) for b in range(2)]
    p.inc(nc.sync.dma_start(out=oh8e[0][:],
                            in_=cst["h_oh8"].ap()[0:8, 0:128]), "la", 16)
    p.wsg2 = [p.sbl(f"e1_wsg2{i}", [128, 512], F32) for i in range(2)]
    p.wf_guard = [0, 0]

    h_end = [0, 0]
    y_end = [0]
    mask_po = [0, 0]
    h_last = [0]

    def load_w1(e):
        nc.sync.wait_ge(p.sem("mm"), h_last[0])
        for i in range(DC):
            for hh in range(8):
                k = (i * 8 + hh) % 2
                if p.cnt.get("we%d" % k, 0):
                    nc.sync.wait_ge(p.sem("dv"), p.wcast_guard[k][0])
                ldv = p.inc(nc.sync.dma_start(
                    out=p.wstage[k][:],
                    in_=io["L1_ffn_w1"][e, i * 128:(i + 1) * 128,
                                        hh * 512:(hh + 1) * 512]),
                    "we%d" % k, 16)
                nc.vector.wait_ge(p.sem("we%d" % k), ldv)
                p.v("tensor_copy", out=wu1[i][:, hh * 512:(hh + 1) * 512],
                    in_=p.wstage[k][:])
                p.wcast_guard[k] = (p.cnt["dv"], p.cnt["po"])

    def load_w2(e):
        nc.gpsimd.wait_ge(p.sem("mm"), y_end[0])
        for u in range(2 * HC):
            i, hh = u // 2, u % 2
            k = u % 2
            if p.cnt.get("wf%d" % k, 0):
                nc.gpsimd.wait_ge(p.sem("po"), p.wf_guard[k])
            ldv = p.inc(nc.gpsimd.dma_start(
                out=p.wsg2[k][:],
                in_=io["L1_ffn_w2"][e, i * 128:(i + 1) * 128,
                                    hh * 512:(hh + 1) * 512]),
                "wf%d" % k, 16)
            nc.gpsimd.wait_ge(p.sem("wf%d" % k), ldv)
            p.g("tensor_copy", out=wd1[i][:, hh * 512:(hh + 1) * 512],
                in_=p.wsg2[k][:])
            p.wf_guard[k] = p.cnt["po"]

    def prep(e, j):
        par = j % 2
        s, jj = j // NJS, j % NJS
        c0 = s * T + jj * TT
        nc.sync.wait_ge(p.sem("mm"), h_end[par])
        for dc in range(DC):
            p.inc(nc.sync.dma_start(out=xgt[par][dc][:],
                                    in_=scr["xg"][dc, :, c0:c0 + TT]),
                  "lc%d" % par, 16)
        p.inc(nc.sync.dma_start(out=w1h[par][:],
                                in_=scr["w1h"][0:8, c0:c0 + TT]),
              "lc%d" % par, 16)
        lav = p.cnt["lc%d" % par]
        nc.tensor.wait_ge(p.sem("lc%d" % par), lav)
        nc.tensor.wait_ge(p.sem("la"), p.cnt["la"])
        p.bank_acquire(4)
        mmv = p.inc(nc.tensor.matmul(
            p.psum[4][:], oh8e[e % 2][:],
            w1h[par][0:8, :], start=True, stop=True), "mm", 1)
        nc.vector.wait_ge(p.sem("mm"), mmv)
        nc.vector.wait_ge(p.sem("po"), mask_po[par])
        p.v("tensor_copy", out=msk[par][:], in_=p.psum[4][:])
        p.bank_release(4, "dv")
        nc.gpsimd.wait_ge(p.sem("lc%d" % par), lav)
        nc.gpsimd.wait_ge(p.sem("dv"), p.cnt["dv"])
        for dc in range(DC):
            p.g("tensor_tensor", out=xgt[par][dc][:], in0=xgt[par][dc][:],
                in1=msk[par][:], op=ALU.mult)
        mask_po[par] = p.cnt["po"]

    for e in range(8):
        if e + 1 < 8:
            nc.sync.wait_ge(p.sem("mm"), h_last[0])
            p.inc(nc.sync.dma_start(
                out=oh8e[(e + 1) % 2][:],
                in_=cst["h_oh8"].ap()[0:8, (e + 1) * 128:(e + 2) * 128]),
                "la", 16)
        load_w1(e)
        load_w2(e)
        if e == 0:
            prep(0, 0)
        for j in range(NJ):
            par = j % 2
            s, jj = j // NJS, j % NJS
            c0 = s * T + jj * TT
            nc.tensor.wait_ge(p.sem("po"), mask_po[par])
            nc.tensor.wait_ge(p.sem("dv"), p.wcast_guard[0][0])
            nc.tensor.wait_ge(p.sem("dv"), p.wcast_guard[1][0])
            nc.tensor.wait_ge(p.sem("po"), p.wf_guard[0])
            nc.tensor.wait_ge(p.sem("po"), p.wf_guard[1])
            for fc in range(HC):
                b = fc % 2
                p.bank_acquire(b)
                for dc in range(DC):
                    inst = nc.tensor.matmul(
                        p.psum[b][:], wu1[dc][:, fc * 128:(fc + 1) * 128],
                        xgt[par][dc][:], start=(dc == 0), stop=(dc == DC - 1))
                mmv = p.inc(inst, "mm", 1)
                nc.scalar.wait_ge(p.sem("mm"), mmv)
                p.s("activation", out=hsb[fc][:], in_=p.psum[b][:],
                    func=AF.Gelu_apprx_tanh)
                p.bank_release(b, "ac")
            h_end[par] = p.cnt["mm"]
            h_last[0] = h_end[par]
            if j + 1 < NJ:
                prep(e, j + 1)
            elif e + 1 < 8:
                prep(e + 1, 0)
            p.wait(nc.tensor, "ac")
            for dc in range(DC):
                nc.sync.wait_ge(p.sem("st%d" % (dc % 2)),
                                p.cnt.get("st%d" % (dc % 2), 0))
                p.inc(nc.sync.dma_start(out=pot[dc % 2][:],
                                        in_=scr["pre_out"][dc, :, c0:c0 + TT]),
                      "lb%d" % (dc % 2), 16)
                b = 2 + dc % 2
                p.bank_acquire(b)
                for fc in range(HC):
                    inst = nc.tensor.matmul(
                        p.psum[b][:], wd1[fc][:, dc * 128:(dc + 1) * 128],
                        hsb[fc][:], start=(fc == 0), stop=(fc == HC - 1))
                mmv = p.inc(inst, "mm", 1)
                nc.vector.wait_ge(p.sem("mm"), mmv)
                p.wait(nc.vector, "lb%d" % (dc % 2))
                p.v("tensor_tensor", out=pot[dc % 2][:], in0=pot[dc % 2][:],
                    in1=p.psum[b][:], op=ALU.add)
                p.bank_release(b, "dv")
                nc.gpsimd.wait_ge(p.sem("dv"), p.cnt["dv"])
                p.store_slot(dc % 2, out=scr["pre_out"][dc, :, c0:c0 + TT],
                             in_=pot[dc % 2][:])
        y_end[0] = p.cnt["mm"]
    p.tile_end()


def phase_final(p, cst, io, scr):
    nc = p.nc
    pot = [p.sbl(f"fn_po{i}", [128, TT], F32) for i in range(DC)]
    otm = [p.sbl(f"fn_otm{i}", [128, D], F32) for i in range(4)]
    for j in range(NJ):
        s, jj = j // NJS, j % NJS
        t0 = jj * TT
        c0 = s * T + t0
        for dc in range(DC):
            p.inc(nc.sync.dma_start(out=pot[dc][:],
                                    in_=scr["pre_out"][dc, :, c0:c0 + TT]),
                  "la", 16)
        nc.tensor.wait_ge(p.sem("la"), p.cnt["la"])
        for a in range(4):
            for dc in range(DC):
                b = (a * DC + dc) % 4
                p.bank_acquire(b)
                mmv = p.inc(nc.tensor.transpose(
                    p.psum[b][:, 0:128], pot[dc][:, a * 128:(a + 1) * 128],
                    cst["idf"][:]), "mm", 1)
                nc.vector.wait_ge(p.sem("mm"), mmv)
                p.v("tensor_copy", out=otm[a][:, dc * 128:(dc + 1) * 128],
                    in_=p.psum[b][:, 0:128])
                p.bank_release(b, "dv")
        nc.gpsimd.wait_ge(p.sem("dv"), p.cnt["dv"])
        for a in range(4):
            p.inc(nc.gpsimd.dma_start(
                out=io["out"][s, t0 + a * 128:t0 + (a + 1) * 128, :],
                in_=otm[a][:]), "st", 16)
        p.tile_end()


def build_program(upto=None, dump=()):
    p = Prog()
    nc = p.nc
    with p.es:
        io = declare_io(nc, dump)
        scr = declare_scratch(nc, dump)
        p.psum = [p.es.enter_context(nc.psum_tensor(f"pb{i}", [128, TT], F32))
                  for i in range(8)]
        p.tmp_sq = p.sb("tmp_sq", [128, TT], F32)
        p.tmp_stats = p.sb("tmp_stats", [1, 4 * TT], F32)
        p.wstage = [p.sb(f"wstage{i}", [128, 512], F32) for i in range(2)]
        p.wcast_guard = {0: (0, 0), 1: (0, 0)}
        cst = make_consts(p, io)
        phases = ALL_PHASES[:ALL_PHASES.index(upto) + 1] if upto else ALL_PHASES
        emitters = {"p0": phase_p0,
                    "k": lambda *a: phase_proj(*a, "k"),
                    "v": lambda *a: phase_proj(*a, "v"),
                    "r": lambda *a: phase_proj(*a, "r"),
                    "cmk": lambda *a: phase_proj(*a, "cmk"),
                    "scan": phase_scan,
                    "wo": phase_wo,
                    "cmv": phase_cmv,
                    "route": phase_route,
                    "exp0": phase_exp0,
                    "qkv1": phase_qkv1,
                    "att1": phase_att1,
                    "msaqkv": phase_msaqkv,
                    "msaatt": phase_msaatt,
                    "msaproj": phase_msaproj,
                    "exp1": phase_exp1,
                    "final": phase_final}
        for ph in phases:
            if ph in emitters:
                with ExitStack() as les:
                    p.local_es = les
                    emitters[ph](p, cst, io, scr)
                    p.tile_end()
                p.local_es = None
        p.wait(nc.gpsimd, "st")
        p.wait(nc.gpsimd, "la")
        for i in range(4):
            if ("st%d" % i) in p.sems:
                p.wait(nc.gpsimd, "st%d" % i)
    return p, io, scr


def kernel(**inputs):
    p, io, scr = build_program()
    x = np.ascontiguousarray(np.asarray(inputs["L0_attn_x"], dtype=np.float32))
    base = {k: np.ascontiguousarray(np.asarray(v, dtype=np.float32))
            for k, v in inputs.items() if k != "L0_attn_x"}
    in_maps = []
    for c in range(8):
        m = dict(base)
        m["x_in"] = x[2 * c:2 * c + 2]
        in_maps.append(m)
    res = run_bass_kernel_spmd(p.nc, in_maps, core_ids=list(range(8)))
    return np.concatenate([r["out"] for r in res.results], axis=0)


# version 10
# speedup vs baseline: 1.1348x; 1.1348x over previous
import numpy as np
from contextlib import ExitStack
import concourse.bass as bass
import concourse.mybir as mybir
from concourse.bass_utils import run_bass_kernel_spmd

F32 = mybir.dt.float32
BF16 = mybir.dt.bfloat16
AF = mybir.ActivationFunctionType
ALU = mybir.AluOpType
AX = mybir.AxisListType

D = 1024
HU = 4096
T = 2048
NS = 2
TC = NS * T
DC = D // 128
HC = HU // 128
TT = 512
NJS = T // TT
NJ = TC // TT
LN_EPS = 1e-5
K_CLAMP = 60.0
K_EPS = 1e-8

ALL_PHASES = ["p0", "k", "v", "r", "scan", "wo", "cmk", "cmv", "route",
              "exp0", "qkv1", "att1", "msaqkv", "msaatt", "msaproj",
              "exp1", "final"]


class Prog:

    def __init__(self):
        self.nc = bass.Bass("TRN2", target_bir_lowering=False)
        self.es = ExitStack()
        self.sems = {}
        self.cnt = {}
        self.names = set()
        self.bank_guard = {}
        self.local_es = None

    def sem(self, name):
        if name not in self.sems:
            self.sems[name] = self.es.enter_context(self.nc.semaphore(name))
            self.cnt[name] = 0
        return self.sems[name]

    def inc(self, inst, name, k=1):
        inst.then_inc(self.sem(name), k)
        self.cnt[name] += k
        return self.cnt[name]

    def wait(self, eng, name, val=None):
        eng.wait_ge(self.sem(name), self.cnt[name] if val is None else val)

    def _chained(self, eng, semname, opname, a, kw):
        s = self.sem(semname)
        if self.cnt[semname]:
            eng.wait_ge(s, self.cnt[semname])
        inst = getattr(eng, opname)(*a, **kw)
        self.inc(inst, semname, 1)
        return inst

    def v(self, opname, *a, **kw):
        return self._chained(self.nc.vector, "dv", opname, a, kw)

    def g(self, opname, *a, **kw):
        return self._chained(self.nc.gpsimd, "po", opname, a, kw)

    def s(self, opname, *a, **kw):
        return self._chained(self.nc.scalar, "ac", opname, a, kw)

    def sb(self, name, shape, dtype=F32):
        assert name not in self.names, name
        self.names.add(name)
        return self.es.enter_context(self.nc.sbuf_tensor(name, shape, dtype))

    def sbl(self, name, shape, dtype=F32):
        assert self.local_es is not None
        self.uniq = getattr(self, "uniq", 0) + 1
        name = f"{name}_{self.uniq}"
        self.names.add(name)
        return self.local_es.enter_context(self.nc.sbuf_tensor(name, shape, dtype))

    def barrier(self):
        self.nc.all_engine_barrier()

    def tile_end(self):
        self.wait(self.nc.gpsimd, "st")
        self.wait(self.nc.sync, "la")
        for s_ in ("lb", "lb0", "lb1", "lc0", "lc1", "tp0", "tp1", "tp2", "tp3",
                   "tq0", "tq1", "tq2", "tq3"):
            if s_ in self.sems:
                self.wait(self.nc.sync, s_)
        for i in range(4):
            if ("st%d" % i) in self.sems:
                self.wait(self.nc.gpsimd, "st%d" % i)
        self.barrier()

    def bank_acquire(self, b):
        for semname, val in self.bank_guard.get(b, {}).items():
            self.nc.tensor.wait_ge(self.sem(semname), val)

    def bank_release(self, b, *semnames):
        self.bank_guard[b] = {sn: self.cnt[sn] for sn in semnames}

    def store_slot(self, i, out, in_):
        sn = "st%d" % (i % 4)
        self.inc(self.nc.gpsimd.dma_start(out=out, in_=in_), sn, 16)

    def slot_guard(self, eng, i):
        sn = "st%d" % (i % 4)
        if sn in self.sems:
            eng.wait_ge(self.sem(sn), self.cnt[sn])


def declare_io(nc, dump):
    io = {}
    io["x_in"] = nc.dram_tensor("x_in", [NS, T, D], F32, kind="ExternalInput").ap()
    wspecs = {
        "L0_attn_ln0_g": [D], "L0_attn_ln0_b": [D],
        "L0_attn_pre_g": [D], "L0_attn_pre_b": [D],
        "L0_attn_post_g": [D], "L0_attn_post_b": [D],
        "L0_attn_tm_decay": [HU], "L0_attn_tm_first": [HU],
        "L0_attn_tm_mix_k": [D], "L0_attn_tm_mix_v": [D], "L0_attn_tm_mix_r": [D],
        "L0_attn_tm_Wk": [D, HU], "L0_attn_tm_Wv": [D, HU],
        "L0_attn_tm_Wr": [D, HU], "L0_attn_tm_Wo": [HU, D],
        "L0_attn_cm_mix_k": [D], "L0_attn_cm_mix_r": [D],
        "L0_attn_cm_Wk": [D, HU], "L0_attn_cm_Wv": [HU, D], "L0_attn_cm_Wr": [D, D],
        "L0_ffn_centroids": [64, D], "L0_ffn_expert_biases": [64],
        "L0_ffn_Ws_up": [D, 512], "L0_ffn_Ws_down": [512, D],
        "L0_ffn_W_up": [64, D, 512], "L0_ffn_W_down": [64, 512, D],
        "L1_attn_Wq": [D, D], "L1_attn_bq": [D],
        "L1_attn_Wk": [D, 128], "L1_attn_bk": [128],
        "L1_attn_Wv": [D, 128], "L1_attn_bv": [128],
        "L1_attn_Wo": [D, D], "L1_attn_bo": [D],
        "L1_ffn_ln1_w": [D], "L1_ffn_ln1_b": [D],
        "L1_ffn_w_qkv": [D, 3 * D], "L1_ffn_w_proj": [D, D],
        "L1_ffn_ln2_w": [D], "L1_ffn_ln2_b": [D],
        "L1_ffn_gate_w": [D, 8], "L1_ffn_gate_b": [8],
        "L1_ffn_w1": [8, D, HU], "L1_ffn_w2": [8, HU, D],
    }
    for k, shp in wspecs.items():
        io[k] = nc.dram_tensor(k, shp, F32, kind="ExternalInput").ap()
    io["out"] = nc.dram_tensor("out", [NS, T, D], F32, kind="ExternalOutput").ap()
    return io


def declare_scratch(nc, dump):
    specs = {
        "x_fm": ([DC, 128, TC], F32),
        "xn":   ([DC, 128, TC], BF16),
        "x0":   ([DC, 128, TC], F32),
        "xn1":  ([NS, DC, 128, T + 1], BF16),
        "kk":   ([HC, 128, TC], BF16),
        "vv_":  ([HC, 128, TC], BF16),
        "sr":   ([HC, 128, TC], BF16),
        "rw":   ([HC, 128, TC], BF16),
        "x0b":  ([DC, 128, TC], F32),
        "xn2":  ([NS, DC, 128, T + 1], BF16),
        "ck":   ([HC, 128, TC], BF16),
        "at0":  ([DC, 128, TC], F32),
        "acc0": ([DC, 128, TC], F32),
        "y0":   ([DC, 128, TC], F32),
        "xn_1": ([DC, 128, TC], BF16),
        "q1":   ([DC, 128, TC], BF16),
        "at1":  ([DC, 128, TC], F32),
        "mq":   ([DC, 128, TC], BF16),
        "mk_":  ([DC, 128, TC], BF16),
        "vtm":  ([NS, T // 128, 128, D], BF16),
        "mo":   ([DC, 128, TC], BF16),
        "xf1":  ([DC, 128, TC], F32),
        "xg":   ([DC, 128, TC], BF16),
        "pre_out": ([DC, 128, TC], F32),
        "dbg_wt": ([64, TC], BF16),
        "k1":  ([128, TC], BF16),
        "v1":  ([128, TC], BF16),
        "w1h": ([8, TC], BF16),
    }
    scr = {}
    for k, (shp, dt) in specs.items():
        kind = "ExternalOutput" if k in dump else "Internal"
        scr[k] = nc.dram_tensor("s_" + k, shp, dt, kind=kind).ap()
    return scr


def make_consts(p, io):
    nc = p.nc
    c = {}
    import ml_dtypes
    ident_np = np.eye(128, dtype=np.float32)
    h_idf = nc.inline_tensor(ident_np, name="c_idf")
    h_idb = nc.inline_tensor(ident_np.astype(ml_dtypes.bfloat16), name="c_idb")
    h_ones_col = nc.inline_tensor(np.ones((128, 1), np.float32), name="c_onescol")
    h_ones_row = nc.inline_tensor(np.ones((1, 128), np.float32), name="c_onesrow")
    h_ones_row_b = nc.inline_tensor(np.ones((1, 128), ml_dtypes.bfloat16),
                                    name="c_onesrowb")
    oh64 = np.zeros((64, 64 * 128), np.float32)
    for e in range(64):
        oh64[e, e * 128:(e + 1) * 128] = 1.0
    h_oh64 = nc.inline_tensor(oh64.astype(ml_dtypes.bfloat16), name="c_oh64")
    oh8 = np.zeros((8, 8 * 128), np.float32)
    for e in range(8):
        oh8[e, e * 128:(e + 1) * 128] = 1.0
    h_oh8 = nc.inline_tensor(oh8.astype(ml_dtypes.bfloat16), name="c_oh8")
    cols = {"ln_eps": LN_EPS, "k_eps": K_EPS, "k_clamp": K_CLAMP, "zero": 0.0,
            "one": 1.0, "neg1e30": -1e30, "inv_d": 1.0 / D, "half": 0.5}
    cc_np = np.zeros((128, len(cols)), np.float32)
    c["col_idx"] = {k: i for i, k in enumerate(cols)}
    for k, v in cols.items():
        cc_np[:, c["col_idx"][k]] = v
    h_cc = nc.inline_tensor(cc_np, name="c_cols")
    band = np.full((128, 384), -1e30, np.float32)
    for i in range(128):
        band[i, i:i + 256] = 0.0
    h_band = nc.inline_tensor(band, name="c_band")

    c["idf"] = p.sb("sb_idf", [128, 128], F32)
    c["idb"] = p.sb("sb_idb", [128, 128], BF16)
    c["ones_col"] = p.sb("sb_onescol", [128, 1], F32)
    c["ones_row"] = p.sb("sb_onesrow", [1, 128], F32)
    c["ones_row_b"] = p.sb("sb_onesrowb", [1, 128], BF16)
    c["cols"] = p.sb("sb_cols", [128, len(cols)], F32)
    c["zcol_b"] = p.sb("sb_zcolb", [128, 4], BF16)
    c["h_oh64"] = h_oh64
    c["h_oh8"] = h_oh8
    c["h_band"] = h_band

    p.g("memset", c["zcol_b"][:], 0.0)
    for dst, src in [(c["idf"], h_idf.ap()), (c["idb"], h_idb.ap()),
                     (c["ones_col"], h_ones_col.ap()),
                     (c["ones_row"], h_ones_row.ap()),
                     (c["ones_row_b"], h_ones_row_b.ap()),
                     (c["cols"], h_cc.ap())]:
        p.inc(nc.sync.dma_start(out=dst[:], in_=src[:]), "la", 16)

    vec_specs = [
        ("ln0_g", "L0_attn_ln0_g", DC), ("ln0_b", "L0_attn_ln0_b", DC),
        ("pre_g", "L0_attn_pre_g", DC), ("pre_b", "L0_attn_pre_b", DC),
        ("post_g", "L0_attn_post_g", DC), ("post_b", "L0_attn_post_b", DC),
        ("mix_k", "L0_attn_tm_mix_k", DC), ("mix_v", "L0_attn_tm_mix_v", DC),
        ("mix_r", "L0_attn_tm_mix_r", DC),
        ("cmix_k", "L0_attn_cm_mix_k", DC), ("cmix_r", "L0_attn_cm_mix_r", DC),
        ("tm_decay", "L0_attn_tm_decay", HC), ("tm_first", "L0_attn_tm_first", HC),
        ("eb", "L0_ffn_expert_biases", "64"),
        ("bq", "L1_attn_bq", DC), ("bk", "L1_attn_bk", 1), ("bv", "L1_attn_bv", 1),
        ("bo", "L1_attn_bo", DC),
        ("ln1_w", "L1_ffn_ln1_w", DC), ("ln1_b", "L1_ffn_ln1_b", DC),
        ("ln2_w", "L1_ffn_ln2_w", DC), ("ln2_b", "L1_ffn_ln2_b", DC),
        ("gate_b", "L1_ffn_gate_b", "8"),
    ]
    stage = p.sb("sb_vecstage", [HC, 128], F32)
    pstat = p.psum[0]
    for name, key, nch in vec_specs:
        if nch in ("64", "8"):
            rows, colw = 1, int(nch)
            src = io[key].rearrange("(o a) -> o a", o=1)
        elif nch == 1:
            rows, colw = 1, 128
            src = io[key].rearrange("(o a) -> o a", o=1)
        else:
            rows, colw = nch, 128
            src = io[key].rearrange("(a b) -> a b", b=128)
        dst = p.sb("sb_v_" + name, [colw, rows], F32)
        c[name] = dst
        p.wait(nc.sync, "mm")
        ldv = p.inc(nc.sync.dma_start(out=stage[0:rows, 0:colw], in_=src), "la", 16)
        nc.tensor.wait_ge(p.sem("la"), ldv)
        p.wait(nc.tensor, "dv")
        p.inc(nc.tensor.transpose(pstat[0:colw, 0:rows], stage[0:rows, 0:colw],
                                  c["idf"][0:rows, 0:rows]), "mm", 1)
        p.wait(nc.vector, "mm")
        p.v("tensor_copy", out=dst[:], in_=pstat[0:colw, 0:rows])
    c["decay"] = p.sb("sb_v_decay", [128, HC], F32)
    c["first"] = p.sb("sb_v_first", [128, HC], F32)
    p.wait(nc.scalar, "dv")
    p.s("activation", out=c["decay"][:], in_=c["tm_decay"][:], func=AF.Exp)
    p.s("activation", out=c["decay"][:], in_=c["decay"][:], func=AF.Exp,
        scale=-1.0)
    p.s("activation", out=c["first"][:], in_=c["tm_first"][:], func=AF.Exp)
    p.barrier()
    return c


def col(p, c, name):
    i = c["col_idx"][name]
    return c["cols"][:, i:i + 1]


def load_weight_bf16(p, src_ap_fn, dst_tiles, n_tiles, ncols):
    nc = p.nc
    hw = 512
    nh = (ncols + hw - 1) // hw
    p.wait(nc.sync, "mm")
    for i in range(n_tiles):
        for h in range(nh):
            cw = min(hw, ncols - h * hw)
            k = (i * nh + h) % 2
            sn = "ws%d" % k
            if p.cnt.get(sn, 0):
                nc.sync.wait_ge(p.sem("dv"), p.wcast_guard[k][0])
                nc.sync.wait_ge(p.sem("po"), p.wcast_guard[k][1])
            ldv = p.inc(nc.sync.dma_start(out=p.wstage[k][:, 0:cw],
                                          in_=src_ap_fn(i)[:, h * hw:h * hw + cw]),
                        sn, 16)
            if (i * nh + h) % 2 == 0:
                nc.vector.wait_ge(p.sem(sn), ldv)
                p.v("tensor_copy", out=dst_tiles[i][:, h * hw:h * hw + cw],
                    in_=p.wstage[k][:, 0:cw])
            else:
                nc.gpsimd.wait_ge(p.sem(sn), ldv)
                p.g("tensor_copy", out=dst_tiles[i][:, h * hw:h * hw + cw],
                    in_=p.wstage[k][:, 0:cw])
            p.wcast_guard[k] = (p.cnt["dv"], p.cnt["po"])
    p.barrier()


def ln_fm(p, cst, xch, out, W, g=None, b=None):
    nc = p.nc
    ps_sx, ps_sq, ps_mb, ps_rb = p.psum[4], p.psum[5], p.psum[6], p.psum[7]
    sq = p.tmp_sq
    st_ = p.tmp_stats
    p.wait(nc.gpsimd, "mm")
    p.wait(nc.tensor, "dv")
    p.wait(nc.gpsimd, "dv")
    for dc in range(DC):
        p.g("tensor_tensor", out=sq[:, 0:W], in0=xch[dc], in1=xch[dc],
            op=ALU.mult)
        nc.tensor.wait_ge(p.sem("po"), p.cnt["po"])
        nc.tensor.matmul(ps_sq[0:1, 0:W], cst["ones_col"][:], sq[:, 0:W],
                         start=(dc == 0), stop=(dc == DC - 1))
        p.inc(nc.tensor.matmul(ps_sx[0:1, 0:W], cst["ones_col"][:], xch[dc],
                               start=(dc == 0), stop=(dc == DC - 1)), "mm", 1)
        if dc < DC - 1:
            nc.gpsimd.wait_ge(p.sem("mm"), p.cnt["mm"])
    p.wait(nc.vector, "mm")
    m = st_[0:1, 0:W]
    msq = st_[0:1, TT:TT + W]
    var = st_[0:1, 2 * TT:2 * TT + W]
    rstd = st_[0:1, 3 * TT:3 * TT + W]
    p.v("tensor_scalar", out=m, in0=ps_sx[0:1, 0:W], scalar1=1.0 / D,
        scalar2=None, op0=ALU.mult)
    p.v("tensor_scalar", out=msq, in0=ps_sq[0:1, 0:W], scalar1=1.0 / D,
        scalar2=None, op0=ALU.mult)
    p.v("tensor_tensor", out=var, in0=m, in1=m, op=ALU.mult)
    p.v("tensor_tensor", out=var, in0=msq, in1=var, op=ALU.subtract)
    p.wait(nc.scalar, "dv")
    p.s("activation", out=rstd, in_=var, func=AF.Ln,
        bias=col(p, cst, "ln_eps")[0:1, :])
    p.s("activation", out=rstd, in_=rstd, func=AF.Exp, scale=-0.5)
    p.wait(nc.tensor, "ac")
    p.wait(nc.tensor, "dv")
    nc.tensor.matmul(ps_mb[:, 0:W], cst["ones_row"][:], m, start=True, stop=True)
    p.inc(nc.tensor.matmul(ps_rb[:, 0:W], cst["ones_row"][:], rstd,
                           start=True, stop=True), "mm", 1)
    p.wait(nc.vector, "mm")
    for dc in range(DC):
        p.v("tensor_tensor", out=out[dc], in0=xch[dc], in1=ps_mb[:, 0:W],
            op=ALU.subtract)
        p.v("tensor_tensor", out=out[dc], in0=out[dc], in1=ps_rb[:, 0:W],
            op=ALU.mult)
        if g is not None:
            p.v("tensor_scalar", out=out[dc], in0=out[dc],
                scalar1=g[:, dc:dc + 1], scalar2=b[:, dc:dc + 1],
                op0=ALU.mult, op1=ALU.add)


def phase_p0(p, cst, io, scr):
    nc = p.nc
    xtm = p.sbl("p0_xtm", [128, 4 * D], F32)
    xfm = [p.sbl(f"p0_xfm{i}", [128, TT], F32) for i in range(DC)]
    xnf = [p.sbl(f"p0_xnf{i}", [128, TT], F32) for i in range(DC)]
    x0f = [p.sbl(f"p0_x0f{i}", [128, TT], F32) for i in range(DC)]
    xn1f = [p.sbl(f"p0_xn1f{i}", [128, TT], F32) for i in range(DC)]
    obf = p.sbl("p0_obf", [128, TT], BF16)

    with nc.allow_non_contiguous_dma(reason="one-column zero init, 32 tiny DMAs"):
        for s in range(NS):
            for dc in range(DC):
                p.inc(nc.gpsimd.dma_start(out=scr["xn1"][s, dc, :, 0:1],
                                          in_=cst["zcol_b"][:, 0:1]), "st", 16)
                p.inc(nc.gpsimd.dma_start(out=scr["xn2"][s, dc, :, 0:1],
                                          in_=cst["zcol_b"][:, 0:1]), "st", 16)

    for j in range(NJ):
        s, jj = j // NJS, j % NJS
        t0 = jj * TT
        c0 = s * T + t0
        ld = p.inc(nc.sync.dma_start(
            out=xtm[:].rearrange("p (a d) -> p a d", a=4),
            in_=io["x_in"][s, t0:t0 + TT, :].rearrange("(a p) d -> p a d", p=128)),
            "la", 16)
        nc.tensor.wait_ge(p.sem("la"), ld)
        for a in range(4):
            for dc in range(DC):
                bank = p.psum[(a * DC + dc) % 4]
                mmv = p.inc(nc.tensor.transpose(
                    bank[:, 0:128], xtm[:, a * D + dc * 128: a * D + (dc + 1) * 128],
                    cst["idf"][:]), "mm", 1)
                nc.vector.wait_ge(p.sem("mm"), mmv)
                p.v("tensor_copy", out=xfm[dc][:, a * 128:(a + 1) * 128],
                    in_=bank[:, 0:128])
                nc.tensor.wait_ge(p.sem("dv"), p.cnt["dv"])
        p.wait(nc.gpsimd, "dv")
        for dc in range(DC):
            p.inc(nc.gpsimd.dma_start(out=scr["x_fm"][dc, :, c0:c0 + TT],
                                      in_=xfm[dc][:]), "st", 16)
        ln_fm(p, cst, [x[:] for x in xfm], [x[:] for x in xnf], TT)
        p.wait(nc.gpsimd, "dv")
        for dc in range(DC):
            p.g("tensor_copy", out=obf[:], in_=xnf[dc][:])
            nc.sync.wait_ge(p.sem("po"), p.cnt["po"])
            p.inc(nc.sync.dma_start(out=scr["xn"][dc, :, c0:c0 + TT],
                                    in_=obf[:]), "la", 16)
            nc.gpsimd.wait_ge(p.sem("la"), p.cnt["la"])
        ln_fm(p, cst, [x[:] for x in xnf], [x[:] for x in x0f], TT,
              g=cst["ln0_g"], b=cst["ln0_b"])
        p.wait(nc.gpsimd, "dv")
        for dc in range(DC):
            p.inc(nc.gpsimd.dma_start(out=scr["x0"][dc, :, c0:c0 + TT],
                                      in_=x0f[dc][:]), "st", 16)
        ln_fm(p, cst, [x[:] for x in x0f], [x[:] for x in xn1f], TT,
              g=cst["pre_g"], b=cst["pre_b"])
        p.wait(nc.gpsimd, "dv")
        for dc in range(DC):
            p.g("tensor_copy", out=obf[:], in_=xn1f[dc][:])
            nc.sync.wait_ge(p.sem("po"), p.cnt["po"])
            p.inc(nc.sync.dma_start(out=scr["xn1"][s, dc, :, t0 + 1:t0 + 1 + TT],
                                    in_=obf[:]), "la", 16)
            nc.gpsimd.wait_ge(p.sem("la"), p.cnt["la"])
        p.tile_end()


def phase_proj(p, cst, io, scr, kind):
    nc = p.nc
    wkey, mixname, src_name, dst_name = {
        "k": ("L0_attn_tm_Wk", "mix_k", "xn1", "kk"),
        "v": ("L0_attn_tm_Wv", "mix_v", "xn1", "vv_"),
        "r": ("L0_attn_tm_Wr", "mix_r", "xn1", "sr"),
        "cmk": ("L0_attn_cm_Wk", "cmix_k", "xn2", "ck"),
    }[kind]
    mix = cst[mixname]
    wbf = [p.sbl(f"pj_w{i}", [128, HU], BF16) for i in range(DC)]
    win = [[p.sbl(f"pj_win{b}_{i}", [128, TT + 1], BF16) for i in range(DC)]
           for b in range(2)]
    xk = [[p.sbl(f"pj_xk{b}_{i}", [128, TT], BF16) for i in range(DC)]
          for b in range(2)]
    tmpd = [p.sbl(f"pj_tmp{i}", [128, TT], F32) for i in range(2)]
    etmp = [p.sbl(f"pj_etmp{i}", [128, TT], F32) for i in range(2)]
    p.stage_bf = [p.sbl(f"pj_stg{i}", [128, TT], BF16) for i in range(4)]

    load_weight_bf16(p, lambda i: io[wkey][i * 128:(i + 1) * 128, :], wbf, DC, HU)

    mix_done = [0, 0]
    mix_done_po = [0, 0]
    mm_done = [0, 0]
    for j in range(NJ):
        s, jj = j // NJS, j % NJS
        t0 = jj * TT
        c0 = s * T + t0
        par = j % 2
        lsn = "lc%d" % par
        nc.sync.wait_ge(p.sem("dv"), mix_done[par])
        nc.sync.wait_ge(p.sem("po"), mix_done_po[par])
        for dc in range(DC):
            p.inc(nc.sync.dma_start(out=win[par][dc][:],
                                    in_=scr[src_name][s, dc, :, t0:t0 + TT + 1]),
                  lsn, 16)
        lav = p.cnt[lsn]
        nc.vector.wait_ge(p.sem(lsn), lav)
        nc.gpsimd.wait_ge(p.sem(lsn), lav)
        nc.vector.wait_ge(p.sem("mm"), mm_done[par])
        nc.gpsimd.wait_ge(p.sem("mm"), mm_done[par])
        for dc in range(DC):
            xx = win[par][dc][:, 0:TT]
            xn_ = win[par][dc][:, 1:TT + 1]
            if dc % 2 == 0:
                p.v("tensor_tensor", out=tmpd[0][:], in0=xn_, in1=xx,
                    op=ALU.subtract)
                p.v("scalar_tensor_tensor", out=xk[par][dc][:], in0=tmpd[0][:],
                    scalar=mix[:, dc:dc + 1], in1=xx, op0=ALU.mult, op1=ALU.add)
            else:
                p.g("tensor_tensor", out=tmpd[1][:], in0=xn_, in1=xx,
                    op=ALU.subtract)
                p.g("tensor_scalar", out=tmpd[1][:], in0=tmpd[1][:],
                    scalar1=mix[:, dc:dc + 1], scalar2=None, op0=ALU.mult)
                p.g("tensor_tensor", out=xk[par][dc][:], in0=tmpd[1][:], in1=xx,
                    op=ALU.add)
        mix_done[par] = p.cnt["dv"]
        mix_done_po[par] = p.cnt["po"]
        nc.tensor.wait_ge(p.sem("dv"), mix_done[par])
        nc.tensor.wait_ge(p.sem("po"), mix_done_po[par])
        for cc in range(HC):
            b = cc % 4
            sl = cc % 4
            p.bank_acquire(b)
            for dc in range(DC):
                inst = nc.tensor.matmul(p.psum[b][:],
                                        wbf[dc][:, cc * 128:(cc + 1) * 128],
                                        xk[par][dc][:], start=(dc == 0),
                                        stop=(dc == DC - 1))
            mmv = p.inc(inst, "mm", 1)
            if kind == "k":
                nc.vector.wait_ge(p.sem("mm"), mmv)
                nc.vector.wait_ge(p.sem("ac"), p.cnt["ac"])
                p.v("tensor_scalar_min", etmp[par][:], p.psum[b][:], K_CLAMP)
                p.slot_guard(nc.scalar, sl)
                p.wait(nc.scalar, "dv")
                p.s("activation", out=p.stage_bf[sl][:], in_=etmp[par][:],
                    func=AF.Exp)
                p.bank_release(b, "dv")
            else:
                nc.scalar.wait_ge(p.sem("mm"), mmv)
                p.slot_guard(nc.scalar, sl)
                fn = {"v": AF.Copy, "r": AF.Sigmoid, "cmk": AF.Silu}[kind]
                p.s("activation", out=p.stage_bf[sl][:], in_=p.psum[b][:], func=fn)
                p.bank_release(b, "ac")
            nc.gpsimd.wait_ge(p.sem("ac"), p.cnt["ac"])
            p.store_slot(sl, out=scr[dst_name][cc, :, c0:c0 + TT],
                         in_=p.stage_bf[sl][:])
        mm_done[par] = p.cnt["mm"]
    p.tile_end()


def phase_scan(p, cst, io, scr):
    nc = p.nc
    kb = [p.sbl(f"sc_k{i}", [128, T], BF16) for i in range(2)]
    vb = [p.sbl(f"sc_v{i}", [128, T], BF16) for i in range(2)]
    sb_ = [p.sbl(f"sc_s{i}", [128, T], BF16) for i in range(2)]
    kv = [p.sbl(f"sc_kv{i}", [128, T], F32) for i in range(2)]
    skb = [p.sbl(f"sc_skb{i}", [128, T + 1], F32) for i in range(2)]
    skvb = [p.sbl(f"sc_skvb{i}", [128, T + 1], F32) for i in range(2)]
    wk = [p.sbl(f"sc_wk{i}", [128, T], F32) for i in range(2)]
    wkv = [p.sbl(f"sc_wkv{i}", [128, T], F32) for i in range(2)]
    ob = [p.sbl(f"sc_ob{i}", [128, T], BF16) for i in range(2)]
    for i in range(2):
        p.v("memset", skb[i][:, 0:1], 0.0)
        p.v("memset", skvb[i][:, 0:1], 0.0)
    dv_guard = [0, 0]
    ac_guard = [0, 0]
    po_guard = [0, 0]
    kv_guard = [0, 0]
    for u in range(NS * HC):
        s, cc = u // HC, u % HC
        par = u % 2
        c0 = s * T
        nc.sync.wait_ge(p.sem("dv"), dv_guard[par])
        nc.sync.wait_ge(p.sem("po"), po_guard[par])
        nc.sync.wait_ge(p.sem("ac"), ac_guard[par])
        lsn = "lc%d" % par
        for dst, srcn in [(kb[par], "kk"), (vb[par], "vv_"), (sb_[par], "sr")]:
            p.inc(nc.sync.dma_start(out=dst[:], in_=scr[srcn][cc, :, c0:c0 + T]),
                  lsn, 16)
        lav = p.cnt[lsn]
        dcol = cst["decay"][:, cc:cc + 1]
        fcol = cst["first"][:, cc:cc + 1]
        nc.vector.wait_ge(p.sem(lsn), lav)
        p.v("tensor_tensor_scan", out=skb[par][:, 1:T + 1],
            data0=dcol.broadcast_to((128, T)), data1=kb[par][:], initial=0.0,
            op0=ALU.mult, op1=ALU.add)
        nc.gpsimd.wait_ge(p.sem(lsn), lav)
        nc.gpsimd.wait_ge(p.sem("dv"), kv_guard[par])
        p.g("tensor_tensor", out=kv[par][:], in0=kb[par][:], in1=vb[par][:],
            op=ALU.mult)
        pkv = p.cnt["po"]
        nc.vector.wait_ge(p.sem("po"), pkv)
        p.v("tensor_tensor_scan", out=skvb[par][:, 1:T + 1],
            data0=dcol.broadcast_to((128, T)), data1=kv[par][:], initial=0.0,
            op0=ALU.mult, op1=ALU.add)
        kv_guard[par] = p.cnt["dv"]
        nc.vector.wait_ge(p.sem("ac"), ac_guard[par])
        p.v("scalar_tensor_tensor", out=wk[par][:], in0=kb[par][:], scalar=fcol,
            in1=skb[par][:, 0:T], op0=ALU.mult, op1=ALU.add)
        dv_guard[par] = p.cnt["dv"]
        nc.scalar.wait_ge(p.sem("dv"), p.cnt["dv"])
        p.s("activation", out=wk[par][:], in_=wk[par][:], func=AF.Ln,
            bias=col(p, cst, "k_eps"))
        p.s("activation", out=wk[par][:], in_=wk[par][:], func=AF.Exp,
            scale=-1.0)
        dvr = p.cnt["ac"]
        ac_guard[par] = dvr
        nc.gpsimd.wait_ge(p.sem("dv"), kv_guard[par])
        p.g("tensor_scalar", out=wkv[par][:], in0=kv[par][:], scalar1=fcol,
            scalar2=None, op0=ALU.mult)
        p.g("tensor_tensor", out=wkv[par][:], in0=wkv[par][:],
            in1=skvb[par][:, 0:T], op=ALU.add)
        nc.gpsimd.wait_ge(p.sem("ac"), dvr)
        p.g("tensor_tensor", out=wkv[par][:], in0=wkv[par][:], in1=wk[par][:],
            op=ALU.mult)
        p.slot_guard(nc.gpsimd, par)
        p.g("tensor_tensor", out=ob[par][:], in0=wkv[par][:], in1=sb_[par][:],
            op=ALU.mult)
        po_guard[par] = p.cnt["po"]
        nc.gpsimd.wait_ge(p.sem("po"), p.cnt["po"])
        p.store_slot(par, out=scr["rw"][cc, :, c0:c0 + T], in_=ob[par][:])
    p.tile_end()


def phase_wo(p, cst, io, scr):
    nc = p.nc
    wo = [p.sbl(f"wo_w{i}", [128, D], BF16) for i in range(HC)]
    rwt = [p.sbl(f"wo_rw{i}", [128, TT], BF16) for i in range(HC)]
    x0t = [p.sbl(f"wo_x0t{i}", [128, TT], F32) for i in range(DC)]
    x0n = [p.sbl(f"wo_x0n{i}", [128, TT], F32) for i in range(DC)]
    xn2f = [p.sbl(f"wo_xn2f{i}", [128, TT], F32) for i in range(DC)]
    obf = p.sbl("wo_obf", [128, TT], BF16)

    load_weight_bf16(p, lambda i: io["L0_attn_tm_Wo"][i * 128:(i + 1) * 128, :],
                     wo, HC, D)

    for j in range(NJ):
        s, jj = j // NJS, j % NJS
        t0 = jj * TT
        c0 = s * T + t0
        for cc in range(HC):
            p.inc(nc.sync.dma_start(out=rwt[cc][:],
                                    in_=scr["rw"][cc, :, c0:c0 + TT]), "la", 16)
        for dc in range(DC):
            p.inc(nc.sync.dma_start(out=x0t[dc][:],
                                    in_=scr["x0"][dc, :, c0:c0 + TT]), "la", 16)
        nc.tensor.wait_ge(p.sem("la"), p.cnt["la"])
        p.wait(nc.vector, "la")
        for dc in range(DC):
            b = dc % 4
            p.bank_acquire(b)
            for cc in range(HC):
                inst = nc.tensor.matmul(p.psum[b][:],
                                        wo[cc][:, dc * 128:(dc + 1) * 128],
                                        rwt[cc][:], start=(cc == 0),
                                        stop=(cc == HC - 1))
            mmv = p.inc(inst, "mm", 1)
            nc.vector.wait_ge(p.sem("mm"), mmv)
            p.v("tensor_tensor", out=x0n[dc][:], in0=x0t[dc][:], in1=p.psum[b][:],
                op=ALU.add)
            p.bank_release(b, "dv")
        p.wait(nc.gpsimd, "dv")
        for dc in range(DC):
            p.inc(nc.gpsimd.dma_start(out=scr["x0b"][dc, :, c0:c0 + TT],
                                      in_=x0n[dc][:]), "st", 16)
        ln_fm(p, cst, [x[:] for x in x0n], [x[:] for x in xn2f], TT,
              g=cst["post_g"], b=cst["post_b"])
        p.wait(nc.gpsimd, "dv")
        for dc in range(DC):
            p.g("tensor_copy", out=obf[:], in_=xn2f[dc][:])
            nc.sync.wait_ge(p.sem("po"), p.cnt["po"])
            p.inc(nc.sync.dma_start(out=scr["xn2"][s, dc, :, t0 + 1:t0 + 1 + TT],
                                    in_=obf[:]), "la", 16)
            nc.gpsimd.wait_ge(p.sem("la"), p.cnt["la"])
        p.tile_end()


def phase_cmv(p, cst, io, scr):
    nc = p.nc
    wv = [p.sbl(f"cv_wv{i}", [128, D], BF16) for i in range(HC)]
    wr = [p.sbl(f"cv_wr{i}", [128, D], BF16) for i in range(DC)]
    ckt = [p.sbl(f"cv_ck{i}", [128, TT], BF16) for i in range(HC)]
    win = [p.sbl(f"cv_win{i}", [128, TT + 1], BF16) for i in range(DC)]
    xr = [p.sbl(f"cv_xr{i}", [128, TT], BF16) for i in range(DC)]
    tmpd = [p.sbl(f"cv_tmp{i}", [128, TT], F32) for i in range(2)]
    x0t = [p.sbl(f"cv_x0t{i}", [128, TT], F32) for i in range(DC)]
    sg = p.sbl("cv_sg", [128, TT], F32)
    att = [p.sbl(f"cv_att{i}", [128, TT], F32) for i in range(2)]

    load_weight_bf16(p, lambda i: io["L0_attn_cm_Wv"][i * 128:(i + 1) * 128, :],
                     wv, HC, D)
    load_weight_bf16(p, lambda i: io["L0_attn_cm_Wr"][i * 128:(i + 1) * 128, :],
                     wr, DC, D)

    for j in range(NJ):
        s, jj = j // NJS, j % NJS
        t0 = jj * TT
        c0 = s * T + t0
        for cc in range(HC):
            p.inc(nc.sync.dma_start(out=ckt[cc][:],
                                    in_=scr["ck"][cc, :, c0:c0 + TT]), "la", 16)
        for dc in range(DC):
            p.inc(nc.sync.dma_start(out=win[dc][:],
                                    in_=scr["xn2"][s, dc, :, t0:t0 + TT + 1]),
                  "la", 16)
            p.inc(nc.sync.dma_start(out=x0t[dc][:],
                                    in_=scr["x0b"][dc, :, c0:c0 + TT]), "la", 16)
        p.wait(nc.vector, "la")
        p.wait(nc.gpsimd, "la")
        for dc in range(DC):
            xx = win[dc][:, 0:TT]
            xn_ = win[dc][:, 1:TT + 1]
            if dc % 2 == 0:
                p.v("tensor_tensor", out=tmpd[0][:], in0=xn_, in1=xx,
                    op=ALU.subtract)
                p.v("scalar_tensor_tensor", out=xr[dc][:], in0=tmpd[0][:],
                    scalar=cst["cmix_r"][:, dc:dc + 1], in1=xx,
                    op0=ALU.mult, op1=ALU.add)
            else:
                p.g("tensor_tensor", out=tmpd[1][:], in0=xn_, in1=xx,
                    op=ALU.subtract)
                p.g("tensor_scalar", out=tmpd[1][:], in0=tmpd[1][:],
                    scalar1=cst["cmix_r"][:, dc:dc + 1], scalar2=None,
                    op0=ALU.mult)
                p.g("tensor_tensor", out=xr[dc][:], in0=tmpd[1][:], in1=xx,
                    op=ALU.add)
        p.wait(nc.tensor, "dv")
        p.wait(nc.tensor, "po")
        for dc in range(DC):
            b = dc % 2
            br = 2 + dc % 2
            p.bank_acquire(b)
            for cc in range(HC):
                inst = nc.tensor.matmul(p.psum[b][:],
                                        wv[cc][:, dc * 128:(dc + 1) * 128],
                                        ckt[cc][:], start=(cc == 0),
                                        stop=(cc == HC - 1))
            p.inc(inst, "mm", 1)
            p.bank_acquire(br)
            for kc in range(DC):
                inst = nc.tensor.matmul(p.psum[br][:],
                                        wr[kc][:, dc * 128:(dc + 1) * 128],
                                        xr[kc][:], start=(kc == 0),
                                        stop=(kc == DC - 1))
            mmv = p.inc(inst, "mm", 1)
            nc.scalar.wait_ge(p.sem("mm"), mmv)
            p.wait(nc.scalar, "dv")
            p.s("activation", out=sg[:], in_=p.psum[br][:], func=AF.Sigmoid)
            p.bank_release(br, "ac")
            nc.vector.wait_ge(p.sem("ac"), p.cnt["ac"])
            a = att[dc % 2]
            p.slot_guard(nc.vector, dc % 2)
            p.v("tensor_tensor", out=a[:], in0=x0t[dc][:], in1=p.psum[b][:],
                op=ALU.add)
            p.bank_release(b, "dv")
            p.v("tensor_tensor", out=a[:], in0=a[:], in1=sg[:], op=ALU.add)
            nc.gpsimd.wait_ge(p.sem("dv"), p.cnt["dv"])
            p.store_slot(dc % 2, out=scr["at0"][dc, :, c0:c0 + TT], in_=a[:])
        p.tile_end()



def phase_route(p, cst, io, scr):
    nc = p.nc
    centT = [p.sbl(f"rt_ct{i}", [128, 64], F32) for i in range(DC)]
    xnf = [p.sbl(f"rt_xnf{i}", [128, TT], F32) for i in range(DC)]
    wsu = [p.sbl(f"rt_wsu{i}", [128, 512], BF16) for i in range(DC)]
    wsd = [p.sbl(f"rt_wsd{i}", [128, D], BF16) for i in range(4)]
    xnt = [p.sbl(f"rt_xn{i}", [128, TT], BF16) for i in range(DC)]
    xft = [p.sbl(f"rt_xf{i}", [128, TT], F32) for i in range(DC)]
    a0t = [p.sbl(f"rt_a0{i}", [128, TT], F32) for i in range(DC)]
    aff = p.sbl("rt_aff", [64, TT], F32)
    biased = p.sbl("rt_biased", [64, TT], F32)
    btm = p.sbl("rt_btm", [128, 4 * 64], F32)
    maxs = p.sbl("rt_maxs", [128, 8 * 4], F32)
    throw = p.sbl("rt_throw", [1, TT], F32)
    mask = p.sbl("rt_mask", [64, TT], F32)
    sgm = p.sbl("rt_sgm", [64, TT], F32)
    srow = p.sbl("rt_srow", [1, TT], F32)
    hs = [p.sbl(f"rt_hs{i}", [128, TT], BF16) for i in range(4)]
    accb = [p.sbl(f"rt_acc{i}", [128, TT], F32) for i in range(2)]
    wtt = p.sbl("rt_wtt", [64, TT], BF16)

    cstage = p.sbl("rt_cstage", [64, D], F32)
    ldv = p.inc(nc.sync.dma_start(out=cstage[:],
                                  in_=io["L0_ffn_centroids"][:]), "ws0", 16)
    nc.tensor.wait_ge(p.sem("ws0"), ldv)
    for dc in range(DC):
        b = dc % 2
        p.bank_acquire(b)
        mmv = p.inc(nc.tensor.transpose(
            p.psum[b][:, 0:64], cstage[0:64, dc * 128:(dc + 1) * 128],
            cst["idf"][0:64, 0:64]), "mm", 1)
        nc.vector.wait_ge(p.sem("mm"), mmv)
        p.v("tensor_copy", out=centT[dc][:], in_=p.psum[b][:, 0:64])
        p.bank_release(b, "dv")
    load_weight_bf16(p, lambda i: io["L0_ffn_Ws_up"][i * 128:(i + 1) * 128, :],
                     wsu, DC, 512)
    load_weight_bf16(p, lambda i: io["L0_ffn_Ws_down"][i * 128:(i + 1) * 128, :],
                     wsd, 4, D)

    for j in range(NJ):
        s, jj = j // NJS, j % NJS
        t0 = jj * TT
        c0 = s * T + t0
        for dc in range(DC):
            p.inc(nc.sync.dma_start(out=xnt[dc][:],
                                    in_=scr["xn"][dc, :, c0:c0 + TT]), "la", 16)
            p.inc(nc.sync.dma_start(out=xft[dc][:],
                                    in_=scr["x_fm"][dc, :, c0:c0 + TT]), "la", 16)
            p.inc(nc.sync.dma_start(out=a0t[dc][:],
                                    in_=scr["at0"][dc, :, c0:c0 + TT]), "la", 16)
        nc.tensor.wait_ge(p.sem("la"), p.cnt["la"])
        p.wait(nc.vector, "la")
        p.wait(nc.gpsimd, "la")
        ln_fm(p, cst, [x[:] for x in xft], [x[:] for x in xnf], TT)
        p.wait(nc.tensor, "dv")
        p.bank_acquire(0)
        for dc in range(DC):
            inst = nc.tensor.matmul(p.psum[0][0:64, :], centT[dc][:], xnf[dc][:],
                                    start=(dc == 0), stop=(dc == DC - 1))
        mmv = p.inc(inst, "mm", 1)
        nc.vector.wait_ge(p.sem("mm"), mmv)
        p.v("tensor_copy", out=aff[:], in_=p.psum[0][0:64, :])
        p.v("tensor_scalar", out=biased[:], in0=aff[:], scalar1=cst["eb"][:],
            scalar2=None, op0=ALU.add)
        p.bank_release(0, "dv")
        p.wait(nc.tensor, "dv")
        for a in range(4):
            b = 1 + a % 2
            p.bank_acquire(b)
            mmv = p.inc(nc.tensor.transpose(
                p.psum[b][:, 0:64], biased[:, a * 128:(a + 1) * 128],
                cst["idf"][0:64, 0:64]), "mm", 1)
            nc.vector.wait_ge(p.sem("mm"), mmv)
            p.v("tensor_copy", out=btm[:, a * 64:(a + 1) * 64],
                in_=p.psum[b][:, 0:64])
            p.bank_release(b, "dv")
        for a in range(4):
            p.v("max", out=maxs[:, a * 8:(a + 1) * 8],
                in_=btm[:, a * 64:(a + 1) * 64])
        p.wait(nc.tensor, "dv")
        for a in range(4):
            b = 1 + a % 2
            p.bank_acquire(b)
            mmv = p.inc(nc.tensor.transpose(
                p.psum[b][0:1, 0:128], maxs[:, a * 8 + 7:a * 8 + 8],
                cst["idf"][:]), "mm", 1)
            nc.vector.wait_ge(p.sem("mm"), mmv)
            p.v("tensor_copy", out=throw[0:1, a * 128:(a + 1) * 128],
                in_=p.psum[b][0:1, 0:128])
            p.bank_release(b, "dv")
        p.wait(nc.tensor, "dv")
        p.bank_acquire(2)
        mmv = p.inc(nc.tensor.matmul(p.psum[2][0:64, :], cst["ones_row"][:, 0:64],
                                     throw[:], start=True, stop=True), "mm", 1)
        nc.vector.wait_ge(p.sem("mm"), mmv)
        p.v("tensor_tensor", out=mask[:], in0=biased[:], in1=p.psum[2][0:64, :],
            op=ALU.is_ge)
        p.bank_release(2, "dv")
        p.wait(nc.scalar, "dv")
        p.s("activation", out=sgm[:], in_=aff[:], func=AF.Sigmoid)
        nc.vector.wait_ge(p.sem("ac"), p.cnt["ac"])
        p.v("tensor_tensor", out=mask[:], in0=mask[:], in1=sgm[:], op=ALU.mult)
        p.wait(nc.tensor, "dv")
        p.bank_acquire(3)
        mmv = p.inc(nc.tensor.matmul(p.psum[3][0:1, :], cst["ones_col"][0:64, :],
                                     mask[:], start=True, stop=True), "mm", 1)
        nc.vector.wait_ge(p.sem("mm"), mmv)
        p.v("tensor_scalar", out=srow[:], in0=p.psum[3][0:1, :], scalar1=K_EPS,
            scalar2=None, op0=ALU.add)
        p.v("reciprocal", out=srow[:], in_=srow[:])
        p.bank_release(3, "dv")
        p.wait(nc.tensor, "dv")
        p.bank_acquire(3)
        mmv = p.inc(nc.tensor.matmul(p.psum[3][0:64, :], cst["ones_row"][:, 0:64],
                                     srow[:], start=True, stop=True), "mm", 1)
        nc.vector.wait_ge(p.sem("mm"), mmv)
        p.v("tensor_tensor", out=wtt[:], in0=mask[:],
            in1=p.psum[3][0:64, :], op=ALU.mult)
        p.bank_release(3, "dv")
        nc.gpsimd.wait_ge(p.sem("dv"), p.cnt["dv"])
        p.inc(nc.gpsimd.dma_start(out=scr["dbg_wt"][0:64, c0:c0 + TT],
                                  in_=wtt[:]), "st", 16)
        for fc in range(4):
            b = 4 + fc % 2
            p.bank_acquire(b)
            for dc in range(DC):
                inst = nc.tensor.matmul(p.psum[b][:],
                                        wsu[dc][:, fc * 128:(fc + 1) * 128],
                                        xnt[dc][:], start=(dc == 0),
                                        stop=(dc == DC - 1))
            mmv = p.inc(inst, "mm", 1)
            nc.scalar.wait_ge(p.sem("mm"), mmv)
            p.wait(nc.scalar, "dv")
            p.s("activation", out=hs[fc][:], in_=p.psum[b][:], func=AF.Silu)
            p.bank_release(b, "ac")
        p.wait(nc.tensor, "ac")
        for dc in range(DC):
            b = 6 + dc % 2
            p.bank_acquire(b)
            for fc in range(4):
                inst = nc.tensor.matmul(p.psum[b][:],
                                        wsd[fc][:, dc * 128:(dc + 1) * 128],
                                        hs[fc][:], start=(fc == 0),
                                        stop=(fc == 3))
            mmv = p.inc(inst, "mm", 1)
            nc.vector.wait_ge(p.sem("mm"), mmv)
            ab = accb[dc % 2]
            p.slot_guard(nc.vector, dc % 2)
            p.v("tensor_tensor", out=ab[:], in0=xft[dc][:], in1=p.psum[b][:],
                op=ALU.add)
            p.bank_release(b, "dv")
            p.v("tensor_tensor", out=ab[:], in0=ab[:], in1=a0t[dc][:], op=ALU.add)
            nc.gpsimd.wait_ge(p.sem("dv"), p.cnt["dv"])
            p.store_slot(dc % 2, out=scr["acc0"][dc, :, c0:c0 + TT], in_=ab[:])
        p.tile_end()


def phase_exp0(p, cst, io, scr):
    nc = p.nc
    xft = [p.sbl(f"e0_xf{i}", [128, T], BF16) for i in range(DC)]
    acc = [p.sbl(f"e0_acc{i}", [128, T], F32) for i in range(DC)]
    wup = [[p.sbl(f"e0_wu{b}_{i}", [128, 512], BF16) for i in range(DC)]
           for b in range(2)]
    wdn = [[p.sbl(f"e0_wd{b}_{i}", [128, D], BF16) for i in range(4)]
           for b in range(2)]
    wsg = [p.sbl(f"e0_wsg{i}", [128, 1024], F32) for i in range(2)]
    hsb = [p.sbl(f"e0_h{i}", [128, TT], BF16) for i in range(4)]
    xn1f = [p.sbl(f"e0_xn1f{i}", [128, TT], F32) for i in range(DC)]
    obf = p.sbl("e0_obf", [128, TT], BF16)
    oh64 = p.sbl("e0_oh64", [64, 64 * 128], BF16)
    wts = p.sbl("e0_wts", [64, T], BF16)
    p.inc(nc.sync.dma_start(out=oh64[:], in_=cst["h_oh64"].ap()[:]), "la", 16)

    buf_free = [0, 0]
    buf_ready = [0, 0]
    stage_cast = [0, 0]

    def prefetch(e, par):
        units = ([("L0_ffn_W_up", i, 512, wup[par][i]) for i in range(DC)]
                 + [("L0_ffn_W_down", i, 1024, wdn[par][i]) for i in range(4)])
        nc.sync.wait_ge(p.sem("mm"), buf_free[par])
        for u, (key, i, cols, dst) in enumerate(units):
            k = u % 2
            sn = "we%d" % k
            nc.sync.wait_ge(p.sem("po"), stage_cast[k])
            ldv = p.inc(nc.sync.dma_start(
                out=wsg[k][:, 0:cols],
                in_=io[key][e, i * 128:(i + 1) * 128, :]), sn, 16)
            nc.gpsimd.wait_ge(p.sem(sn), ldv)
            p.g("tensor_copy", out=dst[:], in_=wsg[k][:, 0:cols])
            stage_cast[k] = p.cnt["po"]
        buf_ready[par] = p.cnt["po"]

    for s in range(NS):
        cs = s * T
        p.inc(nc.sync.dma_start(out=wts[:], in_=scr["dbg_wt"][0:64, cs:cs + T]),
              "la", 16)
        for dc in range(DC):
            p.inc(nc.sync.dma_start(out=xft[dc][:],
                                    in_=scr["xn"][dc, :, cs:cs + T]), "la", 16)
            p.inc(nc.sync.dma_start(out=acc[dc][:],
                                    in_=scr["acc0"][dc, :, cs:cs + T]), "la", 16)
        p.wait(nc.tensor, "la")
        p.wait(nc.vector, "la")
        if s == 0:
            prefetch(0, 0)
        for e in range(64):
            par = e % 2
            if e + 1 < 64:
                prefetch(e + 1, 1 - par)
            elif s == 0:
                prefetch(0, 1 - par)
            nc.tensor.wait_ge(p.sem("po"), buf_ready[par])
            for jj in range(NJS):
                t0 = jj * TT
                p.bank_acquire(4)
                mmv = p.inc(nc.tensor.matmul(
                    p.psum[4][:], oh64[:, e * 128:(e + 1) * 128],
                    wts[0:64, t0:t0 + TT], start=True, stop=True), "mm", 1)
                for fc in range(4):
                    b = fc % 2
                    p.bank_acquire(b)
                    for dc in range(DC):
                        inst = nc.tensor.matmul(
                            p.psum[b][:], wup[par][dc][:, fc * 128:(fc + 1) * 128],
                            xft[dc][:, t0:t0 + TT], start=(dc == 0),
                            stop=(dc == DC - 1))
                    mmv = p.inc(inst, "mm", 1)
                    nc.scalar.wait_ge(p.sem("mm"), mmv)
                    p.s("activation", out=hsb[fc][:], in_=p.psum[b][:],
                        func=AF.Silu)
                    p.bank_release(b, "ac")
                    nc.vector.wait_ge(p.sem("ac"), p.cnt["ac"])
                    p.v("tensor_tensor", out=hsb[fc][:], in0=hsb[fc][:],
                        in1=p.psum[4][:], op=ALU.mult)
                p.bank_release(4, "dv")
                p.wait(nc.tensor, "dv")
                for dc in range(DC):
                    b = 2 + dc % 2
                    p.bank_acquire(b)
                    for fc in range(4):
                        inst = nc.tensor.matmul(
                            p.psum[b][:], wdn[par][fc][:, dc * 128:(dc + 1) * 128],
                            hsb[fc][:], start=(fc == 0), stop=(fc == 3))
                    mmv = p.inc(inst, "mm", 1)
                    nc.vector.wait_ge(p.sem("mm"), mmv)
                    p.v("tensor_tensor", out=acc[dc][:, t0:t0 + TT],
                        in0=acc[dc][:, t0:t0 + TT], in1=p.psum[b][:], op=ALU.add)
                    p.bank_release(b, "dv")
            buf_free[par] = p.cnt["mm"]
        p.barrier()
        p.wait(nc.gpsimd, "dv")
        for dc in range(DC):
            p.inc(nc.gpsimd.dma_start(out=scr["y0"][dc, :, cs:cs + T],
                                      in_=acc[dc][:]), "st", 16)
        for jj in range(NJS):
            t0 = jj * TT
            ln_fm(p, cst, [acc[dc][:, t0:t0 + TT] for dc in range(DC)],
                  [x[:] for x in xn1f], TT)
            p.wait(nc.gpsimd, "dv")
            for dc in range(DC):
                p.g("tensor_copy", out=obf[:], in_=xn1f[dc][:])
                nc.sync.wait_ge(p.sem("po"), p.cnt["po"])
                p.inc(nc.sync.dma_start(out=scr["xn_1"][dc, :, cs + t0:cs + t0 + TT],
                                        in_=obf[:]), "la", 16)
                nc.gpsimd.wait_ge(p.sem("la"), p.cnt["la"])
            p.tile_end()
        p.tile_end()


def phase_qkv1(p, cst, io, scr):
    nc = p.nc
    wq = [p.sbl(f"q1_wq{i}", [128, D], BF16) for i in range(DC)]
    wk = [p.sbl(f"q1_wk{i}", [128, 128], BF16) for i in range(DC)]
    wv = [p.sbl(f"q1_wv{i}", [128, 128], BF16) for i in range(DC)]
    xnt = [p.sbl(f"q1_xn{i}", [128, TT], BF16) for i in range(DC)]

    load_weight_bf16(p, lambda i: io["L1_attn_Wq"][i * 128:(i + 1) * 128, :],
                     wq, DC, D)
    load_weight_bf16(p, lambda i: io["L1_attn_Wk"][i * 128:(i + 1) * 128, :],
                     wk, DC, 128)
    load_weight_bf16(p, lambda i: io["L1_attn_Wv"][i * 128:(i + 1) * 128, :],
                     wv, DC, 128)
    stg = [p.sbl(f"q1_stg{i}", [128, TT], BF16) for i in range(4)]

    for j in range(NJ):
        s, jj = j // NJS, j % NJS
        t0 = jj * TT
        c0 = s * T + t0
        for dc in range(DC):
            p.inc(nc.sync.dma_start(out=xnt[dc][:],
                                    in_=scr["xn_1"][dc, :, c0:c0 + TT]), "la", 16)
        nc.tensor.wait_ge(p.sem("la"), p.cnt["la"])
        for do in range(DC):
            b = do % 4
            sl = do % 4
            p.bank_acquire(b)
            for dc in range(DC):
                inst = nc.tensor.matmul(p.psum[b][:],
                                        wq[dc][:, do * 128:(do + 1) * 128],
                                        xnt[dc][:], start=(dc == 0),
                                        stop=(dc == DC - 1))
            mmv = p.inc(inst, "mm", 1)
            nc.vector.wait_ge(p.sem("mm"), mmv)
            p.slot_guard(nc.vector, sl)
            p.v("tensor_scalar", out=stg[sl][:], in0=p.psum[b][:],
                scalar1=cst["bq"][:, do:do + 1], scalar2=None, op0=ALU.add)
            p.bank_release(b, "dv")
            nc.gpsimd.wait_ge(p.sem("dv"), p.cnt["dv"])
            p.store_slot(sl, out=scr["q1"][do, :, c0:c0 + TT], in_=stg[sl][:])
        for which, wt, bias, dst in [(0, wk, "bk", "k1"), (1, wv, "bv", "v1")]:
            b = 4 + which
            sl = which
            p.bank_acquire(b)
            for dc in range(DC):
                inst = nc.tensor.matmul(p.psum[b][:], wt[dc][:], xnt[dc][:],
                                        start=(dc == 0), stop=(dc == DC - 1))
            mmv = p.inc(inst, "mm", 1)
            nc.vector.wait_ge(p.sem("mm"), mmv)
            p.slot_guard(nc.vector, sl)
            p.v("tensor_scalar", out=stg[sl][:], in0=p.psum[b][:],
                scalar1=cst[bias][:], scalar2=None, op0=ALU.add)
            p.bank_release(b, "dv")
            nc.gpsimd.wait_ge(p.sem("dv"), p.cnt["dv"])
            p.store_slot(sl, out=scr[dst][:, c0:c0 + TT], in_=stg[sl][:])
        p.tile_end()


def phase_att1(p, cst, io, scr):
    nc = p.nc
    PAD = 128
    TP = T + 2 * PAD
    NP = 4
    wo1 = [p.sbl(f"a1_wo{i}", [128, D], BF16) for i in range(DC)]
    qt = [p.sbl(f"a1_q{i}", [64, T], BF16) for i in range(16)]
    kp = [p.sbl(f"a1_kp{g}", [64, TP], BF16) for g in range(2)]
    vp = [p.sbl(f"a1_vp{g}", [64, TP], BF16) for g in range(2)]
    vtm = [p.sbl(f"a1_vtm{g}", [128, (TP // 128) * 64], BF16) for g in range(2)]
    band = p.sbl("a1_band", [128, 384], F32)
    sc_sb = [p.sbl(f"a1_sc{i}", [128, 384], F32) for i in range(NP)]
    ncol = [p.sbl(f"a1_nc{i}", [128, 1], F32) for i in range(NP)]
    ssum = [p.sbl(f"a1_ss{i}", [128, 1], F32) for i in range(NP)]
    rc = [p.sbl(f"a1_rc{i}", [128, 1], F32) for i in range(NP)]
    probs = [p.sbl(f"a1_pr{i}", [128, 384], BF16) for i in range(NP)]
    probsT = [p.sbl(f"a1_prT{i}", [128, 384], BF16) for i in range(NP)]
    ao = [p.sbl(f"a1_ao{i}", [128, 128], BF16) for i in range(DC)]
    att = [p.sbl(f"a1_att{i}", [128, 128], F32) for i in range(2)]

    p.inc(nc.sync.dma_start(out=band[:], in_=cst["h_band"].ap()[:]), "la", 16)
    load_weight_bf16(p, lambda i: io["L1_attn_Wo"][i * 128:(i + 1) * 128, :],
                     wo1, DC, D)

    st = {"exp": [0] * NP, "rm": [0] * NP, "tc_mm": [0] * NP,
          "scale": [0] * NP, "scmm": [0] * 16}

    for s in range(NS):
        cs = s * T
        p.g("memset", kp[0][:], 0.0)
        p.g("memset", kp[1][:], 0.0)
        p.g("memset", vp[0][:], 0.0)
        p.g("memset", vp[1][:], 0.0)
        nc.sync.wait_ge(p.sem("po"), p.cnt["po"])
        for g in range(2):
            p.inc(nc.sync.dma_start(out=kp[g][:, PAD:PAD + T],
                                    in_=scr["k1"][g * 64:(g + 1) * 64, cs:cs + T]),
                  "la", 16)
            p.inc(nc.sync.dma_start(out=vp[g][:, PAD:PAD + T],
                                    in_=scr["v1"][g * 64:(g + 1) * 64, cs:cs + T]),
                  "la", 16)
        for h in range(16):
            p.inc(nc.sync.dma_start(
                out=qt[h][:],
                in_=scr["q1"][h // 2, (h % 2) * 64:(h % 2) * 64 + 64, cs:cs + T]),
                "la", 16)
        nc.tensor.wait_ge(p.sem("la"), p.cnt["la"])
        for g in range(2):
            for wc in range(TP // 128):
                b = wc % 2
                p.bank_acquire(b)
                mmv = p.inc(nc.tensor.transpose(
                    p.psum[b][:].bitcast(BF16)[:, 0:64],
                    vp[g][:, wc * 128:(wc + 1) * 128],
                    cst["idb"][0:64, 0:64]), "mm", 1)
                nc.vector.wait_ge(p.sem("mm"), mmv)
                p.v("tensor_copy", out=vtm[g][:, wc * 64:(wc + 1) * 64],
                    in_=p.psum[b][:].bitcast(BF16)[:, 0:64])
                p.bank_release(b, "dv")
        p.wait(nc.tensor, "dv")
        wo_done = 0
        for i in range(T // 128):
            nc.vector.wait_ge(p.sem("mm"), wo_done)

            def emitA(h):
                g = h // 8
                b = h % 2
                p.bank_acquire(b)
                st["scmm"][h] = p.inc(nc.tensor.matmul(
                    p.psum[b][:, 0:384],
                    qt[h][:, i * 128:(i + 1) * 128],
                    kp[g][:, i * 128:i * 128 + 384],
                    start=True, stop=True), "mm", 1)

            def emitT1a(h):
                par = h % NP
                b = h % 2
                nc.vector.wait_ge(p.sem("mm"), st["scmm"][h])
                nc.vector.wait_ge(p.sem("ac"), st["exp"][par])
                p.v("tensor_tensor", out=sc_sb[par][:], in0=p.psum[b][:, 0:384],
                    in1=band[:], op=ALU.add)
                p.bank_release(b, "dv")
                p.v("reduce_max", out=ncol[par][:], in_=sc_sb[par][:], axis=AX.X,
                    negate=True)
                st["rm"][par] = p.cnt["dv"]

            def emitTX(h):
                par = h % NP
                nc.scalar.wait_ge(p.sem("dv"), st["rm"][par])
                nc.scalar.wait_ge(p.sem("mm"), st["tc_mm"][par])
                p.s("activation", out=probs[par][:], in_=sc_sb[par][:],
                    func=AF.Exp, bias=ncol[par][:], accum_out=ssum[par][:])
                st["exp"][par] = p.cnt["ac"]

            def emitTail2(h):
                par = h % NP
                nc.vector.wait_ge(p.sem("ac"), st["exp"][par])
                p.v("reciprocal", out=rc[par][:], in_=ssum[par][:])
                p.v("tensor_scalar", out=probs[par][:], in0=probs[par][:],
                    scalar1=rc[par][:], scalar2=None, op0=ALU.mult)
                st["scale"][par] = p.cnt["dv"]

            def emitB(h):
                par = h % NP
                g = h // 8
                ch, po_ = h // 2, (h % 2) * 64
                nc.tensor.wait_ge(p.sem("dv"), st["scale"][par])
                for wc in range(3):
                    b2 = 2 + wc % 2
                    p.bank_acquire(b2)
                    mmv = p.inc(nc.tensor.transpose(
                        p.psum[b2][:].bitcast(BF16)[:, 0:128],
                        probs[par][:, wc * 128:(wc + 1) * 128], cst["idb"][:]),
                        "mm", 1)
                    if wc % 2 == 0:
                        nc.vector.wait_ge(p.sem("mm"), mmv)
                        p.v("tensor_copy",
                            out=probsT[par][:, wc * 128:(wc + 1) * 128],
                            in_=p.psum[b2][:].bitcast(BF16)[:, 0:128])
                        p.bank_release(b2, "dv")
                    else:
                        nc.scalar.wait_ge(p.sem("mm"), mmv)
                        p.s("activation",
                            out=probsT[par][:, wc * 128:(wc + 1) * 128],
                            in_=p.psum[b2][:].bitcast(BF16)[:, 0:128],
                            func=AF.Copy)
                        p.bank_release(b2, "ac")
                st["tc_mm"][par] = p.cnt["mm"]
                nc.tensor.wait_ge(p.sem("dv"), p.cnt["dv"])
                nc.tensor.wait_ge(p.sem("ac"), p.cnt["ac"])
                b3 = 4 + h % 2
                p.bank_acquire(b3)
                for wc in range(3):
                    inst = nc.tensor.matmul(
                        p.psum[b3][0:64, 0:128],
                        vtm[g][:, (i + wc) * 64:(i + wc + 1) * 64],
                        probsT[par][:, wc * 128:(wc + 1) * 128],
                        start=(wc == 0), stop=(wc == 2))
                mmv = p.inc(inst, "mm", 1)
                nc.vector.wait_ge(p.sem("mm"), mmv)
                p.v("tensor_copy", out=ao[ch][po_:po_ + 64, :],
                    in_=p.psum[b3][0:64, 0:128])
                p.bank_release(b3, "dv")

            for h in range(16):
                emitA(h)
                emitT1a(h)
                if h >= 1:
                    emitTX(h - 1)
                if h >= 2:
                    emitB(h - 2)
                if h >= 1:
                    emitTail2(h - 1)
            emitTX(15)
            emitTail2(15)
            emitB(14)
            emitB(15)
            p.wait(nc.tensor, "dv")
            for do in range(DC):
                b = 6 + do % 2
                p.bank_acquire(b)
                for dc in range(DC):
                    inst = nc.tensor.matmul(p.psum[b][:, 0:128],
                                            wo1[dc][:, do * 128:(do + 1) * 128],
                                            ao[dc][:], start=(dc == 0),
                                            stop=(dc == DC - 1))
                mmv = p.inc(inst, "mm", 1)
                nc.vector.wait_ge(p.sem("mm"), mmv)
                a = att[do % 2]
                p.slot_guard(nc.vector, do % 2)
                p.v("tensor_scalar", out=a[:], in0=p.psum[b][:, 0:128],
                    scalar1=cst["bo"][:, do:do + 1], scalar2=None, op0=ALU.add)
                p.bank_release(b, "dv")
                nc.gpsimd.wait_ge(p.sem("dv"), p.cnt["dv"])
                p.store_slot(do % 2,
                             out=scr["at1"][do, :, cs + i * 128:cs + (i + 1) * 128],
                             in_=a[:])
            wo_done = p.cnt["mm"]
        p.tile_end()


def phase_msaqkv(p, cst, io, scr):
    nc = p.nc
    wqkv = [p.sbl(f"mq_w{i}", [128, 3 * D], BF16) for i in range(DC)]
    xnt = [p.sbl(f"mq_xn{i}", [128, TT], BF16) for i in range(DC)]
    xf32 = [p.sbl(f"mq_xf{i}", [128, TT], F32) for i in range(DC)]
    xln = [p.sbl(f"mq_xln{i}", [128, TT], F32) for i in range(DC)]
    xlb = [p.sbl(f"mq_xlb{i}", [128, TT], BF16) for i in range(DC)]
    vfm = [p.sbl(f"mq_vfm{i}", [128, TT], BF16) for i in range(DC)]
    vtms = [p.sbl(f"mq_vtm{i}", [128, D], BF16) for i in range(4)]
    stg = [p.sbl(f"mq_stg{i}", [128, TT], BF16) for i in range(4)]

    load_weight_bf16(p, lambda i: io["L1_ffn_w_qkv"][i * 128:(i + 1) * 128, :],
                     wqkv, DC, 3 * D)

    for j in range(NJ):
        s, jj = j // NJS, j % NJS
        t0 = jj * TT
        c0 = s * T + t0
        for dc in range(DC):
            p.inc(nc.sync.dma_start(out=xnt[dc][:],
                                    in_=scr["xn_1"][dc, :, c0:c0 + TT]), "la", 16)
        p.wait(nc.gpsimd, "la")
        p.wait(nc.vector, "la")
        for dc in range(DC):
            p.g("tensor_copy", out=xf32[dc][:], in_=xnt[dc][:])
        ln_fm(p, cst, [x[:] for x in xf32], [x[:] for x in xln], TT,
              g=cst["ln1_w"], b=cst["ln1_b"])
        p.wait(nc.gpsimd, "dv")
        for dc in range(DC):
            p.g("tensor_copy", out=xlb[dc][:], in_=xln[dc][:])
        p.wait(nc.tensor, "po")
        for oc in range(24):
            b = oc % 4
            sl = oc % 4
            p.bank_acquire(b)
            for dc in range(DC):
                inst = nc.tensor.matmul(p.psum[b][:],
                                        wqkv[dc][:, oc * 128:(oc + 1) * 128],
                                        xlb[dc][:], start=(dc == 0),
                                        stop=(dc == DC - 1))
            mmv = p.inc(inst, "mm", 1)
            if oc < 16:
                nc.scalar.wait_ge(p.sem("mm"), mmv)
                p.slot_guard(nc.scalar, sl)
                p.s("activation", out=stg[sl][:], in_=p.psum[b][:], func=AF.Copy)
                p.bank_release(b, "ac")
                nc.gpsimd.wait_ge(p.sem("ac"), p.cnt["ac"])
                dst = scr["mq"] if oc < 8 else scr["mk_"]
                p.store_slot(sl, out=dst[oc % 8, :, c0:c0 + TT], in_=stg[sl][:])
            else:
                nc.scalar.wait_ge(p.sem("mm"), mmv)
                p.wait(nc.scalar, "mm")
                p.s("activation", out=vfm[oc - 16][:], in_=p.psum[b][:],
                    func=AF.Copy)
                p.bank_release(b, "ac")
        p.wait(nc.tensor, "ac")
        for a in range(4):
            for dc in range(DC):
                b = 4 + (a * DC + dc) % 2
                p.bank_acquire(b)
                mmv = p.inc(nc.tensor.transpose(
                    p.psum[b][:].bitcast(BF16)[:, 0:128],
                    vfm[dc][:, a * 128:(a + 1) * 128], cst["idb"][:]), "mm", 1)
                nc.vector.wait_ge(p.sem("mm"), mmv)
                p.v("tensor_copy", out=vtms[a][:, dc * 128:(dc + 1) * 128],
                    in_=p.psum[b][:].bitcast(BF16)[:, 0:128])
                p.bank_release(b, "dv")
        nc.gpsimd.wait_ge(p.sem("dv"), p.cnt["dv"])
        for a in range(4):
            r = (t0 + a * 128) // 128
            p.inc(nc.gpsimd.dma_start(out=scr["vtm"][s, r], in_=vtms[a][:]),
                  "st", 16)
        p.tile_end()


def phase_msaatt(p, cst, io, scr):
    nc = p.nc
    NP = 4
    mqs = [p.sbl(f"ma_q{i}", [128, T], BF16) for i in range(DC)]
    mks = [p.sbl(f"ma_k{i}", [128, T], BF16) for i in range(DC)]
    vt = [p.sbl(f"ma_vt{i}", [128, D], BF16) for i in range(T // 128)]
    probs = [p.sbl(f"ma_pr{i}", [128, T], BF16) for i in range(NP)]
    probsT = [p.sbl(f"ma_prT{i}", [128, T], BF16) for i in range(NP)]
    nm4 = [p.sbl(f"ma_nm4{i}", [128, 4], F32) for i in range(NP)]
    nmax = [p.sbl(f"ma_nmax{i}", [128, 1], F32) for i in range(NP)]
    ss4 = [p.sbl(f"ma_ss4{i}", [128, 4], F32) for i in range(NP)]
    rc = [p.sbl(f"ma_rc{i}", [128, 1], F32) for i in range(NP)]
    mo = [p.sbl(f"ma_mo{i}", [128, 128], BF16) for i in range(DC)]

    st = {"exp": [0] * NP, "rm": [0] * NP, "tc_mm": [0] * NP,
          "scale": [0] * NP, "scmm": [[0] * 4 for _ in range(16)]}

    for s in range(NS):
        cs = s * T
        for ch in range(DC):
            p.inc(nc.sync.dma_start(
                out=mqs[ch][:], in_=scr["mq"][ch, :, cs:cs + T]), "la", 16)
            p.inc(nc.sync.dma_start(
                out=mks[ch][:], in_=scr["mk_"][ch, :, cs:cs + T]), "la", 16)
        for wc in range(T // 128):
            p.inc(nc.sync.dma_start(out=vt[wc][:], in_=scr["vtm"][s, wc]),
                  "la", 16)
        nc.tensor.wait_ge(p.sem("la"), p.cnt["la"])
        p.wait(nc.vector, "la")
        p.wait(nc.scalar, "la")
        for i in range(T // 128):
            nc.vector.wait_ge(p.sem("st0"), p.cnt.get("st0", 0))

            def emitA(h):
                ch, po_ = h // 2, (h % 2) * 64
                for fq in range(4):
                    p.bank_acquire(fq)
                    st["scmm"][h][fq] = p.inc(nc.tensor.matmul(
                        p.psum[fq][:],
                        mqs[ch][po_:po_ + 64, i * 128:(i + 1) * 128],
                        mks[ch][po_:po_ + 64, fq * TT:(fq + 1) * TT],
                        start=True, stop=True), "mm", 1)

            def emitR(h):
                par = h % NP
                for fq in range(4):
                    nc.vector.wait_ge(p.sem("mm"), st["scmm"][h][fq])
                    p.v("reduce_max", out=nm4[par][:, fq:fq + 1],
                        in_=p.psum[fq][:], axis=AX.X, negate=True)
                p.v("tensor_reduce", out=nmax[par][:], in_=nm4[par][:],
                    axis=AX.X, op=ALU.min)
                p.v("tensor_scalar", out=nmax[par][:], in0=nmax[par][:],
                    scalar1=0.125, scalar2=None, op0=ALU.mult)
                st["rm"][par] = p.cnt["dv"]

            def emitX(h):
                par = h % NP
                nc.scalar.wait_ge(p.sem("dv"), st["rm"][par])
                nc.scalar.wait_ge(p.sem("mm"), st["tc_mm"][par])
                for fq in range(4):
                    p.s("activation", out=probs[par][:, fq * TT:(fq + 1) * TT],
                        in_=p.psum[fq][:], func=AF.Exp, scale=0.125,
                        bias=nmax[par][:], accum_out=ss4[par][:, fq:fq + 1])
                    p.bank_release(fq, "ac")
                st["exp"][par] = p.cnt["ac"]

            def emitT2(h):
                par = h % NP
                nc.vector.wait_ge(p.sem("ac"), st["exp"][par])
                p.v("tensor_reduce", out=rc[par][:], in_=ss4[par][:],
                    axis=AX.X, op=ALU.add)
                p.v("reciprocal", out=rc[par][:], in_=rc[par][:])
                p.v("tensor_scalar", out=probs[par][:], in0=probs[par][:],
                    scalar1=rc[par][:], scalar2=None, op0=ALU.mult)
                st["scale"][par] = p.cnt["dv"]

            def emitB(h):
                par = h % NP
                ch, po_ = h // 2, (h % 2) * 64
                nc.tensor.wait_ge(p.sem("dv"), st["scale"][par])
                for wc in range(T // 128):
                    b2 = 4 + wc % 2
                    p.bank_acquire(b2)
                    mmv = p.inc(nc.tensor.transpose(
                        p.psum[b2][:].bitcast(BF16)[:, 0:128],
                        probs[par][:, wc * 128:(wc + 1) * 128], cst["idb"][:]),
                        "mm", 1)
                    if wc % 2 == 0:
                        nc.vector.wait_ge(p.sem("mm"), mmv)
                        p.v("tensor_copy",
                            out=probsT[par][:, wc * 128:(wc + 1) * 128],
                            in_=p.psum[b2][:].bitcast(BF16)[:, 0:128])
                        p.bank_release(b2, "dv")
                    else:
                        nc.scalar.wait_ge(p.sem("mm"), mmv)
                        p.s("activation",
                            out=probsT[par][:, wc * 128:(wc + 1) * 128],
                            in_=p.psum[b2][:].bitcast(BF16)[:, 0:128],
                            func=AF.Copy)
                        p.bank_release(b2, "ac")
                st["tc_mm"][par] = p.cnt["mm"]
                nc.tensor.wait_ge(p.sem("dv"), p.cnt["dv"])
                nc.tensor.wait_ge(p.sem("ac"), p.cnt["ac"])
                b3 = 6 + h % 2
                p.bank_acquire(b3)
                for wc in range(T // 128):
                    inst = nc.tensor.matmul(
                        p.psum[b3][0:64, 0:128],
                        vt[wc][:, h * 64:(h + 1) * 64],
                        probsT[par][:, wc * 128:(wc + 1) * 128],
                        start=(wc == 0), stop=(wc == T // 128 - 1))
                mmv = p.inc(inst, "mm", 1)
                nc.vector.wait_ge(p.sem("mm"), mmv)
                p.v("tensor_copy", out=mo[ch][po_:po_ + 64, :],
                    in_=p.psum[b3][0:64, 0:128])
                p.bank_release(b3, "dv")

            for h in range(16):
                emitA(h)
                emitR(h)
                emitX(h)
                if h >= 2:
                    emitB(h - 2)
                emitT2(h)
            emitB(14)
            emitB(15)
            nc.gpsimd.wait_ge(p.sem("dv"), p.cnt["dv"])
            p.wait(nc.gpsimd, "st0")
            for dc in range(DC):
                p.store_slot(0, out=scr["mo"][dc, :, cs + i * 128:cs + (i + 1) * 128],
                             in_=mo[dc][:])
        p.tile_end()


def phase_msaproj(p, cst, io, scr):
    nc = p.nc
    wp = [p.sbl(f"mp_w{i}", [128, D], BF16) for i in range(DC)]
    gwT = [p.sbl(f"mp_gw{i}", [128, 8], F32) for i in range(DC)]
    mot = [p.sbl(f"mp_mo{i}", [128, TT], BF16) for i in range(DC)]
    xnt = [p.sbl(f"mp_xn{i}", [128, TT], BF16) for i in range(DC)]
    y0t = [p.sbl(f"mp_y0{i}", [128, TT], F32) for i in range(2)]
    a1t = [p.sbl(f"mp_a1{i}", [128, TT], F32) for i in range(2)]
    xf1 = [p.sbl(f"mp_xf1{i}", [128, TT], F32) for i in range(DC)]
    xgf = [p.sbl(f"mp_xg{i}", [128, TT], F32) for i in range(DC)]
    obf = p.sbl("mp_obf", [128, TT], BF16)
    pot = [p.sbl(f"mp_po{i}", [128, TT], F32) for i in range(2)]
    lg = p.sbl("mp_lg", [8, TT], F32)
    ltm = p.sbl("mp_ltm", [128, 8], F32)
    lmax = p.sbl("mp_lmax", [128, 1], F32)
    ohtm = p.sbl("mp_ohtm", [128, 8], F32)
    ohr = p.sbl("mp_ohr", [8, TT], BF16)

    load_weight_bf16(p, lambda i: io["L1_ffn_w_proj"][i * 128:(i + 1) * 128, :],
                     wp, DC, D)
    for dc in range(DC):
        p.inc(nc.sync.dma_start(out=gwT[dc][:],
                                in_=io["L1_ffn_gate_w"][dc * 128:(dc + 1) * 128, :]),
              "la", 16)

    for j in range(NJ):
        s, jj = j // NJS, j % NJS
        t0 = jj * TT
        c0 = s * T + t0
        for dc in range(DC):
            p.inc(nc.sync.dma_start(out=mot[dc][:],
                                    in_=scr["mo"][dc, :, c0:c0 + TT]), "la", 16)
            p.inc(nc.sync.dma_start(out=xnt[dc][:],
                                    in_=scr["xn_1"][dc, :, c0:c0 + TT]), "la", 16)
        nc.tensor.wait_ge(p.sem("la"), p.cnt["la"])
        p.wait(nc.gpsimd, "la")
        p.wait(nc.vector, "la")
        for dc in range(DC):
            p.g("tensor_copy", out=xf1[dc][:], in_=xnt[dc][:])
        p.wait(nc.vector, "po")
        for do in range(DC):
            b = do % 4
            p.bank_acquire(b)
            for dc in range(DC):
                inst = nc.tensor.matmul(p.psum[b][:],
                                        wp[dc][:, do * 128:(do + 1) * 128],
                                        mot[dc][:], start=(dc == 0),
                                        stop=(dc == DC - 1))
            mmv = p.inc(inst, "mm", 1)
            nc.vector.wait_ge(p.sem("mm"), mmv)
            p.v("tensor_tensor", out=xf1[do][:], in0=xf1[do][:],
                in1=p.psum[b][:], op=ALU.add)
            p.bank_release(b, "dv")
        p.wait(nc.gpsimd, "dv")
        for dc in range(DC):
            p.inc(nc.gpsimd.dma_start(out=scr["xf1"][dc, :, c0:c0 + TT],
                                      in_=xf1[dc][:]), "st", 16)
        ln_fm(p, cst, [x[:] for x in xf1], [x[:] for x in xgf], TT,
              g=cst["ln2_w"], b=cst["ln2_b"])
        p.wait(nc.gpsimd, "dv")
        for dc in range(DC):
            p.g("tensor_copy", out=obf[:], in_=xgf[dc][:])
            nc.sync.wait_ge(p.sem("po"), p.cnt["po"])
            p.inc(nc.sync.dma_start(out=scr["xg"][dc, :, c0:c0 + TT],
                                    in_=obf[:]), "la", 16)
            nc.gpsimd.wait_ge(p.sem("la"), p.cnt["la"])
        for dc in range(DC):
            p.wait(nc.sync, "dv")
            p.slot_guard(nc.sync, 2 + dc % 2)
            p.inc(nc.sync.dma_start(out=y0t[dc % 2][:],
                                    in_=scr["y0"][dc, :, c0:c0 + TT]), "lb", 16)
            p.inc(nc.sync.dma_start(out=a1t[dc % 2][:],
                                    in_=scr["at1"][dc, :, c0:c0 + TT]), "lb", 16)
            p.wait(nc.vector, "lb")
            po_ = pot[dc % 2]
            p.v("tensor_tensor", out=po_[:], in0=y0t[dc % 2][:],
                in1=a1t[dc % 2][:], op=ALU.add)
            p.v("tensor_tensor", out=po_[:], in0=po_[:], in1=xf1[dc][:],
                op=ALU.add)
            nc.gpsimd.wait_ge(p.sem("dv"), p.cnt["dv"])
            p.store_slot(2 + dc % 2, out=scr["pre_out"][dc, :, c0:c0 + TT],
                         in_=po_[:])
        p.wait(nc.tensor, "dv")
        p.bank_acquire(6)
        for dc in range(DC):
            inst = nc.tensor.matmul(p.psum[6][0:8, :], gwT[dc][:], xgf[dc][:],
                                    start=(dc == 0), stop=(dc == DC - 1))
        mmv = p.inc(inst, "mm", 1)
        nc.vector.wait_ge(p.sem("mm"), mmv)
        p.v("tensor_scalar", out=lg[:], in0=p.psum[6][0:8, :],
            scalar1=cst["gate_b"][:], scalar2=None, op0=ALU.add)
        p.bank_release(6, "dv")
        for a in range(4):
            p.wait(nc.tensor, "dv")
            p.bank_acquire(7)
            mmv = p.inc(nc.tensor.transpose(
                p.psum[7][:, 0:8], lg[:, a * 128:(a + 1) * 128],
                cst["idf"][0:8, 0:8]), "mm", 1)
            nc.vector.wait_ge(p.sem("mm"), mmv)
            p.v("tensor_copy", out=ltm[:], in_=p.psum[7][:, 0:8])
            p.bank_release(7, "dv")
            p.v("reduce_max", out=lmax[:], in_=ltm[:], axis=AX.X)
            p.v("tensor_scalar", out=ohtm[:], in0=ltm[:], scalar1=lmax[:],
                scalar2=None, op0=ALU.is_ge)
            p.wait(nc.tensor, "dv")
            p.bank_acquire(7)
            mmv = p.inc(nc.tensor.transpose(
                p.psum[7][0:8, 0:128], ohtm[:], cst["idf"][:]), "mm", 1)
            nc.vector.wait_ge(p.sem("mm"), mmv)
            p.v("tensor_copy", out=ohr[0:8, a * 128:(a + 1) * 128],
                in_=p.psum[7][0:8, 0:128])
            p.bank_release(7, "dv")
        nc.gpsimd.wait_ge(p.sem("dv"), p.cnt["dv"])
        p.inc(nc.gpsimd.dma_start(out=scr["w1h"][0:8, c0:c0 + TT],
                                  in_=ohr[:]), "st", 16)
        p.tile_end()


def phase_exp1(p, cst, io, scr):
    nc = p.nc
    wu1 = [p.sbl(f"e1_wu{i}", [128, HU], BF16) for i in range(DC)]
    wd1 = [p.sbl(f"e1_wd{i}", [128, D], BF16) for i in range(HC)]
    hsb = [p.sbl(f"e1_h{i}", [128, TT], BF16) for i in range(HC)]
    xgt = [[p.sbl(f"e1_xg{b}_{i}", [128, TT], BF16) for i in range(DC)]
           for b in range(2)]
    pot = [p.sbl(f"e1_po{i}", [128, TT], F32) for i in range(2)]
    oh8e = [p.sbl(f"e1_oh8{b}", [8, 128], BF16) for b in range(2)]
    w1h = [p.sbl(f"e1_w1h{b}", [8, TT], BF16) for b in range(2)]
    msk = [p.sbl(f"e1_msk{b}", [128, TT], BF16) for b in range(2)]
    p.inc(nc.sync.dma_start(out=oh8e[0][:],
                            in_=cst["h_oh8"].ap()[0:8, 0:128]), "la", 16)
    p.wsg2 = [p.sbl(f"e1_wsg2{i}", [128, 512], F32) for i in range(2)]
    p.wf_guard = [0, 0]

    h_end = [0, 0]
    y_end = [0]
    mask_po = [0, 0]
    h_last = [0]

    def load_w1(e):
        nc.sync.wait_ge(p.sem("mm"), h_last[0])
        for i in range(DC):
            for hh in range(8):
                k = (i * 8 + hh) % 2
                if p.cnt.get("we%d" % k, 0):
                    nc.sync.wait_ge(p.sem("dv"), p.wcast_guard[k][0])
                ldv = p.inc(nc.sync.dma_start(
                    out=p.wstage[k][:],
                    in_=io["L1_ffn_w1"][e, i * 128:(i + 1) * 128,
                                        hh * 512:(hh + 1) * 512]),
                    "we%d" % k, 16)
                nc.vector.wait_ge(p.sem("we%d" % k), ldv)
                p.v("tensor_copy", out=wu1[i][:, hh * 512:(hh + 1) * 512],
                    in_=p.wstage[k][:])
                p.wcast_guard[k] = (p.cnt["dv"], p.cnt["po"])

    def load_w2(e):
        nc.gpsimd.wait_ge(p.sem("mm"), y_end[0])
        for u in range(2 * HC):
            i, hh = u // 2, u % 2
            k = u % 2
            if p.cnt.get("wf%d" % k, 0):
                nc.gpsimd.wait_ge(p.sem("po"), p.wf_guard[k])
            ldv = p.inc(nc.gpsimd.dma_start(
                out=p.wsg2[k][:],
                in_=io["L1_ffn_w2"][e, i * 128:(i + 1) * 128,
                                    hh * 512:(hh + 1) * 512]),
                "wf%d" % k, 16)
            nc.gpsimd.wait_ge(p.sem("wf%d" % k), ldv)
            p.g("tensor_copy", out=wd1[i][:, hh * 512:(hh + 1) * 512],
                in_=p.wsg2[k][:])
            p.wf_guard[k] = p.cnt["po"]

    def prep(e, j):
        par = j % 2
        s, jj = j // NJS, j % NJS
        c0 = s * T + jj * TT
        nc.sync.wait_ge(p.sem("mm"), h_end[par])
        for dc in range(DC):
            p.inc(nc.sync.dma_start(out=xgt[par][dc][:],
                                    in_=scr["xg"][dc, :, c0:c0 + TT]),
                  "lc%d" % par, 16)
        p.inc(nc.sync.dma_start(out=w1h[par][:],
                                in_=scr["w1h"][0:8, c0:c0 + TT]),
              "lc%d" % par, 16)
        lav = p.cnt["lc%d" % par]
        nc.tensor.wait_ge(p.sem("lc%d" % par), lav)
        nc.tensor.wait_ge(p.sem("la"), p.cnt["la"])
        p.bank_acquire(4)
        mmv = p.inc(nc.tensor.matmul(
            p.psum[4][:], oh8e[e % 2][:],
            w1h[par][0:8, :], start=True, stop=True), "mm", 1)
        nc.vector.wait_ge(p.sem("mm"), mmv)
        nc.vector.wait_ge(p.sem("po"), mask_po[par])
        p.v("tensor_copy", out=msk[par][:], in_=p.psum[4][:])
        p.bank_release(4, "dv")
        nc.gpsimd.wait_ge(p.sem("lc%d" % par), lav)
        nc.gpsimd.wait_ge(p.sem("dv"), p.cnt["dv"])
        for dc in range(DC):
            p.g("tensor_tensor", out=xgt[par][dc][:], in0=xgt[par][dc][:],
                in1=msk[par][:], op=ALU.mult)
        mask_po[par] = p.cnt["po"]

    for e in range(8):
        if e + 1 < 8:
            nc.sync.wait_ge(p.sem("mm"), h_last[0])
            p.inc(nc.sync.dma_start(
                out=oh8e[(e + 1) % 2][:],
                in_=cst["h_oh8"].ap()[0:8, (e + 1) * 128:(e + 2) * 128]),
                "la", 16)
        load_w1(e)
        load_w2(e)
        if e == 0:
            prep(0, 0)
        for j in range(NJ):
            par = j % 2
            s, jj = j // NJS, j % NJS
            c0 = s * T + jj * TT
            nc.tensor.wait_ge(p.sem("po"), mask_po[par])
            nc.tensor.wait_ge(p.sem("dv"), p.wcast_guard[0][0])
            nc.tensor.wait_ge(p.sem("dv"), p.wcast_guard[1][0])
            nc.tensor.wait_ge(p.sem("po"), p.wf_guard[0])
            nc.tensor.wait_ge(p.sem("po"), p.wf_guard[1])
            for fc in range(HC):
                b = fc % 2
                p.bank_acquire(b)
                for dc in range(DC):
                    inst = nc.tensor.matmul(
                        p.psum[b][:], wu1[dc][:, fc * 128:(fc + 1) * 128],
                        xgt[par][dc][:], start=(dc == 0), stop=(dc == DC - 1))
                mmv = p.inc(inst, "mm", 1)
                nc.scalar.wait_ge(p.sem("mm"), mmv)
                p.s("activation", out=hsb[fc][:], in_=p.psum[b][:],
                    func=AF.Gelu_apprx_tanh)
                p.bank_release(b, "ac")
            h_end[par] = p.cnt["mm"]
            h_last[0] = h_end[par]
            if j + 1 < NJ:
                prep(e, j + 1)
            elif e + 1 < 8:
                prep(e + 1, 0)
            p.wait(nc.tensor, "ac")
            for dc in range(DC):
                nc.sync.wait_ge(p.sem("st%d" % (dc % 2)),
                                p.cnt.get("st%d" % (dc % 2), 0))
                p.inc(nc.sync.dma_start(out=pot[dc % 2][:],
                                        in_=scr["pre_out"][dc, :, c0:c0 + TT]),
                      "lb%d" % (dc % 2), 16)
                b = 2 + dc % 2
                p.bank_acquire(b)
                for fc in range(HC):
                    inst = nc.tensor.matmul(
                        p.psum[b][:], wd1[fc][:, dc * 128:(dc + 1) * 128],
                        hsb[fc][:], start=(fc == 0), stop=(fc == HC - 1))
                mmv = p.inc(inst, "mm", 1)
                nc.vector.wait_ge(p.sem("mm"), mmv)
                p.wait(nc.vector, "lb%d" % (dc % 2))
                p.v("tensor_tensor", out=pot[dc % 2][:], in0=pot[dc % 2][:],
                    in1=p.psum[b][:], op=ALU.add)
                p.bank_release(b, "dv")
                nc.gpsimd.wait_ge(p.sem("dv"), p.cnt["dv"])
                p.store_slot(dc % 2, out=scr["pre_out"][dc, :, c0:c0 + TT],
                             in_=pot[dc % 2][:])
        y_end[0] = p.cnt["mm"]
    p.tile_end()


def phase_final(p, cst, io, scr):
    nc = p.nc
    pot = [p.sbl(f"fn_po{i}", [128, TT], F32) for i in range(DC)]
    otm = [p.sbl(f"fn_otm{i}", [128, D], F32) for i in range(4)]
    for j in range(NJ):
        s, jj = j // NJS, j % NJS
        t0 = jj * TT
        c0 = s * T + t0
        for dc in range(DC):
            p.inc(nc.sync.dma_start(out=pot[dc][:],
                                    in_=scr["pre_out"][dc, :, c0:c0 + TT]),
                  "la", 16)
        nc.tensor.wait_ge(p.sem("la"), p.cnt["la"])
        for a in range(4):
            for dc in range(DC):
                b = (a * DC + dc) % 4
                p.bank_acquire(b)
                mmv = p.inc(nc.tensor.transpose(
                    p.psum[b][:, 0:128], pot[dc][:, a * 128:(a + 1) * 128],
                    cst["idf"][:]), "mm", 1)
                nc.vector.wait_ge(p.sem("mm"), mmv)
                p.v("tensor_copy", out=otm[a][:, dc * 128:(dc + 1) * 128],
                    in_=p.psum[b][:, 0:128])
                p.bank_release(b, "dv")
        nc.gpsimd.wait_ge(p.sem("dv"), p.cnt["dv"])
        for a in range(4):
            p.inc(nc.gpsimd.dma_start(
                out=io["out"][s, t0 + a * 128:t0 + (a + 1) * 128, :],
                in_=otm[a][:]), "st", 16)
        p.tile_end()


def build_program(upto=None, dump=()):
    p = Prog()
    nc = p.nc
    with p.es:
        io = declare_io(nc, dump)
        scr = declare_scratch(nc, dump)
        p.psum = [p.es.enter_context(nc.psum_tensor(f"pb{i}", [128, TT], F32))
                  for i in range(8)]
        p.tmp_sq = p.sb("tmp_sq", [128, TT], F32)
        p.tmp_stats = p.sb("tmp_stats", [1, 4 * TT], F32)
        p.wstage = [p.sb(f"wstage{i}", [128, 512], F32) for i in range(2)]
        p.wcast_guard = {0: (0, 0), 1: (0, 0)}
        cst = make_consts(p, io)
        phases = ALL_PHASES[:ALL_PHASES.index(upto) + 1] if upto else ALL_PHASES
        emitters = {"p0": phase_p0,
                    "k": lambda *a: phase_proj(*a, "k"),
                    "v": lambda *a: phase_proj(*a, "v"),
                    "r": lambda *a: phase_proj(*a, "r"),
                    "cmk": lambda *a: phase_proj(*a, "cmk"),
                    "scan": phase_scan,
                    "wo": phase_wo,
                    "cmv": phase_cmv,
                    "route": phase_route,
                    "exp0": phase_exp0,
                    "qkv1": phase_qkv1,
                    "att1": phase_att1,
                    "msaqkv": phase_msaqkv,
                    "msaatt": phase_msaatt,
                    "msaproj": phase_msaproj,
                    "exp1": phase_exp1,
                    "final": phase_final}
        for ph in phases:
            if ph in emitters:
                with ExitStack() as les:
                    p.local_es = les
                    emitters[ph](p, cst, io, scr)
                    p.tile_end()
                p.local_es = None
        p.wait(nc.gpsimd, "st")
        p.wait(nc.gpsimd, "la")
        for i in range(4):
            if ("st%d" % i) in p.sems:
                p.wait(nc.gpsimd, "st%d" % i)
    return p, io, scr


def kernel(**inputs):
    p, io, scr = build_program()
    x = np.ascontiguousarray(np.asarray(inputs["L0_attn_x"], dtype=np.float32))
    base = {k: np.ascontiguousarray(np.asarray(v, dtype=np.float32))
            for k, v in inputs.items() if k != "L0_attn_x"}
    in_maps = []
    for c in range(8):
        m = dict(base)
        m["x_in"] = x[2 * c:2 * c + 2]
        in_maps.append(m)
    res = run_bass_kernel_spmd(p.nc, in_maps, core_ids=list(range(8)))
    return np.concatenate([r["out"] for r in res.results], axis=0)


# version 11 (final)
# speedup vs baseline: 1.1856x; 1.0448x over previous
import numpy as np
from contextlib import ExitStack
import concourse.bass as bass
import concourse.mybir as mybir
from concourse.bass_utils import run_bass_kernel_spmd

F32 = mybir.dt.float32
BF16 = mybir.dt.bfloat16
AF = mybir.ActivationFunctionType
ALU = mybir.AluOpType
AX = mybir.AxisListType

D = 1024
HU = 4096
T = 2048
NS = 2
TC = NS * T
DC = D // 128
HC = HU // 128
TT = 512
NJS = T // TT
NJ = TC // TT
LN_EPS = 1e-5
K_CLAMP = 60.0
K_EPS = 1e-8

ALL_PHASES = ["p0", "k", "v", "r", "scan", "wo", "cmk", "cmv", "route",
              "exp0", "qkv1", "att1", "msaqkv", "msaatt", "msaproj",
              "exp1", "final"]


class Prog:

    def __init__(self):
        self.nc = bass.Bass("TRN2", target_bir_lowering=False)
        self.es = ExitStack()
        self.sems = {}
        self.cnt = {}
        self.names = set()
        self.bank_guard = {}
        self.local_es = None

    def sem(self, name):
        if name not in self.sems:
            self.sems[name] = self.es.enter_context(self.nc.semaphore(name))
            self.cnt[name] = 0
        return self.sems[name]

    def inc(self, inst, name, k=1):
        inst.then_inc(self.sem(name), k)
        self.cnt[name] += k
        return self.cnt[name]

    def wait(self, eng, name, val=None):
        eng.wait_ge(self.sem(name), self.cnt[name] if val is None else val)

    def _chained(self, eng, semname, opname, a, kw):
        s = self.sem(semname)
        if self.cnt[semname]:
            eng.wait_ge(s, self.cnt[semname])
        inst = getattr(eng, opname)(*a, **kw)
        self.inc(inst, semname, 1)
        return inst

    def v(self, opname, *a, **kw):
        return self._chained(self.nc.vector, "dv", opname, a, kw)

    def g(self, opname, *a, **kw):
        return self._chained(self.nc.gpsimd, "po", opname, a, kw)

    def s(self, opname, *a, **kw):
        return self._chained(self.nc.scalar, "ac", opname, a, kw)

    def sb(self, name, shape, dtype=F32):
        assert name not in self.names, name
        self.names.add(name)
        return self.es.enter_context(self.nc.sbuf_tensor(name, shape, dtype))

    def sbl(self, name, shape, dtype=F32):
        assert self.local_es is not None
        self.uniq = getattr(self, "uniq", 0) + 1
        name = f"{name}_{self.uniq}"
        self.names.add(name)
        return self.local_es.enter_context(self.nc.sbuf_tensor(name, shape, dtype))

    def barrier(self):
        self.nc.all_engine_barrier()

    def tile_end(self):
        self.wait(self.nc.gpsimd, "st")
        self.wait(self.nc.sync, "la")
        for s_ in ("lb", "lb0", "lb1", "lc0", "lc1", "tp0", "tp1", "tp2", "tp3",
                   "tq0", "tq1", "tq2", "tq3"):
            if s_ in self.sems:
                self.wait(self.nc.sync, s_)
        for i in range(4):
            if ("st%d" % i) in self.sems:
                self.wait(self.nc.gpsimd, "st%d" % i)
        self.barrier()

    def bank_acquire(self, b):
        for semname, val in self.bank_guard.get(b, {}).items():
            self.nc.tensor.wait_ge(self.sem(semname), val)

    def bank_release(self, b, *semnames):
        self.bank_guard[b] = {sn: self.cnt[sn] for sn in semnames}

    def store_slot(self, i, out, in_):
        sn = "st%d" % (i % 4)
        self.inc(self.nc.gpsimd.dma_start(out=out, in_=in_), sn, 16)

    def slot_guard(self, eng, i):
        sn = "st%d" % (i % 4)
        if sn in self.sems:
            eng.wait_ge(self.sem(sn), self.cnt[sn])


def declare_io(nc, dump):
    io = {}
    io["x_in"] = nc.dram_tensor("x_in", [NS, T, D], F32, kind="ExternalInput").ap()
    wspecs = {
        "L0_attn_ln0_g": [D], "L0_attn_ln0_b": [D],
        "L0_attn_pre_g": [D], "L0_attn_pre_b": [D],
        "L0_attn_post_g": [D], "L0_attn_post_b": [D],
        "L0_attn_tm_decay": [HU], "L0_attn_tm_first": [HU],
        "L0_attn_tm_mix_k": [D], "L0_attn_tm_mix_v": [D], "L0_attn_tm_mix_r": [D],
        "L0_attn_tm_Wk": [D, HU], "L0_attn_tm_Wv": [D, HU],
        "L0_attn_tm_Wr": [D, HU], "L0_attn_tm_Wo": [HU, D],
        "L0_attn_cm_mix_k": [D], "L0_attn_cm_mix_r": [D],
        "L0_attn_cm_Wk": [D, HU], "L0_attn_cm_Wv": [HU, D], "L0_attn_cm_Wr": [D, D],
        "L0_ffn_centroids": [64, D], "L0_ffn_expert_biases": [64],
        "L0_ffn_Ws_up": [D, 512], "L0_ffn_Ws_down": [512, D],
        "L0_ffn_W_up": [64, D, 512], "L0_ffn_W_down": [64, 512, D],
        "L1_attn_Wq": [D, D], "L1_attn_bq": [D],
        "L1_attn_Wk": [D, 128], "L1_attn_bk": [128],
        "L1_attn_Wv": [D, 128], "L1_attn_bv": [128],
        "L1_attn_Wo": [D, D], "L1_attn_bo": [D],
        "L1_ffn_ln1_w": [D], "L1_ffn_ln1_b": [D],
        "L1_ffn_w_qkv": [D, 3 * D], "L1_ffn_w_proj": [D, D],
        "L1_ffn_ln2_w": [D], "L1_ffn_ln2_b": [D],
        "L1_ffn_gate_w": [D, 8], "L1_ffn_gate_b": [8],
        "L1_ffn_w1": [8, D, HU], "L1_ffn_w2": [8, HU, D],
    }
    for k, shp in wspecs.items():
        io[k] = nc.dram_tensor(k, shp, F32, kind="ExternalInput").ap()
    io["out"] = nc.dram_tensor("out", [NS, T, D], F32, kind="ExternalOutput").ap()
    return io


def declare_scratch(nc, dump):
    specs = {
        "x_fm": ([DC, 128, TC], F32),
        "xn":   ([DC, 128, TC], BF16),
        "x0":   ([DC, 128, TC], F32),
        "xn1":  ([NS, DC, 128, T + 1], BF16),
        "kk":   ([HC, 128, TC], BF16),
        "vv_":  ([HC, 128, TC], BF16),
        "sr":   ([HC, 128, TC], BF16),
        "rw":   ([HC, 128, TC], BF16),
        "x0b":  ([DC, 128, TC], F32),
        "xn2":  ([NS, DC, 128, T + 1], BF16),
        "ck":   ([HC, 128, TC], BF16),
        "at0":  ([DC, 128, TC], F32),
        "acc0": ([DC, 128, TC], F32),
        "y0":   ([DC, 128, TC], F32),
        "xn_1": ([DC, 128, TC], BF16),
        "q1":   ([DC, 128, TC], BF16),
        "at1":  ([DC, 128, TC], F32),
        "mq":   ([DC, 128, TC], BF16),
        "mk_":  ([DC, 128, TC], BF16),
        "vtm":  ([NS, T // 128, 128, D], BF16),
        "mo":   ([DC, 128, TC], BF16),
        "xf1":  ([DC, 128, TC], F32),
        "xg":   ([DC, 128, TC], BF16),
        "pre_out": ([DC, 128, TC], F32),
        "dbg_wt": ([64, TC], BF16),
        "k1":  ([128, TC], BF16),
        "v1":  ([128, TC], BF16),
        "w1h": ([8, TC], BF16),
    }
    scr = {}
    for k, (shp, dt) in specs.items():
        kind = "ExternalOutput" if k in dump else "Internal"
        scr[k] = nc.dram_tensor("s_" + k, shp, dt, kind=kind).ap()
    return scr


def make_consts(p, io):
    nc = p.nc
    c = {}
    import ml_dtypes
    ident_np = np.eye(128, dtype=np.float32)
    h_idf = nc.inline_tensor(ident_np, name="c_idf")
    h_idb = nc.inline_tensor(ident_np.astype(ml_dtypes.bfloat16), name="c_idb")
    h_ones_col = nc.inline_tensor(np.ones((128, 1), np.float32), name="c_onescol")
    h_ones_row = nc.inline_tensor(np.ones((1, 128), np.float32), name="c_onesrow")
    h_ones_row_b = nc.inline_tensor(np.ones((1, 128), ml_dtypes.bfloat16),
                                    name="c_onesrowb")
    oh64 = np.zeros((64, 64 * 128), np.float32)
    for e in range(64):
        oh64[e, e * 128:(e + 1) * 128] = 1.0
    h_oh64 = nc.inline_tensor(oh64.astype(ml_dtypes.bfloat16), name="c_oh64")
    oh8 = np.zeros((8, 8 * 128), np.float32)
    for e in range(8):
        oh8[e, e * 128:(e + 1) * 128] = 1.0
    h_oh8 = nc.inline_tensor(oh8.astype(ml_dtypes.bfloat16), name="c_oh8")
    cols = {"ln_eps": LN_EPS, "k_eps": K_EPS, "k_clamp": K_CLAMP, "zero": 0.0,
            "one": 1.0, "neg1e30": -1e30, "inv_d": 1.0 / D, "half": 0.5}
    cc_np = np.zeros((128, len(cols)), np.float32)
    c["col_idx"] = {k: i for i, k in enumerate(cols)}
    for k, v in cols.items():
        cc_np[:, c["col_idx"][k]] = v
    h_cc = nc.inline_tensor(cc_np, name="c_cols")
    band = np.full((128, 384), -1e30, np.float32)
    for i in range(128):
        band[i, i:i + 256] = 0.0
    h_band = nc.inline_tensor(band, name="c_band")

    c["idf"] = p.sb("sb_idf", [128, 128], F32)
    c["idb"] = p.sb("sb_idb", [128, 128], BF16)
    c["ones_col"] = p.sb("sb_onescol", [128, 1], F32)
    c["ones_row"] = p.sb("sb_onesrow", [1, 128], F32)
    c["ones_row_b"] = p.sb("sb_onesrowb", [1, 128], BF16)
    c["cols"] = p.sb("sb_cols", [128, len(cols)], F32)
    c["zcol_b"] = p.sb("sb_zcolb", [128, 4], BF16)
    c["h_oh64"] = h_oh64
    c["h_oh8"] = h_oh8
    c["h_band"] = h_band

    p.g("memset", c["zcol_b"][:], 0.0)
    for dst, src in [(c["idf"], h_idf.ap()), (c["idb"], h_idb.ap()),
                     (c["ones_col"], h_ones_col.ap()),
                     (c["ones_row"], h_ones_row.ap()),
                     (c["ones_row_b"], h_ones_row_b.ap()),
                     (c["cols"], h_cc.ap())]:
        p.inc(nc.sync.dma_start(out=dst[:], in_=src[:]), "la", 16)

    vec_specs = [
        ("ln0_g", "L0_attn_ln0_g", DC), ("ln0_b", "L0_attn_ln0_b", DC),
        ("pre_g", "L0_attn_pre_g", DC), ("pre_b", "L0_attn_pre_b", DC),
        ("post_g", "L0_attn_post_g", DC), ("post_b", "L0_attn_post_b", DC),
        ("mix_k", "L0_attn_tm_mix_k", DC), ("mix_v", "L0_attn_tm_mix_v", DC),
        ("mix_r", "L0_attn_tm_mix_r", DC),
        ("cmix_k", "L0_attn_cm_mix_k", DC), ("cmix_r", "L0_attn_cm_mix_r", DC),
        ("tm_decay", "L0_attn_tm_decay", HC), ("tm_first", "L0_attn_tm_first", HC),
        ("eb", "L0_ffn_expert_biases", "64"),
        ("bq", "L1_attn_bq", DC), ("bk", "L1_attn_bk", 1), ("bv", "L1_attn_bv", 1),
        ("bo", "L1_attn_bo", DC),
        ("ln1_w", "L1_ffn_ln1_w", DC), ("ln1_b", "L1_ffn_ln1_b", DC),
        ("ln2_w", "L1_ffn_ln2_w", DC), ("ln2_b", "L1_ffn_ln2_b", DC),
        ("gate_b", "L1_ffn_gate_b", "8"),
    ]
    stage = p.sb("sb_vecstage", [HC, 128], F32)
    pstat = p.psum[0]
    for name, key, nch in vec_specs:
        if nch in ("64", "8"):
            rows, colw = 1, int(nch)
            src = io[key].rearrange("(o a) -> o a", o=1)
        elif nch == 1:
            rows, colw = 1, 128
            src = io[key].rearrange("(o a) -> o a", o=1)
        else:
            rows, colw = nch, 128
            src = io[key].rearrange("(a b) -> a b", b=128)
        dst = p.sb("sb_v_" + name, [colw, rows], F32)
        c[name] = dst
        p.wait(nc.sync, "mm")
        ldv = p.inc(nc.sync.dma_start(out=stage[0:rows, 0:colw], in_=src), "la", 16)
        nc.tensor.wait_ge(p.sem("la"), ldv)
        p.wait(nc.tensor, "dv")
        p.inc(nc.tensor.transpose(pstat[0:colw, 0:rows], stage[0:rows, 0:colw],
                                  c["idf"][0:rows, 0:rows]), "mm", 1)
        p.wait(nc.vector, "mm")
        p.v("tensor_copy", out=dst[:], in_=pstat[0:colw, 0:rows])
    c["decay"] = p.sb("sb_v_decay", [128, HC], F32)
    c["first"] = p.sb("sb_v_first", [128, HC], F32)
    p.wait(nc.scalar, "dv")
    p.s("activation", out=c["decay"][:], in_=c["tm_decay"][:], func=AF.Exp)
    p.s("activation", out=c["decay"][:], in_=c["decay"][:], func=AF.Exp,
        scale=-1.0)
    p.s("activation", out=c["first"][:], in_=c["tm_first"][:], func=AF.Exp)
    p.barrier()
    return c


def col(p, c, name):
    i = c["col_idx"][name]
    return c["cols"][:, i:i + 1]


def load_weight_bf16(p, src_ap_fn, dst_tiles, n_tiles, ncols):
    nc = p.nc
    hw = 512
    nh = (ncols + hw - 1) // hw
    p.wait(nc.sync, "mm")
    for i in range(n_tiles):
        for h in range(nh):
            cw = min(hw, ncols - h * hw)
            k = (i * nh + h) % 2
            sn = "ws%d" % k
            if p.cnt.get(sn, 0):
                nc.sync.wait_ge(p.sem("dv"), p.wcast_guard[k][0])
                nc.sync.wait_ge(p.sem("po"), p.wcast_guard[k][1])
            ldv = p.inc(nc.sync.dma_start(out=p.wstage[k][:, 0:cw],
                                          in_=src_ap_fn(i)[:, h * hw:h * hw + cw]),
                        sn, 16)
            if (i * nh + h) % 2 == 0:
                nc.vector.wait_ge(p.sem(sn), ldv)
                p.v("tensor_copy", out=dst_tiles[i][:, h * hw:h * hw + cw],
                    in_=p.wstage[k][:, 0:cw])
            else:
                nc.gpsimd.wait_ge(p.sem(sn), ldv)
                p.g("tensor_copy", out=dst_tiles[i][:, h * hw:h * hw + cw],
                    in_=p.wstage[k][:, 0:cw])
            p.wcast_guard[k] = (p.cnt["dv"], p.cnt["po"])
    p.barrier()


def ln_fm(p, cst, xch, out, W, g=None, b=None):
    nc = p.nc
    ps_sx, ps_sq, ps_mb, ps_rb = p.psum[4], p.psum[5], p.psum[6], p.psum[7]
    sq = p.tmp_sq
    st_ = p.tmp_stats
    p.wait(nc.gpsimd, "mm")
    p.wait(nc.tensor, "dv")
    p.wait(nc.gpsimd, "dv")
    for dc in range(DC):
        p.g("tensor_tensor", out=sq[:, 0:W], in0=xch[dc], in1=xch[dc],
            op=ALU.mult)
        nc.tensor.wait_ge(p.sem("po"), p.cnt["po"])
        nc.tensor.matmul(ps_sq[0:1, 0:W], cst["ones_col"][:], sq[:, 0:W],
                         start=(dc == 0), stop=(dc == DC - 1))
        p.inc(nc.tensor.matmul(ps_sx[0:1, 0:W], cst["ones_col"][:], xch[dc],
                               start=(dc == 0), stop=(dc == DC - 1)), "mm", 1)
        if dc < DC - 1:
            nc.gpsimd.wait_ge(p.sem("mm"), p.cnt["mm"])
    p.wait(nc.vector, "mm")
    m = st_[0:1, 0:W]
    msq = st_[0:1, TT:TT + W]
    var = st_[0:1, 2 * TT:2 * TT + W]
    rstd = st_[0:1, 3 * TT:3 * TT + W]
    p.v("tensor_scalar", out=m, in0=ps_sx[0:1, 0:W], scalar1=1.0 / D,
        scalar2=None, op0=ALU.mult)
    p.v("tensor_scalar", out=msq, in0=ps_sq[0:1, 0:W], scalar1=1.0 / D,
        scalar2=None, op0=ALU.mult)
    p.v("tensor_tensor", out=var, in0=m, in1=m, op=ALU.mult)
    p.v("tensor_tensor", out=var, in0=msq, in1=var, op=ALU.subtract)
    p.wait(nc.scalar, "dv")
    p.s("activation", out=rstd, in_=var, func=AF.Ln,
        bias=col(p, cst, "ln_eps")[0:1, :])
    p.s("activation", out=rstd, in_=rstd, func=AF.Exp, scale=-0.5)
    p.wait(nc.tensor, "ac")
    p.wait(nc.tensor, "dv")
    nc.tensor.matmul(ps_mb[:, 0:W], cst["ones_row"][:], m, start=True, stop=True)
    p.inc(nc.tensor.matmul(ps_rb[:, 0:W], cst["ones_row"][:], rstd,
                           start=True, stop=True), "mm", 1)
    p.wait(nc.vector, "mm")
    for dc in range(DC):
        p.v("tensor_tensor", out=out[dc], in0=xch[dc], in1=ps_mb[:, 0:W],
            op=ALU.subtract)
        p.v("tensor_tensor", out=out[dc], in0=out[dc], in1=ps_rb[:, 0:W],
            op=ALU.mult)
        if g is not None:
            p.v("tensor_scalar", out=out[dc], in0=out[dc],
                scalar1=g[:, dc:dc + 1], scalar2=b[:, dc:dc + 1],
                op0=ALU.mult, op1=ALU.add)


def phase_p0(p, cst, io, scr):
    nc = p.nc
    xtm = p.sbl("p0_xtm", [128, 4 * D], F32)
    xfm = [p.sbl(f"p0_xfm{i}", [128, TT], F32) for i in range(DC)]
    xnf = [p.sbl(f"p0_xnf{i}", [128, TT], F32) for i in range(DC)]
    x0f = [p.sbl(f"p0_x0f{i}", [128, TT], F32) for i in range(DC)]
    xn1f = [p.sbl(f"p0_xn1f{i}", [128, TT], F32) for i in range(DC)]
    obf = p.sbl("p0_obf", [128, TT], BF16)

    with nc.allow_non_contiguous_dma(reason="one-column zero init, 32 tiny DMAs"):
        for s in range(NS):
            for dc in range(DC):
                p.inc(nc.gpsimd.dma_start(out=scr["xn1"][s, dc, :, 0:1],
                                          in_=cst["zcol_b"][:, 0:1]), "st", 16)
                p.inc(nc.gpsimd.dma_start(out=scr["xn2"][s, dc, :, 0:1],
                                          in_=cst["zcol_b"][:, 0:1]), "st", 16)

    for j in range(NJ):
        s, jj = j // NJS, j % NJS
        t0 = jj * TT
        c0 = s * T + t0
        ld = p.inc(nc.sync.dma_start(
            out=xtm[:].rearrange("p (a d) -> p a d", a=4),
            in_=io["x_in"][s, t0:t0 + TT, :].rearrange("(a p) d -> p a d", p=128)),
            "la", 16)
        nc.tensor.wait_ge(p.sem("la"), ld)
        for a in range(4):
            for dc in range(DC):
                bank = p.psum[(a * DC + dc) % 4]
                mmv = p.inc(nc.tensor.transpose(
                    bank[:, 0:128], xtm[:, a * D + dc * 128: a * D + (dc + 1) * 128],
                    cst["idf"][:]), "mm", 1)
                nc.vector.wait_ge(p.sem("mm"), mmv)
                p.v("tensor_copy", out=xfm[dc][:, a * 128:(a + 1) * 128],
                    in_=bank[:, 0:128])
                nc.tensor.wait_ge(p.sem("dv"), p.cnt["dv"])
        p.wait(nc.gpsimd, "dv")
        for dc in range(DC):
            p.inc(nc.gpsimd.dma_start(out=scr["x_fm"][dc, :, c0:c0 + TT],
                                      in_=xfm[dc][:]), "st", 16)
        ln_fm(p, cst, [x[:] for x in xfm], [x[:] for x in xnf], TT)
        p.wait(nc.gpsimd, "dv")
        for dc in range(DC):
            p.g("tensor_copy", out=obf[:], in_=xnf[dc][:])
            nc.sync.wait_ge(p.sem("po"), p.cnt["po"])
            p.inc(nc.sync.dma_start(out=scr["xn"][dc, :, c0:c0 + TT],
                                    in_=obf[:]), "la", 16)
            nc.gpsimd.wait_ge(p.sem("la"), p.cnt["la"])
        ln_fm(p, cst, [x[:] for x in xnf], [x[:] for x in x0f], TT,
              g=cst["ln0_g"], b=cst["ln0_b"])
        p.wait(nc.gpsimd, "dv")
        for dc in range(DC):
            p.inc(nc.gpsimd.dma_start(out=scr["x0"][dc, :, c0:c0 + TT],
                                      in_=x0f[dc][:]), "st", 16)
        ln_fm(p, cst, [x[:] for x in x0f], [x[:] for x in xn1f], TT,
              g=cst["pre_g"], b=cst["pre_b"])
        p.wait(nc.gpsimd, "dv")
        for dc in range(DC):
            p.g("tensor_copy", out=obf[:], in_=xn1f[dc][:])
            nc.sync.wait_ge(p.sem("po"), p.cnt["po"])
            p.inc(nc.sync.dma_start(out=scr["xn1"][s, dc, :, t0 + 1:t0 + 1 + TT],
                                    in_=obf[:]), "la", 16)
            nc.gpsimd.wait_ge(p.sem("la"), p.cnt["la"])
        p.tile_end()


def phase_proj(p, cst, io, scr, kind):
    nc = p.nc
    wkey, mixname, src_name, dst_name = {
        "k": ("L0_attn_tm_Wk", "mix_k", "xn1", "kk"),
        "v": ("L0_attn_tm_Wv", "mix_v", "xn1", "vv_"),
        "r": ("L0_attn_tm_Wr", "mix_r", "xn1", "sr"),
        "cmk": ("L0_attn_cm_Wk", "cmix_k", "xn2", "ck"),
    }[kind]
    mix = cst[mixname]
    wbf = [p.sbl(f"pj_w{i}", [128, HU], BF16) for i in range(DC)]
    win = [[p.sbl(f"pj_win{b}_{i}", [128, TT + 1], BF16) for i in range(DC)]
           for b in range(2)]
    xk = [[p.sbl(f"pj_xk{b}_{i}", [128, TT], BF16) for i in range(DC)]
          for b in range(2)]
    tmpd = [p.sbl(f"pj_tmp{i}", [128, TT], F32) for i in range(2)]
    etmp = [p.sbl(f"pj_etmp{i}", [128, TT], F32) for i in range(2)]
    p.stage_bf = [p.sbl(f"pj_stg{i}", [128, TT], BF16) for i in range(4)]

    load_weight_bf16(p, lambda i: io[wkey][i * 128:(i + 1) * 128, :], wbf, DC, HU)

    mix_done = [0, 0]
    mix_done_po = [0, 0]
    mm_done = [0, 0]
    for j in range(NJ):
        s, jj = j // NJS, j % NJS
        t0 = jj * TT
        c0 = s * T + t0
        par = j % 2
        lsn = "lc%d" % par
        nc.sync.wait_ge(p.sem("dv"), mix_done[par])
        nc.sync.wait_ge(p.sem("po"), mix_done_po[par])
        for dc in range(DC):
            p.inc(nc.sync.dma_start(out=win[par][dc][:],
                                    in_=scr[src_name][s, dc, :, t0:t0 + TT + 1]),
                  lsn, 16)
        lav = p.cnt[lsn]
        nc.vector.wait_ge(p.sem(lsn), lav)
        nc.gpsimd.wait_ge(p.sem(lsn), lav)
        nc.vector.wait_ge(p.sem("mm"), mm_done[par])
        nc.gpsimd.wait_ge(p.sem("mm"), mm_done[par])
        for dc in range(DC):
            xx = win[par][dc][:, 0:TT]
            xn_ = win[par][dc][:, 1:TT + 1]
            if dc % 2 == 0:
                p.v("tensor_tensor", out=tmpd[0][:], in0=xn_, in1=xx,
                    op=ALU.subtract)
                p.v("scalar_tensor_tensor", out=xk[par][dc][:], in0=tmpd[0][:],
                    scalar=mix[:, dc:dc + 1], in1=xx, op0=ALU.mult, op1=ALU.add)
            else:
                p.g("tensor_tensor", out=tmpd[1][:], in0=xn_, in1=xx,
                    op=ALU.subtract)
                p.g("tensor_scalar", out=tmpd[1][:], in0=tmpd[1][:],
                    scalar1=mix[:, dc:dc + 1], scalar2=None, op0=ALU.mult)
                p.g("tensor_tensor", out=xk[par][dc][:], in0=tmpd[1][:], in1=xx,
                    op=ALU.add)
        mix_done[par] = p.cnt["dv"]
        mix_done_po[par] = p.cnt["po"]
        nc.tensor.wait_ge(p.sem("dv"), mix_done[par])
        nc.tensor.wait_ge(p.sem("po"), mix_done_po[par])
        for cc in range(HC):
            b = cc % 4
            sl = cc % 4
            p.bank_acquire(b)
            for dc in range(DC):
                inst = nc.tensor.matmul(p.psum[b][:],
                                        wbf[dc][:, cc * 128:(cc + 1) * 128],
                                        xk[par][dc][:], start=(dc == 0),
                                        stop=(dc == DC - 1))
            mmv = p.inc(inst, "mm", 1)
            if kind == "k":
                nc.vector.wait_ge(p.sem("mm"), mmv)
                nc.vector.wait_ge(p.sem("ac"), p.cnt["ac"])
                p.v("tensor_scalar_min", etmp[par][:], p.psum[b][:], K_CLAMP)
                p.slot_guard(nc.scalar, sl)
                p.wait(nc.scalar, "dv")
                p.s("activation", out=p.stage_bf[sl][:], in_=etmp[par][:],
                    func=AF.Exp)
                p.bank_release(b, "dv")
            else:
                nc.scalar.wait_ge(p.sem("mm"), mmv)
                p.slot_guard(nc.scalar, sl)
                fn = {"v": AF.Copy, "r": AF.Sigmoid, "cmk": AF.Silu}[kind]
                p.s("activation", out=p.stage_bf[sl][:], in_=p.psum[b][:], func=fn)
                p.bank_release(b, "ac")
            nc.gpsimd.wait_ge(p.sem("ac"), p.cnt["ac"])
            p.store_slot(sl, out=scr[dst_name][cc, :, c0:c0 + TT],
                         in_=p.stage_bf[sl][:])
        mm_done[par] = p.cnt["mm"]
    p.tile_end()


def phase_scan(p, cst, io, scr):
    nc = p.nc
    kb = [p.sbl(f"sc_k{i}", [128, T], BF16) for i in range(2)]
    vb = [p.sbl(f"sc_v{i}", [128, T], BF16) for i in range(2)]
    sb_ = [p.sbl(f"sc_s{i}", [128, T], BF16) for i in range(2)]
    kv = [p.sbl(f"sc_kv{i}", [128, T], F32) for i in range(2)]
    skb = [p.sbl(f"sc_skb{i}", [128, T + 1], F32) for i in range(2)]
    skvb = [p.sbl(f"sc_skvb{i}", [128, T + 1], F32) for i in range(2)]
    wk = [p.sbl(f"sc_wk{i}", [128, T], F32) for i in range(2)]
    wkv = [p.sbl(f"sc_wkv{i}", [128, T], F32) for i in range(2)]
    ob = [p.sbl(f"sc_ob{i}", [128, T], BF16) for i in range(2)]
    for i in range(2):
        p.v("memset", skb[i][:, 0:1], 0.0)
        p.v("memset", skvb[i][:, 0:1], 0.0)
    dv_guard = [0, 0]
    ac_guard = [0, 0]
    po_guard = [0, 0]
    kv_guard = [0, 0]
    for u in range(NS * HC):
        s, cc = u // HC, u % HC
        par = u % 2
        c0 = s * T
        nc.sync.wait_ge(p.sem("dv"), dv_guard[par])
        nc.sync.wait_ge(p.sem("po"), po_guard[par])
        nc.sync.wait_ge(p.sem("ac"), ac_guard[par])
        lsn = "lc%d" % par
        for dst, srcn in [(kb[par], "kk"), (vb[par], "vv_"), (sb_[par], "sr")]:
            p.inc(nc.sync.dma_start(out=dst[:], in_=scr[srcn][cc, :, c0:c0 + T]),
                  lsn, 16)
        lav = p.cnt[lsn]
        dcol = cst["decay"][:, cc:cc + 1]
        fcol = cst["first"][:, cc:cc + 1]
        nc.vector.wait_ge(p.sem(lsn), lav)
        p.v("tensor_tensor_scan", out=skb[par][:, 1:T + 1],
            data0=dcol.broadcast_to((128, T)), data1=kb[par][:], initial=0.0,
            op0=ALU.mult, op1=ALU.add)
        nc.gpsimd.wait_ge(p.sem(lsn), lav)
        nc.gpsimd.wait_ge(p.sem("dv"), kv_guard[par])
        p.g("tensor_tensor", out=kv[par][:], in0=kb[par][:], in1=vb[par][:],
            op=ALU.mult)
        pkv = p.cnt["po"]
        nc.vector.wait_ge(p.sem("po"), pkv)
        p.v("tensor_tensor_scan", out=skvb[par][:, 1:T + 1],
            data0=dcol.broadcast_to((128, T)), data1=kv[par][:], initial=0.0,
            op0=ALU.mult, op1=ALU.add)
        kv_guard[par] = p.cnt["dv"]
        nc.vector.wait_ge(p.sem("ac"), ac_guard[par])
        p.v("scalar_tensor_tensor", out=wk[par][:], in0=kb[par][:], scalar=fcol,
            in1=skb[par][:, 0:T], op0=ALU.mult, op1=ALU.add)
        dv_guard[par] = p.cnt["dv"]
        nc.scalar.wait_ge(p.sem("dv"), p.cnt["dv"])
        p.s("activation", out=wk[par][:], in_=wk[par][:], func=AF.Ln,
            bias=col(p, cst, "k_eps"))
        p.s("activation", out=wk[par][:], in_=wk[par][:], func=AF.Exp,
            scale=-1.0)
        dvr = p.cnt["ac"]
        ac_guard[par] = dvr
        nc.gpsimd.wait_ge(p.sem("dv"), kv_guard[par])
        p.g("tensor_scalar", out=wkv[par][:], in0=kv[par][:], scalar1=fcol,
            scalar2=None, op0=ALU.mult)
        p.g("tensor_tensor", out=wkv[par][:], in0=wkv[par][:],
            in1=skvb[par][:, 0:T], op=ALU.add)
        nc.gpsimd.wait_ge(p.sem("ac"), dvr)
        p.g("tensor_tensor", out=wkv[par][:], in0=wkv[par][:], in1=wk[par][:],
            op=ALU.mult)
        p.slot_guard(nc.gpsimd, par)
        p.g("tensor_tensor", out=ob[par][:], in0=wkv[par][:], in1=sb_[par][:],
            op=ALU.mult)
        po_guard[par] = p.cnt["po"]
        nc.gpsimd.wait_ge(p.sem("po"), p.cnt["po"])
        p.store_slot(par, out=scr["rw"][cc, :, c0:c0 + T], in_=ob[par][:])
    p.tile_end()


def phase_wo(p, cst, io, scr):
    nc = p.nc
    wo = [p.sbl(f"wo_w{i}", [128, D], BF16) for i in range(HC)]
    rwt = [p.sbl(f"wo_rw{i}", [128, TT], BF16) for i in range(HC)]
    x0t = [p.sbl(f"wo_x0t{i}", [128, TT], F32) for i in range(DC)]
    x0n = [p.sbl(f"wo_x0n{i}", [128, TT], F32) for i in range(DC)]
    xn2f = [p.sbl(f"wo_xn2f{i}", [128, TT], F32) for i in range(DC)]
    obf = p.sbl("wo_obf", [128, TT], BF16)

    load_weight_bf16(p, lambda i: io["L0_attn_tm_Wo"][i * 128:(i + 1) * 128, :],
                     wo, HC, D)

    for j in range(NJ):
        s, jj = j // NJS, j % NJS
        t0 = jj * TT
        c0 = s * T + t0
        for cc in range(HC):
            p.inc(nc.sync.dma_start(out=rwt[cc][:],
                                    in_=scr["rw"][cc, :, c0:c0 + TT]), "la", 16)
        for dc in range(DC):
            p.inc(nc.sync.dma_start(out=x0t[dc][:],
                                    in_=scr["x0"][dc, :, c0:c0 + TT]), "la", 16)
        nc.tensor.wait_ge(p.sem("la"), p.cnt["la"])
        p.wait(nc.vector, "la")
        for dc in range(DC):
            b = dc % 4
            p.bank_acquire(b)
            for cc in range(HC):
                inst = nc.tensor.matmul(p.psum[b][:],
                                        wo[cc][:, dc * 128:(dc + 1) * 128],
                                        rwt[cc][:], start=(cc == 0),
                                        stop=(cc == HC - 1))
            mmv = p.inc(inst, "mm", 1)
            nc.vector.wait_ge(p.sem("mm"), mmv)
            p.v("tensor_tensor", out=x0n[dc][:], in0=x0t[dc][:], in1=p.psum[b][:],
                op=ALU.add)
            p.bank_release(b, "dv")
        p.wait(nc.gpsimd, "dv")
        for dc in range(DC):
            p.inc(nc.gpsimd.dma_start(out=scr["x0b"][dc, :, c0:c0 + TT],
                                      in_=x0n[dc][:]), "st", 16)
        ln_fm(p, cst, [x[:] for x in x0n], [x[:] for x in xn2f], TT,
              g=cst["post_g"], b=cst["post_b"])
        p.wait(nc.gpsimd, "dv")
        for dc in range(DC):
            p.g("tensor_copy", out=obf[:], in_=xn2f[dc][:])
            nc.sync.wait_ge(p.sem("po"), p.cnt["po"])
            p.inc(nc.sync.dma_start(out=scr["xn2"][s, dc, :, t0 + 1:t0 + 1 + TT],
                                    in_=obf[:]), "la", 16)
            nc.gpsimd.wait_ge(p.sem("la"), p.cnt["la"])
        p.tile_end()


def phase_cmv(p, cst, io, scr):
    nc = p.nc
    wv = [p.sbl(f"cv_wv{i}", [128, D], BF16) for i in range(HC)]
    wr = [p.sbl(f"cv_wr{i}", [128, D], BF16) for i in range(DC)]
    ckt = [p.sbl(f"cv_ck{i}", [128, TT], BF16) for i in range(HC)]
    win = [p.sbl(f"cv_win{i}", [128, TT + 1], BF16) for i in range(DC)]
    xr = [p.sbl(f"cv_xr{i}", [128, TT], BF16) for i in range(DC)]
    tmpd = [p.sbl(f"cv_tmp{i}", [128, TT], F32) for i in range(2)]
    x0t = [p.sbl(f"cv_x0t{i}", [128, TT], F32) for i in range(DC)]
    sg = p.sbl("cv_sg", [128, TT], F32)
    att = [p.sbl(f"cv_att{i}", [128, TT], F32) for i in range(2)]

    load_weight_bf16(p, lambda i: io["L0_attn_cm_Wv"][i * 128:(i + 1) * 128, :],
                     wv, HC, D)
    load_weight_bf16(p, lambda i: io["L0_attn_cm_Wr"][i * 128:(i + 1) * 128, :],
                     wr, DC, D)

    for j in range(NJ):
        s, jj = j // NJS, j % NJS
        t0 = jj * TT
        c0 = s * T + t0
        for cc in range(HC):
            p.inc(nc.sync.dma_start(out=ckt[cc][:],
                                    in_=scr["ck"][cc, :, c0:c0 + TT]), "la", 16)
        for dc in range(DC):
            p.inc(nc.sync.dma_start(out=win[dc][:],
                                    in_=scr["xn2"][s, dc, :, t0:t0 + TT + 1]),
                  "la", 16)
            p.inc(nc.sync.dma_start(out=x0t[dc][:],
                                    in_=scr["x0b"][dc, :, c0:c0 + TT]), "la", 16)
        p.wait(nc.vector, "la")
        p.wait(nc.gpsimd, "la")
        for dc in range(DC):
            xx = win[dc][:, 0:TT]
            xn_ = win[dc][:, 1:TT + 1]
            if dc % 2 == 0:
                p.v("tensor_tensor", out=tmpd[0][:], in0=xn_, in1=xx,
                    op=ALU.subtract)
                p.v("scalar_tensor_tensor", out=xr[dc][:], in0=tmpd[0][:],
                    scalar=cst["cmix_r"][:, dc:dc + 1], in1=xx,
                    op0=ALU.mult, op1=ALU.add)
            else:
                p.g("tensor_tensor", out=tmpd[1][:], in0=xn_, in1=xx,
                    op=ALU.subtract)
                p.g("tensor_scalar", out=tmpd[1][:], in0=tmpd[1][:],
                    scalar1=cst["cmix_r"][:, dc:dc + 1], scalar2=None,
                    op0=ALU.mult)
                p.g("tensor_tensor", out=xr[dc][:], in0=tmpd[1][:], in1=xx,
                    op=ALU.add)
        p.wait(nc.tensor, "dv")
        p.wait(nc.tensor, "po")
        for dc in range(DC):
            b = dc % 2
            br = 2 + dc % 2
            p.bank_acquire(b)
            for cc in range(HC):
                inst = nc.tensor.matmul(p.psum[b][:],
                                        wv[cc][:, dc * 128:(dc + 1) * 128],
                                        ckt[cc][:], start=(cc == 0),
                                        stop=(cc == HC - 1))
            p.inc(inst, "mm", 1)
            p.bank_acquire(br)
            for kc in range(DC):
                inst = nc.tensor.matmul(p.psum[br][:],
                                        wr[kc][:, dc * 128:(dc + 1) * 128],
                                        xr[kc][:], start=(kc == 0),
                                        stop=(kc == DC - 1))
            mmv = p.inc(inst, "mm", 1)
            nc.scalar.wait_ge(p.sem("mm"), mmv)
            p.wait(nc.scalar, "dv")
            p.s("activation", out=sg[:], in_=p.psum[br][:], func=AF.Sigmoid)
            p.bank_release(br, "ac")
            nc.vector.wait_ge(p.sem("ac"), p.cnt["ac"])
            a = att[dc % 2]
            p.slot_guard(nc.vector, dc % 2)
            p.v("tensor_tensor", out=a[:], in0=x0t[dc][:], in1=p.psum[b][:],
                op=ALU.add)
            p.bank_release(b, "dv")
            p.v("tensor_tensor", out=a[:], in0=a[:], in1=sg[:], op=ALU.add)
            nc.gpsimd.wait_ge(p.sem("dv"), p.cnt["dv"])
            p.store_slot(dc % 2, out=scr["at0"][dc, :, c0:c0 + TT], in_=a[:])
        p.tile_end()



def phase_route(p, cst, io, scr):
    nc = p.nc
    centT = [p.sbl(f"rt_ct{i}", [128, 64], F32) for i in range(DC)]
    xnf = [p.sbl(f"rt_xnf{i}", [128, TT], F32) for i in range(DC)]
    wsu = [p.sbl(f"rt_wsu{i}", [128, 512], BF16) for i in range(DC)]
    wsd = [p.sbl(f"rt_wsd{i}", [128, D], BF16) for i in range(4)]
    xnt = [p.sbl(f"rt_xn{i}", [128, TT], BF16) for i in range(DC)]
    xft = [p.sbl(f"rt_xf{i}", [128, TT], F32) for i in range(DC)]
    a0t = [p.sbl(f"rt_a0{i}", [128, TT], F32) for i in range(DC)]
    aff = p.sbl("rt_aff", [64, TT], F32)
    biased = p.sbl("rt_biased", [64, TT], F32)
    btm = p.sbl("rt_btm", [128, 4 * 64], F32)
    maxs = p.sbl("rt_maxs", [128, 8 * 4], F32)
    throw = p.sbl("rt_throw", [1, TT], F32)
    mask = p.sbl("rt_mask", [64, TT], F32)
    sgm = p.sbl("rt_sgm", [64, TT], F32)
    srow = p.sbl("rt_srow", [1, TT], F32)
    hs = [p.sbl(f"rt_hs{i}", [128, TT], BF16) for i in range(4)]
    accb = [p.sbl(f"rt_acc{i}", [128, TT], F32) for i in range(2)]
    wtt = p.sbl("rt_wtt", [64, TT], BF16)

    cstage = p.sbl("rt_cstage", [64, D], F32)
    ldv = p.inc(nc.sync.dma_start(out=cstage[:],
                                  in_=io["L0_ffn_centroids"][:]), "ws0", 16)
    nc.tensor.wait_ge(p.sem("ws0"), ldv)
    for dc in range(DC):
        b = dc % 2
        p.bank_acquire(b)
        mmv = p.inc(nc.tensor.transpose(
            p.psum[b][:, 0:64], cstage[0:64, dc * 128:(dc + 1) * 128],
            cst["idf"][0:64, 0:64]), "mm", 1)
        nc.vector.wait_ge(p.sem("mm"), mmv)
        p.v("tensor_copy", out=centT[dc][:], in_=p.psum[b][:, 0:64])
        p.bank_release(b, "dv")
    load_weight_bf16(p, lambda i: io["L0_ffn_Ws_up"][i * 128:(i + 1) * 128, :],
                     wsu, DC, 512)
    load_weight_bf16(p, lambda i: io["L0_ffn_Ws_down"][i * 128:(i + 1) * 128, :],
                     wsd, 4, D)

    for j in range(NJ):
        s, jj = j // NJS, j % NJS
        t0 = jj * TT
        c0 = s * T + t0
        for dc in range(DC):
            p.inc(nc.sync.dma_start(out=xnt[dc][:],
                                    in_=scr["xn"][dc, :, c0:c0 + TT]), "la", 16)
            p.inc(nc.sync.dma_start(out=xft[dc][:],
                                    in_=scr["x_fm"][dc, :, c0:c0 + TT]), "la", 16)
            p.inc(nc.sync.dma_start(out=a0t[dc][:],
                                    in_=scr["at0"][dc, :, c0:c0 + TT]), "la", 16)
        nc.tensor.wait_ge(p.sem("la"), p.cnt["la"])
        p.wait(nc.vector, "la")
        p.wait(nc.gpsimd, "la")
        ln_fm(p, cst, [x[:] for x in xft], [x[:] for x in xnf], TT)
        p.wait(nc.tensor, "dv")
        p.bank_acquire(0)
        for dc in range(DC):
            inst = nc.tensor.matmul(p.psum[0][0:64, :], centT[dc][:], xnf[dc][:],
                                    start=(dc == 0), stop=(dc == DC - 1))
        mmv = p.inc(inst, "mm", 1)
        nc.vector.wait_ge(p.sem("mm"), mmv)
        p.v("tensor_copy", out=aff[:], in_=p.psum[0][0:64, :])
        p.v("tensor_scalar", out=biased[:], in0=aff[:], scalar1=cst["eb"][:],
            scalar2=None, op0=ALU.add)
        p.bank_release(0, "dv")
        p.wait(nc.tensor, "dv")
        for a in range(4):
            b = 1 + a % 2
            p.bank_acquire(b)
            mmv = p.inc(nc.tensor.transpose(
                p.psum[b][:, 0:64], biased[:, a * 128:(a + 1) * 128],
                cst["idf"][0:64, 0:64]), "mm", 1)
            nc.vector.wait_ge(p.sem("mm"), mmv)
            p.v("tensor_copy", out=btm[:, a * 64:(a + 1) * 64],
                in_=p.psum[b][:, 0:64])
            p.bank_release(b, "dv")
        for a in range(4):
            p.v("max", out=maxs[:, a * 8:(a + 1) * 8],
                in_=btm[:, a * 64:(a + 1) * 64])
        p.wait(nc.tensor, "dv")
        for a in range(4):
            b = 1 + a % 2
            p.bank_acquire(b)
            mmv = p.inc(nc.tensor.transpose(
                p.psum[b][0:1, 0:128], maxs[:, a * 8 + 7:a * 8 + 8],
                cst["idf"][:]), "mm", 1)
            nc.vector.wait_ge(p.sem("mm"), mmv)
            p.v("tensor_copy", out=throw[0:1, a * 128:(a + 1) * 128],
                in_=p.psum[b][0:1, 0:128])
            p.bank_release(b, "dv")
        p.wait(nc.tensor, "dv")
        p.bank_acquire(2)
        mmv = p.inc(nc.tensor.matmul(p.psum[2][0:64, :], cst["ones_row"][:, 0:64],
                                     throw[:], start=True, stop=True), "mm", 1)
        nc.vector.wait_ge(p.sem("mm"), mmv)
        p.v("tensor_tensor", out=mask[:], in0=biased[:], in1=p.psum[2][0:64, :],
            op=ALU.is_ge)
        p.bank_release(2, "dv")
        p.wait(nc.scalar, "dv")
        p.s("activation", out=sgm[:], in_=aff[:], func=AF.Sigmoid)
        nc.vector.wait_ge(p.sem("ac"), p.cnt["ac"])
        p.v("tensor_tensor", out=mask[:], in0=mask[:], in1=sgm[:], op=ALU.mult)
        p.wait(nc.tensor, "dv")
        p.bank_acquire(3)
        mmv = p.inc(nc.tensor.matmul(p.psum[3][0:1, :], cst["ones_col"][0:64, :],
                                     mask[:], start=True, stop=True), "mm", 1)
        nc.vector.wait_ge(p.sem("mm"), mmv)
        p.v("tensor_scalar", out=srow[:], in0=p.psum[3][0:1, :], scalar1=K_EPS,
            scalar2=None, op0=ALU.add)
        p.v("reciprocal", out=srow[:], in_=srow[:])
        p.bank_release(3, "dv")
        p.wait(nc.tensor, "dv")
        p.bank_acquire(3)
        mmv = p.inc(nc.tensor.matmul(p.psum[3][0:64, :], cst["ones_row"][:, 0:64],
                                     srow[:], start=True, stop=True), "mm", 1)
        nc.vector.wait_ge(p.sem("mm"), mmv)
        p.v("tensor_tensor", out=wtt[:], in0=mask[:],
            in1=p.psum[3][0:64, :], op=ALU.mult)
        p.bank_release(3, "dv")
        nc.gpsimd.wait_ge(p.sem("dv"), p.cnt["dv"])
        p.inc(nc.gpsimd.dma_start(out=scr["dbg_wt"][0:64, c0:c0 + TT],
                                  in_=wtt[:]), "st", 16)
        for fc in range(4):
            b = 4 + fc % 2
            p.bank_acquire(b)
            for dc in range(DC):
                inst = nc.tensor.matmul(p.psum[b][:],
                                        wsu[dc][:, fc * 128:(fc + 1) * 128],
                                        xnt[dc][:], start=(dc == 0),
                                        stop=(dc == DC - 1))
            mmv = p.inc(inst, "mm", 1)
            nc.scalar.wait_ge(p.sem("mm"), mmv)
            p.wait(nc.scalar, "dv")
            p.s("activation", out=hs[fc][:], in_=p.psum[b][:], func=AF.Silu)
            p.bank_release(b, "ac")
        p.wait(nc.tensor, "ac")
        for dc in range(DC):
            b = 6 + dc % 2
            p.bank_acquire(b)
            for fc in range(4):
                inst = nc.tensor.matmul(p.psum[b][:],
                                        wsd[fc][:, dc * 128:(dc + 1) * 128],
                                        hs[fc][:], start=(fc == 0),
                                        stop=(fc == 3))
            mmv = p.inc(inst, "mm", 1)
            nc.vector.wait_ge(p.sem("mm"), mmv)
            ab = accb[dc % 2]
            p.slot_guard(nc.vector, dc % 2)
            p.v("tensor_tensor", out=ab[:], in0=xft[dc][:], in1=p.psum[b][:],
                op=ALU.add)
            p.bank_release(b, "dv")
            p.v("tensor_tensor", out=ab[:], in0=ab[:], in1=a0t[dc][:], op=ALU.add)
            nc.gpsimd.wait_ge(p.sem("dv"), p.cnt["dv"])
            p.store_slot(dc % 2, out=scr["acc0"][dc, :, c0:c0 + TT], in_=ab[:])
        p.tile_end()


def phase_exp0(p, cst, io, scr):
    nc = p.nc
    xft = [p.sbl(f"e0_xf{i}", [128, T], BF16) for i in range(DC)]
    acc = [p.sbl(f"e0_acc{i}", [128, T], F32) for i in range(DC)]
    wup = [[p.sbl(f"e0_wu{b}_{i}", [128, 512], BF16) for i in range(DC)]
           for b in range(2)]
    wdn = [[p.sbl(f"e0_wd{b}_{i}", [128, D], BF16) for i in range(4)]
           for b in range(2)]
    wsg = [p.sbl(f"e0_wsg{i}", [128, 1024], F32) for i in range(2)]
    hsb = [p.sbl(f"e0_h{i}", [128, TT], BF16) for i in range(4)]
    xn1f = [p.sbl(f"e0_xn1f{i}", [128, TT], F32) for i in range(DC)]
    obf = p.sbl("e0_obf", [128, TT], BF16)
    oh64 = p.sbl("e0_oh64", [64, 64 * 128], BF16)
    wts = p.sbl("e0_wts", [64, T], BF16)
    p.inc(nc.sync.dma_start(out=oh64[:], in_=cst["h_oh64"].ap()[:]), "la", 16)

    buf_free = [0, 0]
    buf_ready = [0, 0]
    stage_cast = [0, 0]

    def prefetch(e, par):
        units = ([("L0_ffn_W_up", i, 512, wup[par][i]) for i in range(DC)]
                 + [("L0_ffn_W_down", i, 1024, wdn[par][i]) for i in range(4)])
        nc.sync.wait_ge(p.sem("mm"), buf_free[par])
        for u, (key, i, cols, dst) in enumerate(units):
            k = u % 2
            sn = "we%d" % k
            nc.sync.wait_ge(p.sem("po"), stage_cast[k])
            ldv = p.inc(nc.sync.dma_start(
                out=wsg[k][:, 0:cols],
                in_=io[key][e, i * 128:(i + 1) * 128, :]), sn, 16)
            nc.gpsimd.wait_ge(p.sem(sn), ldv)
            p.g("tensor_copy", out=dst[:], in_=wsg[k][:, 0:cols])
            stage_cast[k] = p.cnt["po"]
        buf_ready[par] = p.cnt["po"]

    for s in range(NS):
        cs = s * T
        p.inc(nc.sync.dma_start(out=wts[:], in_=scr["dbg_wt"][0:64, cs:cs + T]),
              "la", 16)
        for dc in range(DC):
            p.inc(nc.sync.dma_start(out=xft[dc][:],
                                    in_=scr["xn"][dc, :, cs:cs + T]), "la", 16)
            p.inc(nc.sync.dma_start(out=acc[dc][:],
                                    in_=scr["acc0"][dc, :, cs:cs + T]), "la", 16)
        p.wait(nc.tensor, "la")
        p.wait(nc.vector, "la")
        if s == 0:
            prefetch(0, 0)
        for e in range(64):
            par = e % 2
            if e + 1 < 64:
                prefetch(e + 1, 1 - par)
            elif s == 0:
                prefetch(0, 1 - par)
            nc.tensor.wait_ge(p.sem("po"), buf_ready[par])
            for jj in range(NJS):
                t0 = jj * TT
                p.bank_acquire(4)
                mmv = p.inc(nc.tensor.matmul(
                    p.psum[4][:], oh64[:, e * 128:(e + 1) * 128],
                    wts[0:64, t0:t0 + TT], start=True, stop=True), "mm", 1)
                for fc in range(4):
                    b = fc % 2
                    p.bank_acquire(b)
                    for dc in range(DC):
                        inst = nc.tensor.matmul(
                            p.psum[b][:], wup[par][dc][:, fc * 128:(fc + 1) * 128],
                            xft[dc][:, t0:t0 + TT], start=(dc == 0),
                            stop=(dc == DC - 1))
                    mmv = p.inc(inst, "mm", 1)
                    nc.scalar.wait_ge(p.sem("mm"), mmv)
                    p.s("activation", out=hsb[fc][:], in_=p.psum[b][:],
                        func=AF.Silu)
                    p.bank_release(b, "ac")
                    nc.vector.wait_ge(p.sem("ac"), p.cnt["ac"])
                    p.v("tensor_tensor", out=hsb[fc][:], in0=hsb[fc][:],
                        in1=p.psum[4][:], op=ALU.mult)
                p.bank_release(4, "dv")
                p.wait(nc.tensor, "dv")
                for dc in range(DC):
                    b = 2 + dc % 2
                    p.bank_acquire(b)
                    for fc in range(4):
                        inst = nc.tensor.matmul(
                            p.psum[b][:], wdn[par][fc][:, dc * 128:(dc + 1) * 128],
                            hsb[fc][:], start=(fc == 0), stop=(fc == 3))
                    mmv = p.inc(inst, "mm", 1)
                    nc.vector.wait_ge(p.sem("mm"), mmv)
                    p.v("tensor_tensor", out=acc[dc][:, t0:t0 + TT],
                        in0=acc[dc][:, t0:t0 + TT], in1=p.psum[b][:], op=ALU.add)
                    p.bank_release(b, "dv")
            buf_free[par] = p.cnt["mm"]
        p.barrier()
        p.wait(nc.gpsimd, "dv")
        for dc in range(DC):
            p.inc(nc.gpsimd.dma_start(out=scr["y0"][dc, :, cs:cs + T],
                                      in_=acc[dc][:]), "st", 16)
        for jj in range(NJS):
            t0 = jj * TT
            ln_fm(p, cst, [acc[dc][:, t0:t0 + TT] for dc in range(DC)],
                  [x[:] for x in xn1f], TT)
            p.wait(nc.gpsimd, "dv")
            for dc in range(DC):
                p.g("tensor_copy", out=obf[:], in_=xn1f[dc][:])
                nc.sync.wait_ge(p.sem("po"), p.cnt["po"])
                p.inc(nc.sync.dma_start(out=scr["xn_1"][dc, :, cs + t0:cs + t0 + TT],
                                        in_=obf[:]), "la", 16)
                nc.gpsimd.wait_ge(p.sem("la"), p.cnt["la"])
            p.tile_end()
        p.tile_end()


def phase_qkv1(p, cst, io, scr):
    nc = p.nc
    wq = [p.sbl(f"q1_wq{i}", [128, D], BF16) for i in range(DC)]
    wk = [p.sbl(f"q1_wk{i}", [128, 128], BF16) for i in range(DC)]
    wv = [p.sbl(f"q1_wv{i}", [128, 128], BF16) for i in range(DC)]
    xnt = [p.sbl(f"q1_xn{i}", [128, TT], BF16) for i in range(DC)]

    load_weight_bf16(p, lambda i: io["L1_attn_Wq"][i * 128:(i + 1) * 128, :],
                     wq, DC, D)
    load_weight_bf16(p, lambda i: io["L1_attn_Wk"][i * 128:(i + 1) * 128, :],
                     wk, DC, 128)
    load_weight_bf16(p, lambda i: io["L1_attn_Wv"][i * 128:(i + 1) * 128, :],
                     wv, DC, 128)
    stg = [p.sbl(f"q1_stg{i}", [128, TT], BF16) for i in range(4)]

    for j in range(NJ):
        s, jj = j // NJS, j % NJS
        t0 = jj * TT
        c0 = s * T + t0
        for dc in range(DC):
            p.inc(nc.sync.dma_start(out=xnt[dc][:],
                                    in_=scr["xn_1"][dc, :, c0:c0 + TT]), "la", 16)
        nc.tensor.wait_ge(p.sem("la"), p.cnt["la"])
        for do in range(DC):
            b = do % 4
            sl = do % 4
            p.bank_acquire(b)
            for dc in range(DC):
                inst = nc.tensor.matmul(p.psum[b][:],
                                        wq[dc][:, do * 128:(do + 1) * 128],
                                        xnt[dc][:], start=(dc == 0),
                                        stop=(dc == DC - 1))
            mmv = p.inc(inst, "mm", 1)
            nc.vector.wait_ge(p.sem("mm"), mmv)
            p.slot_guard(nc.vector, sl)
            p.v("tensor_scalar", out=stg[sl][:], in0=p.psum[b][:],
                scalar1=cst["bq"][:, do:do + 1], scalar2=None, op0=ALU.add)
            p.bank_release(b, "dv")
            nc.gpsimd.wait_ge(p.sem("dv"), p.cnt["dv"])
            p.store_slot(sl, out=scr["q1"][do, :, c0:c0 + TT], in_=stg[sl][:])
        for which, wt, bias, dst in [(0, wk, "bk", "k1"), (1, wv, "bv", "v1")]:
            b = 4 + which
            sl = which
            p.bank_acquire(b)
            for dc in range(DC):
                inst = nc.tensor.matmul(p.psum[b][:], wt[dc][:], xnt[dc][:],
                                        start=(dc == 0), stop=(dc == DC - 1))
            mmv = p.inc(inst, "mm", 1)
            nc.vector.wait_ge(p.sem("mm"), mmv)
            p.slot_guard(nc.vector, sl)
            p.v("tensor_scalar", out=stg[sl][:], in0=p.psum[b][:],
                scalar1=cst[bias][:], scalar2=None, op0=ALU.add)
            p.bank_release(b, "dv")
            nc.gpsimd.wait_ge(p.sem("dv"), p.cnt["dv"])
            p.store_slot(sl, out=scr[dst][:, c0:c0 + TT], in_=stg[sl][:])
        p.tile_end()


def phase_att1(p, cst, io, scr):
    nc = p.nc
    PAD = 128
    TP = T + 2 * PAD
    NP = 4
    wo1 = [p.sbl(f"a1_wo{i}", [128, D], BF16) for i in range(DC)]
    qt = [p.sbl(f"a1_q{i}", [64, T], BF16) for i in range(16)]
    kp = [p.sbl(f"a1_kp{g}", [64, TP], BF16) for g in range(2)]
    vp = [p.sbl(f"a1_vp{g}", [64, TP], BF16) for g in range(2)]
    vtm = [p.sbl(f"a1_vtm{g}", [128, (TP // 128) * 64], BF16) for g in range(2)]
    band = p.sbl("a1_band", [128, 384], F32)
    sc_sb = [p.sbl(f"a1_sc{i}", [128, 384], F32) for i in range(NP)]
    ncol = [p.sbl(f"a1_nc{i}", [128, 1], F32) for i in range(NP)]
    ssum = [p.sbl(f"a1_ss{i}", [128, 1], F32) for i in range(NP)]
    rc = [p.sbl(f"a1_rc{i}", [128, 1], F32) for i in range(NP)]
    probs = [p.sbl(f"a1_pr{i}", [128, 384], BF16) for i in range(NP)]
    probsT = [p.sbl(f"a1_prT{i}", [128, 384], BF16) for i in range(NP)]
    ao = [p.sbl(f"a1_ao{i}", [128, 128], BF16) for i in range(DC)]
    att = [p.sbl(f"a1_att{i}", [128, 128], F32) for i in range(2)]

    p.inc(nc.sync.dma_start(out=band[:], in_=cst["h_band"].ap()[:]), "la", 16)
    load_weight_bf16(p, lambda i: io["L1_attn_Wo"][i * 128:(i + 1) * 128, :],
                     wo1, DC, D)

    st = {"exp": [0] * NP, "rm": [0] * NP, "tc_mm": [0] * NP,
          "scale": [0] * NP, "scmm": [0] * 16}

    for s in range(NS):
        cs = s * T
        p.g("memset", kp[0][:], 0.0)
        p.g("memset", kp[1][:], 0.0)
        p.g("memset", vp[0][:], 0.0)
        p.g("memset", vp[1][:], 0.0)
        nc.sync.wait_ge(p.sem("po"), p.cnt["po"])
        for g in range(2):
            p.inc(nc.sync.dma_start(out=kp[g][:, PAD:PAD + T],
                                    in_=scr["k1"][g * 64:(g + 1) * 64, cs:cs + T]),
                  "la", 16)
            p.inc(nc.sync.dma_start(out=vp[g][:, PAD:PAD + T],
                                    in_=scr["v1"][g * 64:(g + 1) * 64, cs:cs + T]),
                  "la", 16)
        for h in range(16):
            p.inc(nc.sync.dma_start(
                out=qt[h][:],
                in_=scr["q1"][h // 2, (h % 2) * 64:(h % 2) * 64 + 64, cs:cs + T]),
                "la", 16)
        nc.tensor.wait_ge(p.sem("la"), p.cnt["la"])
        for g in range(2):
            for wc in range(TP // 128):
                b = wc % 2
                p.bank_acquire(b)
                mmv = p.inc(nc.tensor.transpose(
                    p.psum[b][:].bitcast(BF16)[:, 0:64],
                    vp[g][:, wc * 128:(wc + 1) * 128],
                    cst["idb"][0:64, 0:64]), "mm", 1)
                nc.vector.wait_ge(p.sem("mm"), mmv)
                p.v("tensor_copy", out=vtm[g][:, wc * 64:(wc + 1) * 64],
                    in_=p.psum[b][:].bitcast(BF16)[:, 0:64])
                p.bank_release(b, "dv")
        p.wait(nc.tensor, "dv")
        wo_done = 0
        for i in range(T // 128):
            nc.vector.wait_ge(p.sem("mm"), wo_done)

            def emitA(h):
                g = h // 8
                b = h % 2
                p.bank_acquire(b)
                st["scmm"][h] = p.inc(nc.tensor.matmul(
                    p.psum[b][:, 0:384],
                    qt[h][:, i * 128:(i + 1) * 128],
                    kp[g][:, i * 128:i * 128 + 384],
                    start=True, stop=True), "mm", 1)

            def emitT1a(h):
                par = h % NP
                b = h % 2
                nc.vector.wait_ge(p.sem("mm"), st["scmm"][h])
                nc.vector.wait_ge(p.sem("ac"), st["exp"][par])
                p.v("tensor_tensor", out=sc_sb[par][:], in0=p.psum[b][:, 0:384],
                    in1=band[:], op=ALU.add)
                p.bank_release(b, "dv")
                p.v("reduce_max", out=ncol[par][:], in_=sc_sb[par][:], axis=AX.X,
                    negate=True)
                st["rm"][par] = p.cnt["dv"]

            def emitTX(h):
                par = h % NP
                nc.scalar.wait_ge(p.sem("dv"), st["rm"][par])
                nc.scalar.wait_ge(p.sem("mm"), st["tc_mm"][par])
                p.s("activation", out=probs[par][:], in_=sc_sb[par][:],
                    func=AF.Exp, bias=ncol[par][:], accum_out=ssum[par][:])
                st["exp"][par] = p.cnt["ac"]

            def emitTail2(h):
                par = h % NP
                nc.vector.wait_ge(p.sem("ac"), st["exp"][par])
                p.v("reciprocal", out=rc[par][:], in_=ssum[par][:])
                p.v("tensor_scalar", out=probs[par][:], in0=probs[par][:],
                    scalar1=rc[par][:], scalar2=None, op0=ALU.mult)
                st["scale"][par] = p.cnt["dv"]

            def emitB(h):
                par = h % NP
                g = h // 8
                ch, po_ = h // 2, (h % 2) * 64
                nc.tensor.wait_ge(p.sem("dv"), st["scale"][par])
                for wc in range(3):
                    b2 = 2 + wc % 2
                    p.bank_acquire(b2)
                    mmv = p.inc(nc.tensor.transpose(
                        p.psum[b2][:].bitcast(BF16)[:, 0:128],
                        probs[par][:, wc * 128:(wc + 1) * 128], cst["idb"][:]),
                        "mm", 1)
                    if wc % 2 == 0:
                        nc.vector.wait_ge(p.sem("mm"), mmv)
                        p.v("tensor_copy",
                            out=probsT[par][:, wc * 128:(wc + 1) * 128],
                            in_=p.psum[b2][:].bitcast(BF16)[:, 0:128])
                        p.bank_release(b2, "dv")
                    else:
                        nc.scalar.wait_ge(p.sem("mm"), mmv)
                        p.s("activation",
                            out=probsT[par][:, wc * 128:(wc + 1) * 128],
                            in_=p.psum[b2][:].bitcast(BF16)[:, 0:128],
                            func=AF.Copy)
                        p.bank_release(b2, "ac")
                st["tc_mm"][par] = p.cnt["mm"]
                nc.tensor.wait_ge(p.sem("dv"), p.cnt["dv"])
                nc.tensor.wait_ge(p.sem("ac"), p.cnt["ac"])
                b3 = 4 + h % 2
                p.bank_acquire(b3)
                for wc in range(3):
                    inst = nc.tensor.matmul(
                        p.psum[b3][0:64, 0:128],
                        vtm[g][:, (i + wc) * 64:(i + wc + 1) * 64],
                        probsT[par][:, wc * 128:(wc + 1) * 128],
                        start=(wc == 0), stop=(wc == 2))
                mmv = p.inc(inst, "mm", 1)
                nc.vector.wait_ge(p.sem("mm"), mmv)
                p.v("tensor_copy", out=ao[ch][po_:po_ + 64, :],
                    in_=p.psum[b3][0:64, 0:128])
                p.bank_release(b3, "dv")

            for h in range(16):
                emitA(h)
                emitT1a(h)
                if h >= 1:
                    emitTX(h - 1)
                if h >= 2:
                    emitB(h - 2)
                if h >= 1:
                    emitTail2(h - 1)
            emitTX(15)
            emitTail2(15)
            emitB(14)
            emitB(15)
            p.wait(nc.tensor, "dv")
            for do in range(DC):
                b = 6 + do % 2
                p.bank_acquire(b)
                for dc in range(DC):
                    inst = nc.tensor.matmul(p.psum[b][:, 0:128],
                                            wo1[dc][:, do * 128:(do + 1) * 128],
                                            ao[dc][:], start=(dc == 0),
                                            stop=(dc == DC - 1))
                mmv = p.inc(inst, "mm", 1)
                nc.vector.wait_ge(p.sem("mm"), mmv)
                a = att[do % 2]
                p.slot_guard(nc.vector, do % 2)
                p.v("tensor_scalar", out=a[:], in0=p.psum[b][:, 0:128],
                    scalar1=cst["bo"][:, do:do + 1], scalar2=None, op0=ALU.add)
                p.bank_release(b, "dv")
                nc.gpsimd.wait_ge(p.sem("dv"), p.cnt["dv"])
                p.store_slot(do % 2,
                             out=scr["at1"][do, :, cs + i * 128:cs + (i + 1) * 128],
                             in_=a[:])
            wo_done = p.cnt["mm"]
        p.tile_end()


def phase_msaqkv(p, cst, io, scr):
    nc = p.nc
    wqkv = [p.sbl(f"mq_w{i}", [128, 3 * D], BF16) for i in range(DC)]
    xnt = [p.sbl(f"mq_xn{i}", [128, TT], BF16) for i in range(DC)]
    xf32 = [p.sbl(f"mq_xf{i}", [128, TT], F32) for i in range(DC)]
    xln = [p.sbl(f"mq_xln{i}", [128, TT], F32) for i in range(DC)]
    xlb = [p.sbl(f"mq_xlb{i}", [128, TT], BF16) for i in range(DC)]
    vfm = [p.sbl(f"mq_vfm{i}", [128, TT], BF16) for i in range(DC)]
    vtms = [p.sbl(f"mq_vtm{i}", [128, D], BF16) for i in range(4)]
    stg = [p.sbl(f"mq_stg{i}", [128, TT], BF16) for i in range(4)]

    load_weight_bf16(p, lambda i: io["L1_ffn_w_qkv"][i * 128:(i + 1) * 128, :],
                     wqkv, DC, 3 * D)

    for j in range(NJ):
        s, jj = j // NJS, j % NJS
        t0 = jj * TT
        c0 = s * T + t0
        for dc in range(DC):
            p.inc(nc.sync.dma_start(out=xnt[dc][:],
                                    in_=scr["xn_1"][dc, :, c0:c0 + TT]), "la", 16)
        p.wait(nc.gpsimd, "la")
        p.wait(nc.vector, "la")
        for dc in range(DC):
            p.g("tensor_copy", out=xf32[dc][:], in_=xnt[dc][:])
        ln_fm(p, cst, [x[:] for x in xf32], [x[:] for x in xln], TT,
              g=cst["ln1_w"], b=cst["ln1_b"])
        p.wait(nc.gpsimd, "dv")
        for dc in range(DC):
            p.g("tensor_copy", out=xlb[dc][:], in_=xln[dc][:])
        p.wait(nc.tensor, "po")
        for oc in range(24):
            b = oc % 4
            sl = oc % 4
            p.bank_acquire(b)
            for dc in range(DC):
                inst = nc.tensor.matmul(p.psum[b][:],
                                        wqkv[dc][:, oc * 128:(oc + 1) * 128],
                                        xlb[dc][:], start=(dc == 0),
                                        stop=(dc == DC - 1))
            mmv = p.inc(inst, "mm", 1)
            if oc < 16:
                nc.scalar.wait_ge(p.sem("mm"), mmv)
                p.slot_guard(nc.scalar, sl)
                p.s("activation", out=stg[sl][:], in_=p.psum[b][:], func=AF.Copy)
                p.bank_release(b, "ac")
                nc.gpsimd.wait_ge(p.sem("ac"), p.cnt["ac"])
                dst = scr["mq"] if oc < 8 else scr["mk_"]
                p.store_slot(sl, out=dst[oc % 8, :, c0:c0 + TT], in_=stg[sl][:])
            else:
                nc.scalar.wait_ge(p.sem("mm"), mmv)
                p.wait(nc.scalar, "mm")
                p.s("activation", out=vfm[oc - 16][:], in_=p.psum[b][:],
                    func=AF.Copy)
                p.bank_release(b, "ac")
        p.wait(nc.tensor, "ac")
        for a in range(4):
            for dc in range(DC):
                b = 4 + (a * DC + dc) % 2
                p.bank_acquire(b)
                mmv = p.inc(nc.tensor.transpose(
                    p.psum[b][:].bitcast(BF16)[:, 0:128],
                    vfm[dc][:, a * 128:(a + 1) * 128], cst["idb"][:]), "mm", 1)
                nc.vector.wait_ge(p.sem("mm"), mmv)
                p.v("tensor_copy", out=vtms[a][:, dc * 128:(dc + 1) * 128],
                    in_=p.psum[b][:].bitcast(BF16)[:, 0:128])
                p.bank_release(b, "dv")
        nc.gpsimd.wait_ge(p.sem("dv"), p.cnt["dv"])
        for a in range(4):
            r = (t0 + a * 128) // 128
            p.inc(nc.gpsimd.dma_start(out=scr["vtm"][s, r], in_=vtms[a][:]),
                  "st", 16)
        p.tile_end()


def phase_msaatt(p, cst, io, scr):
    nc = p.nc
    NP = 4
    mqs = [p.sbl(f"ma_q{i}", [128, T], BF16) for i in range(DC)]
    mks = [p.sbl(f"ma_k{i}", [128, T], BF16) for i in range(DC)]
    vt = [p.sbl(f"ma_vt{i}", [128, D], BF16) for i in range(T // 128)]
    sc_sb = [p.sbl(f"ma_sc{i}", [128, T], F32) for i in range(NP)]
    probs = [p.sbl(f"ma_pr{i}", [128, T], BF16) for i in range(NP)]
    probsT = [p.sbl(f"ma_prT{i}", [128, T], BF16) for i in range(NP)]
    nmax = [p.sbl(f"ma_nmax{i}", [128, 1], F32) for i in range(NP)]
    ssum = [p.sbl(f"ma_ss{i}", [128, 1], F32) for i in range(NP)]
    rc = [p.sbl(f"ma_rc{i}", [128, 1], F32) for i in range(NP)]
    mo = [p.sbl(f"ma_mo{i}", [128, 128], BF16) for i in range(DC)]

    st = {"exp": [0] * NP, "rm": [0] * NP, "tc_mm": [0] * NP,
          "scale": [0] * NP, "scmm": [[0] * 4 for _ in range(16)],
          "ev": [0] * NP}

    for s in range(NS):
        cs = s * T
        for ch in range(DC):
            p.inc(nc.sync.dma_start(
                out=mqs[ch][:], in_=scr["mq"][ch, :, cs:cs + T]), "la", 16)
            p.inc(nc.sync.dma_start(
                out=mks[ch][:], in_=scr["mk_"][ch, :, cs:cs + T]), "la", 16)
        for wc in range(T // 128):
            p.inc(nc.sync.dma_start(out=vt[wc][:], in_=scr["vtm"][s, wc]),
                  "la", 16)
        nc.tensor.wait_ge(p.sem("la"), p.cnt["la"])
        p.wait(nc.vector, "la")
        p.wait(nc.scalar, "la")
        for i in range(T // 128):
            nc.vector.wait_ge(p.sem("st0"), p.cnt.get("st0", 0))

            def emitA(h):
                ch, po_ = h // 2, (h % 2) * 64
                for fq in range(4):
                    p.bank_acquire(fq)
                    st["scmm"][h][fq] = p.inc(nc.tensor.matmul(
                        p.psum[fq][:],
                        mqs[ch][po_:po_ + 64, i * 128:(i + 1) * 128],
                        mks[ch][po_:po_ + 64, fq * TT:(fq + 1) * TT],
                        start=True, stop=True), "mm", 1)

            def emitEv(h):
                par = h % NP
                nc.scalar.wait_ge(p.sem("dv"), st["rm"][par])
                for fq in range(4):
                    nc.scalar.wait_ge(p.sem("mm"), st["scmm"][h][fq])
                    p.s("activation", out=sc_sb[par][:, fq * TT:(fq + 1) * TT],
                        in_=p.psum[fq][:], func=AF.Copy)
                    p.bank_release(fq, "ac")
                st["ev"][par] = p.cnt["ac"]

            def emitMax(h):
                par = h % NP
                nc.vector.wait_ge(p.sem("ac"), st["ev"][par])
                p.v("reduce_max", out=nmax[par][:], in_=sc_sb[par][:],
                    axis=AX.X, negate=True)
                p.v("tensor_scalar", out=nmax[par][:], in0=nmax[par][:],
                    scalar1=0.125, scalar2=None, op0=ALU.mult)
                st["rm"][par] = p.cnt["dv"]

            def emitX(h):
                par = h % NP
                nc.scalar.wait_ge(p.sem("dv"), st["rm"][par])
                nc.scalar.wait_ge(p.sem("mm"), st["tc_mm"][par])
                p.s("activation", out=probs[par][:], in_=sc_sb[par][:],
                    func=AF.Exp, scale=0.125, bias=nmax[par][:],
                    accum_out=ssum[par][:])
                st["exp"][par] = p.cnt["ac"]

            def emitT2(h):
                par = h % NP
                nc.vector.wait_ge(p.sem("ac"), st["exp"][par])
                p.v("reciprocal", out=rc[par][:], in_=ssum[par][:])
                p.v("tensor_scalar", out=probs[par][:], in0=probs[par][:],
                    scalar1=rc[par][:], scalar2=None, op0=ALU.mult)
                st["scale"][par] = p.cnt["dv"]

            def emitB(h):
                par = h % NP
                ch, po_ = h // 2, (h % 2) * 64
                nc.tensor.wait_ge(p.sem("dv"), st["scale"][par])
                for wc in range(T // 128):
                    b2 = 4 + wc % 2
                    p.bank_acquire(b2)
                    mmv = p.inc(nc.tensor.transpose(
                        p.psum[b2][:].bitcast(BF16)[:, 0:128],
                        probs[par][:, wc * 128:(wc + 1) * 128], cst["idb"][:]),
                        "mm", 1)
                    if wc % 2 == 0:
                        nc.vector.wait_ge(p.sem("mm"), mmv)
                        p.v("tensor_copy",
                            out=probsT[par][:, wc * 128:(wc + 1) * 128],
                            in_=p.psum[b2][:].bitcast(BF16)[:, 0:128])
                        p.bank_release(b2, "dv")
                    else:
                        nc.scalar.wait_ge(p.sem("mm"), mmv)
                        p.s("activation",
                            out=probsT[par][:, wc * 128:(wc + 1) * 128],
                            in_=p.psum[b2][:].bitcast(BF16)[:, 0:128],
                            func=AF.Copy)
                        p.bank_release(b2, "ac")
                st["tc_mm"][par] = p.cnt["mm"]
                nc.tensor.wait_ge(p.sem("dv"), p.cnt["dv"])
                nc.tensor.wait_ge(p.sem("ac"), p.cnt["ac"])
                b3 = 6 + h % 2
                p.bank_acquire(b3)
                for wc in range(T // 128):
                    inst = nc.tensor.matmul(
                        p.psum[b3][0:64, 0:128],
                        vt[wc][:, h * 64:(h + 1) * 64],
                        probsT[par][:, wc * 128:(wc + 1) * 128],
                        start=(wc == 0), stop=(wc == T // 128 - 1))
                mmv = p.inc(inst, "mm", 1)
                nc.vector.wait_ge(p.sem("mm"), mmv)
                p.v("tensor_copy", out=mo[ch][po_:po_ + 64, :],
                    in_=p.psum[b3][0:64, 0:128])
                p.bank_release(b3, "dv")

            for h in range(16):
                emitA(h)
                emitEv(h)
                if h >= 1:
                    emitMax(h - 1)
                if h >= 2:
                    emitX(h - 2)
                if h >= 3:
                    emitB(h - 3)
                if h >= 2:
                    emitT2(h - 2)
            emitMax(15)
            emitX(14)
            emitB(13)
            emitT2(14)
            emitX(15)
            emitB(14)
            emitT2(15)
            emitB(15)
            nc.gpsimd.wait_ge(p.sem("dv"), p.cnt["dv"])
            p.wait(nc.gpsimd, "st0")
            for dc in range(DC):
                p.store_slot(0, out=scr["mo"][dc, :, cs + i * 128:cs + (i + 1) * 128],
                             in_=mo[dc][:])
        p.tile_end()


def phase_msaproj(p, cst, io, scr):
    nc = p.nc
    wp = [p.sbl(f"mp_w{i}", [128, D], BF16) for i in range(DC)]
    gwT = [p.sbl(f"mp_gw{i}", [128, 8], F32) for i in range(DC)]
    mot = [p.sbl(f"mp_mo{i}", [128, TT], BF16) for i in range(DC)]
    xnt = [p.sbl(f"mp_xn{i}", [128, TT], BF16) for i in range(DC)]
    y0t = [p.sbl(f"mp_y0{i}", [128, TT], F32) for i in range(2)]
    a1t = [p.sbl(f"mp_a1{i}", [128, TT], F32) for i in range(2)]
    xf1 = [p.sbl(f"mp_xf1{i}", [128, TT], F32) for i in range(DC)]
    xgf = [p.sbl(f"mp_xg{i}", [128, TT], F32) for i in range(DC)]
    obf = p.sbl("mp_obf", [128, TT], BF16)
    pot = [p.sbl(f"mp_po{i}", [128, TT], F32) for i in range(2)]
    lg = p.sbl("mp_lg", [8, TT], F32)
    ltm = p.sbl("mp_ltm", [128, 8], F32)
    lmax = p.sbl("mp_lmax", [128, 1], F32)
    ohtm = p.sbl("mp_ohtm", [128, 8], F32)
    ohr = p.sbl("mp_ohr", [8, TT], BF16)

    load_weight_bf16(p, lambda i: io["L1_ffn_w_proj"][i * 128:(i + 1) * 128, :],
                     wp, DC, D)
    for dc in range(DC):
        p.inc(nc.sync.dma_start(out=gwT[dc][:],
                                in_=io["L1_ffn_gate_w"][dc * 128:(dc + 1) * 128, :]),
              "la", 16)

    for j in range(NJ):
        s, jj = j // NJS, j % NJS
        t0 = jj * TT
        c0 = s * T + t0
        for dc in range(DC):
            p.inc(nc.sync.dma_start(out=mot[dc][:],
                                    in_=scr["mo"][dc, :, c0:c0 + TT]), "la", 16)
            p.inc(nc.sync.dma_start(out=xnt[dc][:],
                                    in_=scr["xn_1"][dc, :, c0:c0 + TT]), "la", 16)
        nc.tensor.wait_ge(p.sem("la"), p.cnt["la"])
        p.wait(nc.gpsimd, "la")
        p.wait(nc.vector, "la")
        for dc in range(DC):
            p.g("tensor_copy", out=xf1[dc][:], in_=xnt[dc][:])
        p.wait(nc.vector, "po")
        for do in range(DC):
            b = do % 4
            p.bank_acquire(b)
            for dc in range(DC):
                inst = nc.tensor.matmul(p.psum[b][:],
                                        wp[dc][:, do * 128:(do + 1) * 128],
                                        mot[dc][:], start=(dc == 0),
                                        stop=(dc == DC - 1))
            mmv = p.inc(inst, "mm", 1)
            nc.vector.wait_ge(p.sem("mm"), mmv)
            p.v("tensor_tensor", out=xf1[do][:], in0=xf1[do][:],
                in1=p.psum[b][:], op=ALU.add)
            p.bank_release(b, "dv")
        p.wait(nc.gpsimd, "dv")
        for dc in range(DC):
            p.inc(nc.gpsimd.dma_start(out=scr["xf1"][dc, :, c0:c0 + TT],
                                      in_=xf1[dc][:]), "st", 16)
        ln_fm(p, cst, [x[:] for x in xf1], [x[:] for x in xgf], TT,
              g=cst["ln2_w"], b=cst["ln2_b"])
        p.wait(nc.gpsimd, "dv")
        for dc in range(DC):
            p.g("tensor_copy", out=obf[:], in_=xgf[dc][:])
            nc.sync.wait_ge(p.sem("po"), p.cnt["po"])
            p.inc(nc.sync.dma_start(out=scr["xg"][dc, :, c0:c0 + TT],
                                    in_=obf[:]), "la", 16)
            nc.gpsimd.wait_ge(p.sem("la"), p.cnt["la"])
        for dc in range(DC):
            p.wait(nc.sync, "dv")
            p.slot_guard(nc.sync, 2 + dc % 2)
            p.inc(nc.sync.dma_start(out=y0t[dc % 2][:],
                                    in_=scr["y0"][dc, :, c0:c0 + TT]), "lb", 16)
            p.inc(nc.sync.dma_start(out=a1t[dc % 2][:],
                                    in_=scr["at1"][dc, :, c0:c0 + TT]), "lb", 16)
            p.wait(nc.vector, "lb")
            po_ = pot[dc % 2]
            p.v("tensor_tensor", out=po_[:], in0=y0t[dc % 2][:],
                in1=a1t[dc % 2][:], op=ALU.add)
            p.v("tensor_tensor", out=po_[:], in0=po_[:], in1=xf1[dc][:],
                op=ALU.add)
            nc.gpsimd.wait_ge(p.sem("dv"), p.cnt["dv"])
            p.store_slot(2 + dc % 2, out=scr["pre_out"][dc, :, c0:c0 + TT],
                         in_=po_[:])
        p.wait(nc.tensor, "dv")
        p.bank_acquire(6)
        for dc in range(DC):
            inst = nc.tensor.matmul(p.psum[6][0:8, :], gwT[dc][:], xgf[dc][:],
                                    start=(dc == 0), stop=(dc == DC - 1))
        mmv = p.inc(inst, "mm", 1)
        nc.vector.wait_ge(p.sem("mm"), mmv)
        p.v("tensor_scalar", out=lg[:], in0=p.psum[6][0:8, :],
            scalar1=cst["gate_b"][:], scalar2=None, op0=ALU.add)
        p.bank_release(6, "dv")
        for a in range(4):
            p.wait(nc.tensor, "dv")
            p.bank_acquire(7)
            mmv = p.inc(nc.tensor.transpose(
                p.psum[7][:, 0:8], lg[:, a * 128:(a + 1) * 128],
                cst["idf"][0:8, 0:8]), "mm", 1)
            nc.vector.wait_ge(p.sem("mm"), mmv)
            p.v("tensor_copy", out=ltm[:], in_=p.psum[7][:, 0:8])
            p.bank_release(7, "dv")
            p.v("reduce_max", out=lmax[:], in_=ltm[:], axis=AX.X)
            p.v("tensor_scalar", out=ohtm[:], in0=ltm[:], scalar1=lmax[:],
                scalar2=None, op0=ALU.is_ge)
            p.wait(nc.tensor, "dv")
            p.bank_acquire(7)
            mmv = p.inc(nc.tensor.transpose(
                p.psum[7][0:8, 0:128], ohtm[:], cst["idf"][:]), "mm", 1)
            nc.vector.wait_ge(p.sem("mm"), mmv)
            p.v("tensor_copy", out=ohr[0:8, a * 128:(a + 1) * 128],
                in_=p.psum[7][0:8, 0:128])
            p.bank_release(7, "dv")
        nc.gpsimd.wait_ge(p.sem("dv"), p.cnt["dv"])
        p.inc(nc.gpsimd.dma_start(out=scr["w1h"][0:8, c0:c0 + TT],
                                  in_=ohr[:]), "st", 16)
        p.tile_end()


def phase_exp1(p, cst, io, scr):
    nc = p.nc
    wu1 = [p.sbl(f"e1_wu{i}", [128, HU], BF16) for i in range(DC)]
    wd1 = [p.sbl(f"e1_wd{i}", [128, D], BF16) for i in range(HC)]
    hsb = [p.sbl(f"e1_h{i}", [128, TT], BF16) for i in range(HC)]
    xgt = [[p.sbl(f"e1_xg{b}_{i}", [128, TT], BF16) for i in range(DC)]
           for b in range(2)]
    pot = [p.sbl(f"e1_po{i}", [128, TT], F32) for i in range(2)]
    oh8e = [p.sbl(f"e1_oh8{b}", [8, 128], BF16) for b in range(2)]
    w1h = [p.sbl(f"e1_w1h{b}", [8, TT], BF16) for b in range(2)]
    msk = [p.sbl(f"e1_msk{b}", [128, TT], BF16) for b in range(2)]
    p.inc(nc.sync.dma_start(out=oh8e[0][:],
                            in_=cst["h_oh8"].ap()[0:8, 0:128]), "la", 16)
    p.wsg2 = [p.sbl(f"e1_wsg2{i}", [128, 512], F32) for i in range(2)]
    p.wf_guard = [0, 0]

    h_end = [0, 0]
    y_end = [0]
    mask_po = [0, 0]
    h_last = [0]

    def load_w1(e):
        nc.sync.wait_ge(p.sem("mm"), h_last[0])
        for i in range(DC):
            for hh in range(8):
                k = (i * 8 + hh) % 2
                if p.cnt.get("we%d" % k, 0):
                    nc.sync.wait_ge(p.sem("dv"), p.wcast_guard[k][0])
                ldv = p.inc(nc.sync.dma_start(
                    out=p.wstage[k][:],
                    in_=io["L1_ffn_w1"][e, i * 128:(i + 1) * 128,
                                        hh * 512:(hh + 1) * 512]),
                    "we%d" % k, 16)
                nc.vector.wait_ge(p.sem("we%d" % k), ldv)
                p.v("tensor_copy", out=wu1[i][:, hh * 512:(hh + 1) * 512],
                    in_=p.wstage[k][:])
                p.wcast_guard[k] = (p.cnt["dv"], p.cnt["po"])

    def load_w2(e):
        nc.gpsimd.wait_ge(p.sem("mm"), y_end[0])
        for u in range(2 * HC):
            i, hh = u // 2, u % 2
            k = u % 2
            if p.cnt.get("wf%d" % k, 0):
                nc.gpsimd.wait_ge(p.sem("po"), p.wf_guard[k])
            ldv = p.inc(nc.gpsimd.dma_start(
                out=p.wsg2[k][:],
                in_=io["L1_ffn_w2"][e, i * 128:(i + 1) * 128,
                                    hh * 512:(hh + 1) * 512]),
                "wf%d" % k, 16)
            nc.gpsimd.wait_ge(p.sem("wf%d" % k), ldv)
            p.g("tensor_copy", out=wd1[i][:, hh * 512:(hh + 1) * 512],
                in_=p.wsg2[k][:])
            p.wf_guard[k] = p.cnt["po"]

    def prep(e, j):
        par = j % 2
        s, jj = j // NJS, j % NJS
        c0 = s * T + jj * TT
        nc.sync.wait_ge(p.sem("mm"), h_end[par])
        for dc in range(DC):
            p.inc(nc.sync.dma_start(out=xgt[par][dc][:],
                                    in_=scr["xg"][dc, :, c0:c0 + TT]),
                  "lc%d" % par, 16)
        p.inc(nc.sync.dma_start(out=w1h[par][:],
                                in_=scr["w1h"][0:8, c0:c0 + TT]),
              "lc%d" % par, 16)
        lav = p.cnt["lc%d" % par]
        nc.tensor.wait_ge(p.sem("lc%d" % par), lav)
        nc.tensor.wait_ge(p.sem("la"), p.cnt["la"])
        p.bank_acquire(4)
        mmv = p.inc(nc.tensor.matmul(
            p.psum[4][:], oh8e[e % 2][:],
            w1h[par][0:8, :], start=True, stop=True), "mm", 1)
        nc.vector.wait_ge(p.sem("mm"), mmv)
        nc.vector.wait_ge(p.sem("po"), mask_po[par])
        p.v("tensor_copy", out=msk[par][:], in_=p.psum[4][:])
        p.bank_release(4, "dv")
        nc.gpsimd.wait_ge(p.sem("lc%d" % par), lav)
        nc.gpsimd.wait_ge(p.sem("dv"), p.cnt["dv"])
        for dc in range(DC):
            p.g("tensor_tensor", out=xgt[par][dc][:], in0=xgt[par][dc][:],
                in1=msk[par][:], op=ALU.mult)
        mask_po[par] = p.cnt["po"]

    for e in range(8):
        if e + 1 < 8:
            nc.sync.wait_ge(p.sem("mm"), h_last[0])
            p.inc(nc.sync.dma_start(
                out=oh8e[(e + 1) % 2][:],
                in_=cst["h_oh8"].ap()[0:8, (e + 1) * 128:(e + 2) * 128]),
                "la", 16)
        load_w1(e)
        load_w2(e)
        if e == 0:
            prep(0, 0)
        for j in range(NJ):
            par = j % 2
            s, jj = j // NJS, j % NJS
            c0 = s * T + jj * TT
            nc.tensor.wait_ge(p.sem("po"), mask_po[par])
            nc.tensor.wait_ge(p.sem("dv"), p.wcast_guard[0][0])
            nc.tensor.wait_ge(p.sem("dv"), p.wcast_guard[1][0])
            nc.tensor.wait_ge(p.sem("po"), p.wf_guard[0])
            nc.tensor.wait_ge(p.sem("po"), p.wf_guard[1])
            for fc in range(HC):
                b = fc % 2
                p.bank_acquire(b)
                for dc in range(DC):
                    inst = nc.tensor.matmul(
                        p.psum[b][:], wu1[dc][:, fc * 128:(fc + 1) * 128],
                        xgt[par][dc][:], start=(dc == 0), stop=(dc == DC - 1))
                mmv = p.inc(inst, "mm", 1)
                nc.scalar.wait_ge(p.sem("mm"), mmv)
                p.s("activation", out=hsb[fc][:], in_=p.psum[b][:],
                    func=AF.Gelu_apprx_tanh)
                p.bank_release(b, "ac")
            h_end[par] = p.cnt["mm"]
            h_last[0] = h_end[par]
            if j + 1 < NJ:
                prep(e, j + 1)
            elif e + 1 < 8:
                prep(e + 1, 0)
            p.wait(nc.tensor, "ac")
            for dc in range(DC):
                nc.sync.wait_ge(p.sem("st%d" % (dc % 2)),
                                p.cnt.get("st%d" % (dc % 2), 0))
                p.inc(nc.sync.dma_start(out=pot[dc % 2][:],
                                        in_=scr["pre_out"][dc, :, c0:c0 + TT]),
                      "lb%d" % (dc % 2), 16)
                b = 2 + dc % 2
                p.bank_acquire(b)
                for fc in range(HC):
                    inst = nc.tensor.matmul(
                        p.psum[b][:], wd1[fc][:, dc * 128:(dc + 1) * 128],
                        hsb[fc][:], start=(fc == 0), stop=(fc == HC - 1))
                mmv = p.inc(inst, "mm", 1)
                nc.vector.wait_ge(p.sem("mm"), mmv)
                p.wait(nc.vector, "lb%d" % (dc % 2))
                p.v("tensor_tensor", out=pot[dc % 2][:], in0=pot[dc % 2][:],
                    in1=p.psum[b][:], op=ALU.add)
                p.bank_release(b, "dv")
                nc.gpsimd.wait_ge(p.sem("dv"), p.cnt["dv"])
                p.store_slot(dc % 2, out=scr["pre_out"][dc, :, c0:c0 + TT],
                             in_=pot[dc % 2][:])
        y_end[0] = p.cnt["mm"]
    p.tile_end()


def phase_final(p, cst, io, scr):
    nc = p.nc
    pot = [p.sbl(f"fn_po{i}", [128, TT], F32) for i in range(DC)]
    otm = [p.sbl(f"fn_otm{i}", [128, D], F32) for i in range(4)]
    for j in range(NJ):
        s, jj = j // NJS, j % NJS
        t0 = jj * TT
        c0 = s * T + t0
        for dc in range(DC):
            p.inc(nc.sync.dma_start(out=pot[dc][:],
                                    in_=scr["pre_out"][dc, :, c0:c0 + TT]),
                  "la", 16)
        nc.tensor.wait_ge(p.sem("la"), p.cnt["la"])
        for a in range(4):
            for dc in range(DC):
                b = (a * DC + dc) % 4
                p.bank_acquire(b)
                mmv = p.inc(nc.tensor.transpose(
                    p.psum[b][:, 0:128], pot[dc][:, a * 128:(a + 1) * 128],
                    cst["idf"][:]), "mm", 1)
                nc.vector.wait_ge(p.sem("mm"), mmv)
                p.v("tensor_copy", out=otm[a][:, dc * 128:(dc + 1) * 128],
                    in_=p.psum[b][:, 0:128])
                p.bank_release(b, "dv")
        nc.gpsimd.wait_ge(p.sem("dv"), p.cnt["dv"])
        for a in range(4):
            p.inc(nc.gpsimd.dma_start(
                out=io["out"][s, t0 + a * 128:t0 + (a + 1) * 128, :],
                in_=otm[a][:]), "st", 16)
        p.tile_end()


def build_program(upto=None, dump=()):
    p = Prog()
    nc = p.nc
    with p.es:
        io = declare_io(nc, dump)
        scr = declare_scratch(nc, dump)
        p.psum = [p.es.enter_context(nc.psum_tensor(f"pb{i}", [128, TT], F32))
                  for i in range(8)]
        p.tmp_sq = p.sb("tmp_sq", [128, TT], F32)
        p.tmp_stats = p.sb("tmp_stats", [1, 4 * TT], F32)
        p.wstage = [p.sb(f"wstage{i}", [128, 512], F32) for i in range(2)]
        p.wcast_guard = {0: (0, 0), 1: (0, 0)}
        cst = make_consts(p, io)
        phases = ALL_PHASES[:ALL_PHASES.index(upto) + 1] if upto else ALL_PHASES
        emitters = {"p0": phase_p0,
                    "k": lambda *a: phase_proj(*a, "k"),
                    "v": lambda *a: phase_proj(*a, "v"),
                    "r": lambda *a: phase_proj(*a, "r"),
                    "cmk": lambda *a: phase_proj(*a, "cmk"),
                    "scan": phase_scan,
                    "wo": phase_wo,
                    "cmv": phase_cmv,
                    "route": phase_route,
                    "exp0": phase_exp0,
                    "qkv1": phase_qkv1,
                    "att1": phase_att1,
                    "msaqkv": phase_msaqkv,
                    "msaatt": phase_msaatt,
                    "msaproj": phase_msaproj,
                    "exp1": phase_exp1,
                    "final": phase_final}
        for ph in phases:
            if ph in emitters:
                with ExitStack() as les:
                    p.local_es = les
                    emitters[ph](p, cst, io, scr)
                    p.tile_end()
                p.local_es = None
        p.wait(nc.gpsimd, "st")
        p.wait(nc.gpsimd, "la")
        for i in range(4):
            if ("st%d" % i) in p.sems:
                p.wait(nc.gpsimd, "st%d" % i)
    return p, io, scr


def kernel(**inputs):
    p, io, scr = build_program()
    x = np.ascontiguousarray(np.asarray(inputs["L0_attn_x"], dtype=np.float32))
    base = {k: np.ascontiguousarray(np.asarray(v, dtype=np.float32))
            for k, v in inputs.items() if k != "L0_attn_x"}
    in_maps = []
    for c in range(8):
        m = dict(base)
        m["x_in"] = x[2 * c:2 * c + 2]
        in_maps.append(m)
    res = run_bass_kernel_spmd(p.nc, in_maps, core_ids=list(range(8)))
    return np.concatenate([r["out"] for r in res.results], axis=0)
